# version 05
# baseline (speedup 1.0000x reference)
import math
import numpy as np
import ml_dtypes
import concourse.bass as bass
import concourse.mybir as mybir
from concourse.bass_utils import run_bass_kernel_spmd

F32 = mybir.dt.float32
BF16 = mybir.dt.bfloat16
AF = mybir.ActivationFunctionType
ALU = mybir.AluOpType
AX = mybir.AxisListType

D = 2048
DL = 512
NH = 16
SEQ = 4096
FF = 7168
ER = 256
OWN = 2048
QT = 512
NQT = 4
KCS = (8, 16, 24, 32)
NFC = FF // 128
SCALE = 1.0 / math.sqrt(192.0)
EBIAS = -80.0 * SCALE
PLO = math.exp(2.0 * EBIAS)
THETA = 10000.0
EPS = 1e-5

POS_EVEN = (0, 3, 4, 7)
POS_ODD = (1, 2, 5, 6)

FS_W1 = 0
FS_W3 = 2048
FS_B1 = 4096
FS_B3 = 4096 + 256
FS_A2 = 4096 + 512
FS_COLS = 4096 + 768


def _own_rows(parity):
    pos = POS_EVEN if parity == 0 else POS_ODD
    r = []
    for p in pos:
        r.extend(range(p * QT, (p + 1) * QT))
    return np.array(r, dtype=np.int64)


def _lhsT4(wt, mj=128):
    din, dout = wt.shape
    ni, nj = din // 128, dout // mj
    return np.ascontiguousarray(wt.reshape(ni, 128, nj, mj).transpose(1, 0, 2, 3))


def _rope_perm():
    return np.concatenate([np.arange(0, 64, 2), np.arange(1, 64, 2)])


def _rope_tables(positions):
    inv = THETA ** (-np.arange(0, 64, 2, dtype=np.float64) / 64.0)
    ang = positions.astype(np.float64)[None, :] * inv[:, None]
    cos = np.cos(ang)
    sin = np.sin(ang)
    cos2 = np.concatenate([cos, cos], 0).astype(np.float32)
    sin2 = np.concatenate([-sin, sin], 0).astype(np.float32)
    return np.ascontiguousarray(cos2), np.ascontiguousarray(sin2)


def _host_inputs(inp):
    x = np.asarray(inp["L0_attn_x"], dtype=np.float32)
    perm = _rope_perm()
    sh = {}
    sh["wdq"] = _lhsT4(np.ascontiguousarray(np.asarray(inp["L0_attn_W_DQ"]).T))
    sh["wdkv"] = _lhsT4(np.ascontiguousarray(np.asarray(inp["L0_attn_W_DKV"]).T))
    sh["wkr"] = _lhsT4(np.ascontiguousarray(np.asarray(inp["L0_attn_W_KR"]).T[:, perm]), mj=64)
    sh["wuq"] = _lhsT4(np.ascontiguousarray(np.asarray(inp["L0_attn_W_UQ"]).T))
    wqrt = np.asarray(inp["L0_attn_W_QR"]).T.reshape(DL, NH, 64)[:, :, perm]
    sh["wqr"] = _lhsT4(np.ascontiguousarray(wqrt.reshape(DL, NH * 64)), mj=64)
    sh["wuk"] = _lhsT4(np.ascontiguousarray(np.asarray(inp["L0_attn_W_UK"]).T))
    sh["wuv"] = _lhsT4(np.ascontiguousarray(np.asarray(inp["L0_attn_W_UV"]).T))
    wot = np.asarray(inp["L0_attn_W_O"]).T
    sh["wo"] = np.ascontiguousarray(wot.reshape(NH, 128, 4, 512).transpose(1, 2, 0, 3))
    sh["wg"] = np.ascontiguousarray(
        np.asarray(inp["L0_ffn_Wg"]).T.reshape(16, 128, 8).transpose(1, 0, 2))

    w1t = np.asarray(inp["L0_ffn_W1"]).T
    w3t = np.asarray(inp["L0_ffn_W3"]).T
    b1st = np.asarray(inp["L0_ffn_B1"]).transpose(0, 2, 1).reshape(ER, FF)
    b3st = np.asarray(inp["L0_ffn_B3"]).transpose(0, 2, 1).reshape(ER, FF)
    a2st = np.ascontiguousarray(np.asarray(inp["L0_ffn_A2"]).reshape(ER, FF).T)
    fs = np.empty((NFC, 128, FS_COLS), dtype=np.float32)
    fs[:, :, FS_W1:FS_W1 + 2048] = w1t.reshape(16, 128, NFC, 128).transpose(2, 1, 0, 3).reshape(NFC, 128, 2048)
    fs[:, :, FS_W3:FS_W3 + 2048] = w3t.reshape(16, 128, NFC, 128).transpose(2, 1, 0, 3).reshape(NFC, 128, 2048)
    fs[:, :, FS_B1:FS_B1 + 256] = b1st.reshape(2, 128, NFC, 128).transpose(2, 1, 0, 3).reshape(NFC, 128, 256)
    fs[:, :, FS_B3:FS_B3 + 256] = b3st.reshape(2, 128, NFC, 128).transpose(2, 1, 0, 3).reshape(NFC, 128, 256)
    fs[:, :, FS_A2:FS_A2 + 256] = a2st.reshape(NFC, 128, 256)
    sh["fstream"] = fs
    sh["a1s"] = _lhsT4(np.ascontiguousarray(np.asarray(inp["L0_ffn_A1"]).reshape(ER, D).T))
    sh["a3s"] = _lhsT4(np.ascontiguousarray(np.asarray(inp["L0_ffn_A3"]).reshape(ER, D).T))
    b2st = np.asarray(inp["L0_ffn_B2"]).transpose(0, 2, 1).reshape(ER, D)
    sh["b2s"] = np.ascontiguousarray(b2st.reshape(2, 128, D).transpose(1, 0, 2))
    w2t = np.asarray(inp["L0_ffn_W2"]).T
    sh["w2"] = np.ascontiguousarray(w2t.reshape(NFC, 128, 4, 512).transpose(1, 2, 0, 3))

    sh["ident"] = np.eye(128, dtype=ml_dtypes.bfloat16)
    sh["ident32"] = np.eye(128, dtype=np.float32)
    sh["ones_col"] = np.ones((128, 1), dtype=ml_dtypes.bfloat16)
    sh["ones_row"] = np.ones((1, 128), dtype=ml_dtypes.bfloat16)
    sh["ones_row32"] = np.ones((1, 128), dtype=np.float32)
    sh["ebias"] = np.full((128, 1), EBIAS, dtype=np.float32)
    psm = np.zeros((64, 64), dtype=ml_dtypes.bfloat16)
    for i in range(32):
        psm[i, 32 + i] = 1
        psm[32 + i, i] = 1
    sh["ps64"] = psm
    e8 = np.zeros((8, 256), dtype=ml_dtypes.bfloat16)
    for e in range(8):
        e8[e, e * 32:(e + 1) * 32] = 1
    sh["e8"] = e8
    sh["kpos"] = np.ascontiguousarray(
        (np.arange(128)[:, None] + 128 * np.arange(32)[None, :]).astype(np.float32))
    kc2, ks2 = _rope_tables(np.arange(SEQ))
    sh["kcos"] = kc2
    sh["ksin"] = ks2

    maps = []
    for c in range(8):
        b, par = c // 2, c % 2
        rows = _own_rows(par)
        m = dict(sh)
        xb = x[b]
        xt = np.ascontiguousarray(xb.T)
        m["xT"] = np.ascontiguousarray(xt.reshape(16, 128, SEQ).transpose(1, 0, 2))
        xop = np.ascontiguousarray(xb[rows])
        m["xo"] = xop
        m["xTo"] = np.ascontiguousarray(xop.T.reshape(16, 128, OWN).transpose(1, 0, 2))
        qc2, qs2 = _rope_tables(rows)
        m["qcos"] = qc2
        m["qsin"] = qs2
        m["qposb"] = np.ascontiguousarray(
            np.broadcast_to(rows.astype(np.float32)[None, :], (128, OWN)).copy())
        maps.append(m)
    return maps


class Res:
    __slots__ = ("ready", "users")

    def __init__(self):
        self.ready = None
        self.users = []


class Sch:
    ENGS = ("T", "V", "S", "P", "Y")

    def __init__(self, nc):
        self.nc = nc
        self.q = {e: [] for e in self.ENGS}
        self.sem = {}
        self.cnt = {}
        self.hiw = {e: {} for e in self.ENGS}
        self._ctx = []
        for name in ("sT", "sV", "sS", "dY", "dS", "dP"):
            cm = nc.semaphore(name)
            self.sem[name] = cm.__enter__()
            self._ctx.append(cm)
            self.cnt[name] = 0
        self.esem = {"T": "sT", "V": "sV", "S": "sS"}
        self.qsem = {"P": "dP", "Y": "dY", "S": "dS"}
        self.trace = []

    def _wait(self, eng, toks):
        best = {}
        own = self.esem.get(eng)
        for t in toks:
            if t is None:
                continue
            s, v = t
            if s == own:
                continue
            if best.get(s, -1) < v:
                best[s] = v
        for s, v in best.items():
            if self.hiw[eng].get(s, 0) >= v:
                continue
            self.hiw[eng][s] = v
            h = self.sem[s]
            self.trace.append((eng, "wait", s, v))
            self.q[eng].append(lambda e, h=h, v=v: e.wait_ge(h, v))

    def emit(self, eng, fn, reads=(), writes=(), dma=False, ndma=1, haz=False, tag=""):
        toks = []
        for r in reads:
            if r is not None and r.ready is not None:
                toks.append(r.ready)
        for w in writes:
            toks.extend(w.users)
            if w.ready is not None:
                toks.append(w.ready)
        self._wait(eng, toks)
        sname = self.qsem[eng] if dma else self.esem[eng]
        amt = 16 * ndma if dma else 1
        self.cnt[sname] += amt
        tok = (sname, self.cnt[sname])
        h = self.sem[sname]
        inc = 16 if dma else 1
        if haz:
            vna = self.vnop_ap
            def fn2(e, fn=fn, vna=vna):
                fn(e)
                return e.tensor_scalar(vna, vna, 1.0, None, ALU.mult)
            fn = fn2
        self.trace.append((eng, "op:" + tag, sname, tok[1]))
        self.q[eng].append(lambda e, fn=fn, h=h, a=inc: fn(e).then_inc(h, a))
        for r in reads:
            if r is not None:
                r.users.append(tok)
        for w in writes:
            w.ready = tok
            w.users = []
        return tok

    def gate(self):
        toks = [(s, self.cnt[s]) for s in ("sT", "sV", "sS", "dS")]
        for e in self.ENGS:
            self._wait(e, toks)

    def tail(self):
        toks = [(s, self.cnt[s]) for s in self.cnt]
        self._wait("Y", toks)

    def finalize(self, block):
        for attr, eng in (("tensor", "T"), ("vector", "V"), ("scalar", "S"),
                          ("gpsimd", "P"), ("sync", "Y")):
            ops = self.q[eng]

            def body(e, ops=ops):
                for op in ops:
                    op(e)
            getattr(block, attr)(body)

    def close(self):
        for cm in reversed(self._ctx):
            cm.__exit__(None, None, None)


def build_nc(dump=()):
    nc = bass.Bass("TRN2", target_bir_lowering=False)

    def din(name, shape, dtype=F32):
        return nc.dram_tensor(name, list(shape), dtype, kind="ExternalInput")

    xT = din("xT", [128, 16, SEQ])
    xTo = din("xTo", [128, 16, OWN])
    xo = din("xo", [OWN, D])
    wdq = din("wdq", [128, 16, 4, 128])
    wdkv = din("wdkv", [128, 16, 4, 128])
    wkr = din("wkr", [128, 16, 1, 64])
    wuq = din("wuq", [128, 4, 16, 128])
    wqr = din("wqr", [128, 4, 16, 64])
    wuk = din("wuk", [128, 4, 16, 128])
    wuv = din("wuv", [128, 4, 16, 128])
    wo = din("wo", [128, 4, NH, 512])
    wg = din("wg", [128, 16, 8])
    fstream = din("fstream", [NFC, 128, FS_COLS])
    a1s = din("a1s", [128, 16, 2, 128])
    a3s = din("a3s", [128, 16, 2, 128])
    b2s = din("b2s", [128, 2, D])
    w2 = din("w2", [128, 4, NFC, 512])
    ident = din("ident", [128, 128], BF16)
    ident32 = din("ident32", [128, 128], F32)
    ones_col = din("ones_col", [128, 1], BF16)
    ones_row = din("ones_row", [1, 128], BF16)
    ones_row32 = din("ones_row32", [1, 128], F32)
    ebias = din("ebias", [128, 1], F32)
    ps64 = din("ps64", [64, 64], BF16)
    e8 = din("e8", [8, 256], BF16)
    kpos = din("kpos", [128, 32])
    kcos = din("kcos", [64, SEQ])
    ksin = din("ksin", [64, SEQ])
    qcos = din("qcos", [64, OWN])
    qsin = din("qsin", [64, OWN])
    qposb = din("qposb", [128, OWN])

    out = nc.dram_tensor("out", [OWN, D], F32, kind="ExternalOutput")
    x1d = nc.dram_tensor("x1d", [OWN, D], F32,
                         kind="ExternalOutput" if "x1" in dump else "Internal")
    attnd = nc.dram_tensor("attnd", [128, NH * OWN], BF16,
                           kind="ExternalOutput" if "attn" in dump else "Internal")
    s_wo = nc.dram_tensor("s_wo", [128, 4, NH, 512], BF16)
    s_fs = nc.dram_tensor("s_fs", [NFC, 128, FS_COLS], BF16)
    s_w2 = nc.dram_tensor("s_w2", [128, 4, NFC, 512], BF16)
    s_a1s = nc.dram_tensor("s_a1s", [128, 16, 2, 128], BF16)
    s_a3s = nc.dram_tensor("s_a3s", [128, 16, 2, 128], BF16)

    dmp = {}
    if "ffn" in dump:
        dmp["xnf"] = nc.dram_tensor("d_xnf", [128, 16 * 512], BF16, kind="ExternalOutput")
        dmp["maskf"] = nc.dram_tensor("d_maskf", [8, 1024], BF16, kind="ExternalOutput")
        dmp["rwb"] = nc.dram_tensor("d_rwb", [128, 1024], F32, kind="ExternalOutput")
        dmp["ash"] = nc.dram_tensor("d_ash", [128, 8 * 512], BF16, kind="ExternalOutput")
        dmp["y"] = nc.dram_tensor("d_y", [128, NFC * 512], BF16, kind="ExternalOutput")
        dmp["a2e"] = nc.dram_tensor("d_a2e", [128, 4 * 512], BF16, kind="ExternalOutput")
        dmp["fout"] = nc.dram_tensor("d_fout", [128, 4 * 2048], BF16, kind="ExternalOutput")
        dmp["lgE"] = nc.dram_tensor("d_lgE", [128, 32], F32, kind="ExternalOutput")
        dmp["g01"] = nc.dram_tensor("d_g01", [128, 16], BF16, kind="ExternalOutput")
        dmp["rwp"] = nc.dram_tensor("d_rwp", [128, 2], F32, kind="ExternalOutput")
        dmp["rws"] = nc.dram_tensor("d_rws", [1, 1024], F32, kind="ExternalOutput")
    if "attno" in dump:
        dmp["attno"] = nc.dram_tensor("d_attno", [128, 4 * D], F32, kind="ExternalOutput")
        dmp["strip"] = nc.dram_tensor("d_strip", [128, 8], F32, kind="ExternalOutput")
        dmp["sums"] = nc.dram_tensor("d_sums", [128, 16], F32, kind="ExternalOutput")
        dmp["sqc"] = nc.dram_tensor("d_sqc", [128, 4], F32, kind="ExternalOutput")
    if "ckv" in dump:
        dmp["ckv"] = nc.dram_tensor("d_ckv", [128, 4 * SEQ], BF16, kind="ExternalOutput")
        dmp["cq"] = nc.dram_tensor("d_cq", [128, 4 * OWN], BF16, kind="ExternalOutput")
        dmp["kr"] = nc.dram_tensor("d_kr", [64, SEQ], BF16, kind="ExternalOutput")

    sch = Sch(nc)
    E = sch.emit
    stack = []

    def sb(name, shape, dtype):
        cm = nc.sbuf_tensor(name, list(shape), dtype)
        t = cm.__enter__()
        stack.append((name, cm))
        return t

    nbank_res = [Res() for _ in range(8)]

    def pp(name, shape, dtype=F32):
        b0 = nc.psum_base
        cm = nc.psum_tensor(name, list(shape), dtype)
        t = cm.__enter__()
        b1 = nc.psum_base
        stack.append((name, cm))
        return t, nbank_res[b0:b1]

    def free(upto):
        while stack:
            name, cm = stack.pop()
            cm.__exit__(None, None, None)
            if name == upto:
                return
        raise RuntimeError(f"free: {upto} not found")

    R = Res

    vnop_sb = sb("vnop_sb", [128, 2], F32)
    sch.vnop_ap = vnop_sb[:, 0:2]
    ident_sb = sb("ident_sb", [128, 128], BF16)
    id32_sb = sb("id32_sb", [128, 128], F32)
    ones_c = sb("ones_c", [128, 1], BF16)
    ones_r = sb("ones_r", [1, 128], BF16)
    ones_r32 = sb("ones_r32", [1, 128], F32)
    ebias_sb = sb("ebias_sb", [128, 1], F32)
    ps64_sb = sb("ps64_sb", [64, 64], BF16)
    e8_sb = sb("e8_sb", [8, 256], BF16)
    kpos_sb = sb("kpos_sb", [128, 32], F32)
    wg_sb = sb("wg_sb", [128, 16 * 8], BF16)
    b2s_sb = sb("b2s_sb", [128, 2 * D], BF16)
    rconst = R()
    for dst, src in ((ident_sb, ident), (id32_sb, ident32), (ones_c, ones_col),
                     (ones_r, ones_row), (ones_r32, ones_row32), (ebias_sb, ebias),
                     (ps64_sb, ps64), (e8_sb, e8), (kpos_sb, kpos)):
        E("Y", lambda e, d=dst, s=src: e.dma_start(d[:], s[:]), writes=[rconst], dma=True)
    E("P", lambda e: e.dma_start(wg_sb[:].rearrange("p (a b) -> p a b", a=16), wg[:]),
      writes=[rconst], dma=True)
    E("P", lambda e: e.dma_start(b2s_sb[:].rearrange("p (a b) -> p a b", a=2), b2s[:]),
      writes=[rconst], dma=True)

    r_swo, r_sfs, r_sw2, r_sa13 = R(), R(), R(), R()
    cast_jobs = [(s_wo, wo, r_swo), (s_a1s, a1s, r_sa13), (s_a3s, a3s, r_sa13)]
    for i in range(0, NFC, 7):
        cast_jobs.append((s_fs[i:i + 7], fstream[i:i + 7], r_sfs))
    for i in range(4):
        cast_jobs.append((s_w2[:, i], w2[:, i], r_sw2))

    def emit_casts(k=1):
        for _ in range(k):
            if not cast_jobs:
                return
            dst, src, res = cast_jobs.pop(0)
            E("P", lambda e, d=dst, s=src: e.dma_start(d[:], s[:]), writes=[res], dma=True)

    ckv_sb = sb("ckv_sb", [128, 4 * SEQ], BF16)
    cq_sb = sb("cq_sb", [128, 4 * OWN], BF16)
    kr_sb = sb("kr_sb", [64, SEQ], BF16)
    r_ckv, r_cq, r_kr = R(), R(), R()
    wdq_sb = sb("wdq_sb", [128, 16 * 4 * 128], BF16)
    wdkv_sb = sb("wdkv_sb", [128, 16 * 4 * 128], BF16)
    wkr_sb = sb("wkr_sb", [128, 16 * 64], BF16)
    r_wdq, r_wdkv, r_wkr = R(), R(), R()
    E("P", lambda e: e.dma_start(
        wdq_sb[:].rearrange("p (a b c) -> p a b c", a=16, b=4), wdq[:]),
      writes=[r_wdq], dma=True)
    E("P", lambda e: e.dma_start(
        wdkv_sb[:].rearrange("p (a b c) -> p a b c", a=16, b=4), wdkv[:]),
      writes=[r_wdkv], dma=True)
    E("P", lambda e: e.dma_start(
        wkr_sb[:].rearrange("p (a c) -> p a c", a=16),
        wkr[:].rearrange("p a b c -> p a (b c)")),
      writes=[r_wkr], dma=True)
    kcos_sb = sb("kcos_sb", [64, SEQ], F32)
    ksin_sb = sb("ksin_sb", [64, SEQ], F32)
    r_ktab = R()
    E("Y", lambda e: e.dma_start(kcos_sb[:], kcos[:]), writes=[r_ktab], dma=True)
    E("Y", lambda e: e.dma_start(ksin_sb[:], ksin[:]), writes=[r_ktab], dma=True)

    xb_sb = sb("xb_sb", [128, 3 * 16 * 512], BF16)
    r_xb = [R(), R(), R()]
    xsq_sb = sb("xsq_sb", [128, 2048], BF16)
    r_xsq = R()
    stripA = sb("stripA", [1, 4 * 512], F32)
    r_stripA = R()
    mb_sb = sb("mb_sb", [128, 1024], F32)
    r_mb = R()
    kraw_sb = sb("kraw_sb", [64, 512], BF16)
    r_kraw = R()
    kt_sb = sb("kt_sb", [64, 1024], F32)
    r_kt = R()

    pA_sum, bA_sum = pp("pA_sum", [1, 512])
    pA_sq, bA_sq = pp("pA_sq", [1, 512])
    pA_bc, bA_bc = pp("pA_bc", [128, 1024])
    pA_p1, bA_p1 = pp("pA_p1", [128, 1024])
    pA_p2, bA_p2 = pp("pA_p2", [128, 1024])
    bA_sum, bA_sq = bA_sum[0], bA_sq[0]

    def ln_fm_stats(xs, rx):
        def f_sum(e, x=xs):
            for i in range(16):
                ins = e.matmul(pA_sum[:], ones_c[:], x[:, i * 512:(i + 1) * 512],
                               start=(i == 0), stop=(i == 15))
            return ins
        E("T", f_sum, reads=[rx, rconst], writes=[bA_sum])
        for g in range(4):
            E("V", lambda e, x=xs, g=g: e.tensor_tensor(
                xsq_sb[:], x[:, g * 2048:(g + 1) * 2048],
                x[:, g * 2048:(g + 1) * 2048], ALU.mult),
              reads=[rx], writes=[r_xsq])

            def f_sq(e, g=g):
                for i in range(4):
                    ins = e.matmul(pA_sq[:], ones_c[:], xsq_sb[:, i * 512:(i + 1) * 512],
                                   start=(g == 0 and i == 0), stop=(g == 3 and i == 3))
                return ins
            E("T", f_sq, reads=[r_xsq, rconst], writes=[bA_sq] if g == 3 else [])
        mS = stripA[:, 0:512]
        vS = stripA[:, 512:1024]
        tS = stripA[:, 1024:1536]
        E("V", lambda e: e.tensor_scalar(mS, pA_sum[:], 1.0 / D, None, ALU.mult),
          reads=[bA_sum], writes=[r_stripA])
        E("V", lambda e: e.tensor_scalar(tS, pA_sq[:], 1.0 / D, EPS, ALU.mult, ALU.add),
          reads=[bA_sq], writes=[r_stripA])
        E("V", lambda e: e.tensor_tensor(vS, mS, mS, ALU.mult), writes=[r_stripA])
        E("V", lambda e: e.tensor_tensor(vS, tS, vS, ALU.subtract), writes=[r_stripA])
        E("S", lambda e: e.activation(vS, vS, AF.Ln), reads=[r_stripA], writes=[r_stripA])
        E("S", lambda e: e.activation(vS, vS, AF.Exp, scale=-0.5),
          reads=[r_stripA], writes=[r_stripA])
        E("T", lambda e: e.matmul(pA_bc[:, 0:512], ones_r32[:], mS, start=True, stop=True),
          reads=[r_stripA, rconst], writes=[bA_bc[0]])
        E("T", lambda e: e.matmul(pA_bc[:, 512:1024], ones_r32[:], vS, start=True, stop=True),
          reads=[r_stripA], writes=[bA_bc[1]])
        E("V", lambda e: e.tensor_copy(mb_sb[:], pA_bc[:]),
          reads=bA_bc, writes=[r_mb])
        for i in range(16):
            E("V", lambda e, x=xs, i=i: e.tensor_tensor(
                x[:, i * 512:(i + 1) * 512], x[:, i * 512:(i + 1) * 512],
                mb_sb[:, 0:512], ALU.subtract), reads=[r_mb], writes=[rx])
        for i in range(16):
            E("V", lambda e, x=xs, i=i: e.tensor_tensor(
                x[:, i * 512:(i + 1) * 512], x[:, i * 512:(i + 1) * 512],
                mb_sb[:, 512:1024], ALU.mult), reads=[r_mb], writes=[rx])

    prjL = [(pA_p1[:, 0:512], bA_p1[0]), (pA_p1[:, 512:1024], bA_p1[1]),
            (pA_p2[:, 0:512], bA_p2[0]), (pA_p2[:, 512:1024], bA_p2[1])]

    def phaseA_load(idx, src):
        slot = idx % 3
        xs = xb_sb[:, slot * 8192:(slot + 1) * 8192]
        rx = r_xb[slot]
        E("P", lambda e, s=src, x=xs: e.dma_start(
            x.rearrange("p (i t) -> p i t", i=16), s), writes=[rx], dma=True)

    def phaseA_block(idx, tb, own):
        slot = idx % 3
        xs = xb_sb[:, slot * 8192:(slot + 1) * 8192]
        rx = r_xb[slot]
        ln_fm_stats(xs, rx)
        if own:
            wsb, rw, dst, rdst, blk = wdq_sb, r_wdq, cq_sb, r_cq, OWN
        else:
            wsb, rw, dst, rdst, blk = wdkv_sb, r_wdkv, ckv_sb, r_ckv, SEQ
        for j in range(4):
            pv, br = prjL[j]

            def f_prj(e, x=xs, w=wsb, j=j, pv=pv):
                for i in range(16):
                    ins = e.matmul(pv, w[:, (i * 4 + j) * 128:(i * 4 + j) * 128 + 128],
                                   x[:, i * 512:(i + 1) * 512],
                                   start=(i == 0), stop=(i == 15))
                return ins
            E("T", f_prj, reads=[rx, rw], writes=[br])
            E("V", lambda e, pv=pv, d=dst, j=j, tb=tb, blk=blk: e.tensor_copy(
                d[:, j * blk + tb * 512: j * blk + tb * 512 + 512], pv),
              reads=[br], writes=[rdst])
        if not own:
            pv, br = prjL[0]

            def f_kr(e, x=xs, pv=pv):
                pv64 = pv[0:64, :]
                for i in range(16):
                    ins = e.matmul(pv64, wkr_sb[:, i * 64:(i + 1) * 64],
                                   x[:, i * 512:(i + 1) * 512],
                                   start=(i == 0), stop=(i == 15))
                return ins
            E("T", f_kr, reads=[rx, r_wkr], writes=[br])
            E("V", lambda e, pv=pv: e.tensor_copy(kraw_sb[:], pv[0:64, :]),
              reads=[br], writes=[r_kraw])
            pv2, br2 = prjL[1]
            E("T", lambda e, pv2=pv2: e.matmul(pv2[0:64, :], ps64_sb[:], kraw_sb[:],
                                               start=True, stop=True),
              reads=[r_kraw, rconst], writes=[br2])
            t1 = kt_sb[:, 0:512]
            t2 = kt_sb[:, 512:1024]
            E("V", lambda e, tb=tb, t1=t1: e.tensor_tensor(
                t1, kraw_sb[:], kcos_sb[:, tb * 512:(tb + 1) * 512], ALU.mult),
              reads=[r_kraw, r_ktab], writes=[r_kt])
            E("V", lambda e, pv2=pv2, tb=tb, t2=t2: e.tensor_tensor(
                t2, pv2[0:64, :], ksin_sb[:, tb * 512:(tb + 1) * 512], ALU.mult),
              reads=[br2, r_ktab], writes=[r_kt])
            E("V", lambda e, tb=tb, t1=t1, t2=t2: e.tensor_tensor(
                kr_sb[:, tb * 512:(tb + 1) * 512], t1, t2, ALU.add),
              reads=[r_kt], writes=[r_kr])

    A_jobs = [(tb, xT[:, :, tb * 512:(tb + 1) * 512], False) for tb in range(8)]
    A_jobs += [(tb, xTo[:, :, tb * 512:(tb + 1) * 512], True) for tb in range(4)]
    phaseA_load(0, A_jobs[0][1])
    phaseA_load(1, A_jobs[1][1])
    for idx, (tb, src, own) in enumerate(A_jobs):
        if idx + 2 < len(A_jobs):
            phaseA_load(idx + 2, A_jobs[idx + 2][1])
        phaseA_block(idx, tb, own)

    if "ckv" in dump:
        rd = R()
        E("S", lambda e: e.dma_start(dmp["ckv"][:], ckv_sb[:]), reads=[r_ckv],
          writes=[rd], dma=True)
        E("S", lambda e: e.dma_start(dmp["cq"][:], cq_sb[:]), reads=[r_cq],
          writes=[rd], dma=True)
        E("S", lambda e: e.dma_start(dmp["kr"][:], kr_sb[:]), reads=[r_kr],
          writes=[rd], dma=True)

    free("wdq_sb")
    sch.gate()

    qcos_sb = sb("qcos_sb", [64, OWN], F32)
    qsin_sb = sb("qsin_sb", [64, OWN], F32)
    qpos_sb = sb("qpos_sb", [128, OWN], F32)
    r_qtab = R()
    E("Y", lambda e: e.dma_start(qcos_sb[:], qcos[:]), writes=[r_qtab], dma=True)
    E("Y", lambda e: e.dma_start(qsin_sb[:], qsin[:]), writes=[r_qtab], dma=True)
    E("Y", lambda e: e.dma_start(qpos_sb[:], qposb[:]), writes=[r_qtab], dma=True)

    WSL = 4 * 128 * 3 + 4 * 64
    wslf = sb("wslf", [128, WSL], F32)
    wslb = sb("wslb", [128, 2 * WSL], BF16)
    r_wslf = R()
    r_wslb = [R(), R()]
    khi_sb = sb("khi_sb", [128, SEQ], BF16)
    v_sb = sb("v_sb", [128, SEQ], BF16)
    qhi_sb = sb("qhi_sb", [128, OWN], BF16)
    qlo_sb = sb("qlo_sb", [64, OWN], BF16)
    qraw_sb = sb("qraw_sb", [64, OWN], BF16)
    qt_sb = sb("qt_sb", [64, 1024], F32)
    r_khi, r_v, r_qhi, r_qlo, r_qraw, r_qt = R(), R(), R(), R(), R(), R()
    p_sb = sb("p_sb", [128, 3 * 1024], BF16)
    r_p = [R(), R(), R()]
    rl_sb = sb("rl_sb", [1, 512], F32)
    r_rl = R()
    atmp_sb = sb("atmp_sb", [128, 2 * 512], BF16)
    r_atmp = [R(), R()]

    pB_sc, bB_sc6 = pp("pB_sc", [128, 3072])
    pB_acc, bB_acc = pp("pB_acc", [128, 512])
    pB_l, bB_l = pp("pB_l", [1, 512])
    bB_sc = [bB_sc6[0:2], bB_sc6[2:4], bB_sc6[4:6]]
    pB_pr = pB_sc[:, 2048:3072]
    bB_pr = bB_sc6[4:6]
    bB_acc, bB_l = bB_acc[0], bB_l[0]

    def load_wslice(h):
        tok = None
        first = True
        for off, src, w in ((0, wuq, 512), (512, wuk, 512), (1024, wuv, 512),
                            (1536, wqr, 256)):
            tok = E("Y", lambda e, o=off, s=src, w=w, h=h: e.dma_start(
                wslf[:, o:o + w].rearrange("p (i c) -> p i c", i=4), s[:, :, h, :]),
                writes=[r_wslf] if first else [], dma=True)
            first = False
        r_wslf.ready = tok
        r_wslf.users = []
        slot = h % 2
        E("V", lambda e, slot=slot: e.tensor_copy(
            wslb[:, slot * WSL:(slot + 1) * WSL], wslf[:]),
          reads=[r_wslf], writes=[r_wslb[slot]])

    load_wslice(0)
    pending_evac = [None]
    for h in range(NH):
        emit_casts(3)
        slot = h % 2
        base = slot * WSL
        rb = r_wslb[slot]
        if h + 1 < NH:
            load_wslice(h + 1)
        for kb in range(8):
            half = kb % 2
            pv = pB_pr[:, half * 512:(half + 1) * 512]
            br = bB_pr[half]

            def f_khi(e, kb=kb, base=base, pv=pv):
                for i in range(4):
                    ins = e.matmul(pv, wslb[:, base + 512 + i * 128:base + 512 + i * 128 + 128],
                                   ckv_sb[:, i * SEQ + kb * 512:i * SEQ + kb * 512 + 512],
                                   start=(i == 0), stop=(i == 3))
                return ins
            E("T", f_khi, reads=[rb, r_ckv], writes=[br])
            E("V", lambda e, pv=pv, kb=kb: e.tensor_copy(
                khi_sb[:, kb * 512:(kb + 1) * 512], pv), reads=[br], writes=[r_khi])
        for g in range(8):
            half = g % 2
            pv = pB_pr[:, half * 512:(half + 1) * 512]
            br = bB_pr[half]

            def f_v(e, g=g, base=base, pv=pv):
                for c in range(4):
                    kc = g * 4 + c
                    for i in range(4):
                        ins = e.matmul(
                            pv[:, c * 128:(c + 1) * 128],
                            ckv_sb[:, i * SEQ + kc * 128:i * SEQ + kc * 128 + 128],
                            wslb[:, base + 1024 + i * 128:base + 1024 + i * 128 + 128],
                            start=(i == 0), stop=(i == 3))
                return ins
            E("T", f_v, reads=[rb, r_ckv], writes=[br])
            E("V", lambda e, pv=pv, g=g: e.tensor_copy(
                v_sb[:, g * 512:(g + 1) * 512], pv), reads=[br], writes=[r_v])
        for qb in range(4):
            half = qb % 2
            pv = pB_pr[:, half * 512:(half + 1) * 512]
            br = bB_pr[half]

            def f_qhi(e, qb=qb, base=base, pv=pv):
                for i in range(4):
                    ins = e.matmul(pv, wslb[:, base + i * 128:base + i * 128 + 128],
                                   cq_sb[:, i * OWN + qb * 512:i * OWN + qb * 512 + 512],
                                   start=(i == 0), stop=(i == 3))
                return ins
            E("T", f_qhi, reads=[rb, r_cq], writes=[br])
            E("V", lambda e, pv=pv, qb=qb: e.tensor_copy(
                qhi_sb[:, qb * 512:(qb + 1) * 512], pv), reads=[br], writes=[r_qhi])
        for qb in range(4):
            half = qb % 2
            pv = pB_pr[:, half * 512:(half + 1) * 512]
            br = bB_pr[half]

            def f_qr(e, qb=qb, base=base, pv=pv):
                pv64 = pv[0:64, :]
                for i in range(4):
                    ins = e.matmul(pv64, wslb[:, base + 1536 + i * 64:base + 1536 + i * 64 + 64],
                                   cq_sb[:, i * OWN + qb * 512:i * OWN + qb * 512 + 512],
                                   start=(i == 0), stop=(i == 3))
                return ins
            E("T", f_qr, reads=[rb, r_cq], writes=[br])
            E("V", lambda e, pv=pv, qb=qb: e.tensor_copy(
                qraw_sb[:, qb * 512:(qb + 1) * 512], pv[0:64, :]),
              reads=[br], writes=[r_qraw])
            pv2 = pB_pr[:, (1 - half) * 512:(2 - half) * 512]
            br2 = bB_pr[1 - half]
            E("T", lambda e, pv2=pv2, qb=qb: e.matmul(
                pv2[0:64, :], ps64_sb[:], qraw_sb[:, qb * 512:(qb + 1) * 512],
                start=True, stop=True),
              reads=[r_qraw, rconst], writes=[br2])
            t1 = qt_sb[:, 0:512]
            t2 = qt_sb[:, 512:1024]
            E("V", lambda e, qb=qb, t1=t1: e.tensor_tensor(
                t1, qraw_sb[:, qb * 512:(qb + 1) * 512],
                qcos_sb[:, qb * 512:(qb + 1) * 512], ALU.mult),
              reads=[r_qraw, r_qtab], writes=[r_qt])
            E("V", lambda e, pv2=pv2, qb=qb, t2=t2: e.tensor_tensor(
                t2, pv2[0:64, :], qsin_sb[:, qb * 512:(qb + 1) * 512], ALU.mult),
              reads=[br2, r_qtab], writes=[r_qt])
            E("V", lambda e, qb=qb, t1=t1, t2=t2: e.tensor_tensor(
                qlo_sb[:, qb * 512:(qb + 1) * 512], t1, t2, ALU.add),
              reads=[r_qt], writes=[r_qlo])
        for qt in range(NQT):
            np2 = KCS[qt] // 2

            def emit_scores(j2, qt=qt, np2=np2):
                sslot = j2 % 3
                sc = pB_sc[:, sslot * 1024:(sslot + 1) * 1024]
                bsc = bB_sc[sslot]
                kc0 = 2 * j2

                def f_sc(e, kc0=kc0, qt=qt, sc=sc):
                    for half in range(2):
                        kc = kc0 + half
                        e.matmul(sc[:, half * 512:(half + 1) * 512],
                                 khi_sb[:, kc * 128:(kc + 1) * 128],
                                 qhi_sb[:, qt * 512:(qt + 1) * 512],
                                 start=True, stop=False)
                        ins = e.matmul(sc[:, half * 512:(half + 1) * 512],
                                       kr_sb[:, kc * 128:(kc + 1) * 128],
                                       qlo_sb[:, qt * 512:(qt + 1) * 512],
                                       start=False, stop=True)
                    return ins
                E("T", f_sc, reads=[r_khi, r_qhi, r_qlo, r_kr], writes=list(bsc))
                pslice = p_sb[:, sslot * 1024:(sslot + 1) * 1024]
                rp = r_p[sslot]
                E("S", lambda e, sc=sc, pq=pslice: e.activation(
                    pq, sc, AF.Exp, bias=ebias_sb[:, 0:1], scale=SCALE),
                  reads=list(bsc), writes=[rp])
                E("V", lambda e, pq=pslice: e.tensor_scalar(
                    pq, pq, 1.0, PLO, ALU.min, ALU.max), writes=[rp])
                if j2 >= np2 - 4:
                    kc0b = 2 * j2
                    for half in range(2):
                        kc = kc0b + half
                        E("V", lambda e, pq=pslice, half=half, qt=qt, kc=kc:
                          e.scalar_tensor_tensor(
                              pq[:, half * 512:(half + 1) * 512],
                              qpos_sb[:, qt * 512:(qt + 1) * 512],
                              kpos_sb[:, kc:kc + 1],
                              pq[:, half * 512:(half + 1) * 512],
                              ALU.is_ge, ALU.mult),
                          reads=[rconst, r_qtab], writes=[rp], haz=True)

            def emit_pv(j2, qt=qt):
                sslot = j2 % 3
                pslice = p_sb[:, sslot * 1024:(sslot + 1) * 1024]
                rp = r_p[sslot]
                kc0 = 2 * j2

                def f_pv(e, kc0=kc0, j2=j2, pq=pslice):
                    for half in range(2):
                        kc = kc0 + half
                        st = (j2 == 0 and half == 0)
                        e.matmul(pB_l[:], ones_c[:], pq[:, half * 512:(half + 1) * 512],
                                 start=st, stop=False)
                        ins = e.matmul(pB_acc[:], v_sb[:, kc * 128:(kc + 1) * 128],
                                       pq[:, half * 512:(half + 1) * 512],
                                       start=st, stop=False)
                    return ins
                E("T", f_pv, reads=[rp, r_v, rconst], writes=[bB_acc, bB_l])

            emit_scores(0)
            if np2 > 1:
                emit_scores(1)
            if pending_evac[0] is not None:
                pending_evac[0]()
                pending_evac[0] = None
            for j2 in range(np2):
                if j2 + 2 < np2:
                    emit_scores(j2 + 2)
                emit_pv(j2)

            def evac(h=h, qt=qt):
                aslot = qt % 2
                atile = atmp_sb[:, aslot * 512:(aslot + 1) * 512]
                ratile = r_atmp[aslot]
                E("V", lambda e, a=atile: e.tensor_copy(a, pB_acc[:]),
                  reads=[bB_acc], writes=[ratile])
                E("V", lambda e: e.reciprocal(rl_sb[:], pB_l[:]), reads=[bB_l],
                  writes=[r_rl], haz=True)
                E("T", lambda e: e.matmul(pB_acc[:], ones_r32[:], rl_sb[:],
                                          start=True, stop=True),
                  reads=[r_rl, rconst], writes=[bB_acc])
                E("V", lambda e, a=atile: e.tensor_tensor(a, a, pB_acc[:], ALU.mult),
                  reads=[bB_acc], writes=[ratile])
                E("S", lambda e, a=atile, h=h, qt=qt: e.dma_start(
                    attnd[:, h * OWN + qt * 512: h * OWN + qt * 512 + 512], a),
                  reads=[ratile], dma=True)
            pending_evac[0] = evac

    if pending_evac[0] is not None:
        pending_evac[0]()
        pending_evac[0] = None
    emit_casts(100)
    r_attnd = R()
    r_attnd.ready = ("dS", sch.cnt["dS"])
    free("qcos_sb")
    free("ckv_sb")
    sch.gate()

    wo_sb = sb("wo_sb", [128, 2 * NH * 512], BF16)
    r_wo = [R(), R()]
    attc_sb = sb("attc_sb", [128, 2 * NH * 512], BF16)
    r_attc = [R(), R()]
    attno_sb = sb("attno_sb", [128, 4 * D], F32)
    r_attno = [R() for _ in range(4)]
    sums_sb = sb("sums_sb", [128, 16], F32)
    r_sums = [R() for _ in range(4)]
    sqc_sb = sb("sqc_sb", [128, 4], F32)
    r_sqc = [R() for _ in range(4)]
    scrC = sb("scrC", [128, 2048], F32)
    r_scrC = R()
    stripC = sb("stripC", [128, 8], F32)
    r_stripC = R()
    xtm_sb = sb("xtm_sb", [128, 2 * D], F32)
    r_xtm = [R(), R()]
    x1b_sb = sb("x1b_sb", [128, 2 * D], F32)
    r_x1b = [R(), R()]
    pC, bC = pp("pC", [128, 2048])
    r_x1d = R()

    def ln_tm_strips(sumcol, sqcol, icol, ncol, rs, extra_reads):
        E("V", lambda e: e.tensor_scalar(ncol, sumcol, 1.0 / D, None, ALU.mult),
          reads=extra_reads, writes=[rs])
        E("V", lambda e: e.tensor_scalar(icol, sqcol, 1.0 / D, EPS, ALU.mult, ALU.add),
          reads=extra_reads, writes=[rs])
        E("V", lambda e: e.scalar_tensor_tensor(
            icol, ncol, ncol, icol, ALU.mult, ALU.subtract),
          writes=[rs], haz=True)
        E("V", lambda e: e.tensor_scalar(icol, icol, -1.0, None, ALU.mult),
          writes=[rs])
        E("S", lambda e: e.activation(icol, icol, AF.Ln), reads=[rs], writes=[rs])
        E("S", lambda e: e.activation(icol, icol, AF.Exp, scale=-0.5),
          reads=[rs], writes=[rs])
        E("V", lambda e: e.scalar_tensor_tensor(
            ncol, ncol, -1.0, icol, ALU.mult, ALU.mult),
          writes=[rs], haz=True)

    for qs in range(4):
        aslot = qs % 2
        attc = attc_sb[:, aslot * NH * 512:(aslot + 1) * NH * 512]
        rattc = r_attc[aslot]
        E("Y", lambda e, qs=qs, a=attc: e.dma_start(
            a.rearrange("p (h c) -> p h c", h=NH),
            attnd[:].rearrange("p (h t) -> p h t", h=NH)[:, :, qs * 512:(qs + 1) * 512]),
          reads=[r_attnd], writes=[rattc], dma=True)
        for ob in range(4):
            wslot = ob % 2
            wdst = wo_sb[:, wslot * NH * 512:(wslot + 1) * NH * 512]
            rwo = r_wo[wslot]
            E("Y", lambda e, ob=ob, w=wdst: e.dma_start(
                w.rearrange("p (h c) -> p h c", h=NH), s_wo[:, ob]),
              reads=[r_swo], writes=[rwo], dma=True)
            for qc4 in range(4):
                pv = pC[:, qc4 * 512:(qc4 + 1) * 512]
                bb = bC[qc4]

                def f_op(e, a=attc, qc4=qc4, w=wdst, pv=pv):
                    for hh in range(NH):
                        ins = e.matmul(pv, a[:, hh * 512 + qc4 * 128:hh * 512 + qc4 * 128 + 128],
                                       w[:, hh * 512:(hh + 1) * 512],
                                       start=(hh == 0), stop=(hh == NH - 1))
                    return ins
                E("T", f_op, reads=[rwo, rattc], writes=[bb])
                E("S", lambda e, pv=pv, qc4=qc4, ob=ob: e.activation(
                    attno_sb[:, qc4 * D + ob * 512:qc4 * D + ob * 512 + 512], pv,
                    AF.Copy),
                  reads=[bb], writes=[r_attno[qc4]])
        for qc4 in range(4):
            qc = qs * 4 + qc4
            asl = attno_sb[:, qc4 * D:(qc4 + 1) * D]
            xslot = qc4 % 2
            xtile = xtm_sb[:, xslot * D:(xslot + 1) * D]
            rx = r_xtm[xslot]
            E("Y", lambda e, qc=qc, xx=xtile: e.dma_start(xx, xo[qc * 128:(qc + 1) * 128, :]),
              writes=[rx], dma=True)
            E("V", lambda e, a=asl, qc4=qc4: e.scalar_tensor_tensor(
                scrC[:], a, 1.0, a, ALU.mult, ALU.mult,
                accum_out=sqc_sb[:, qc4:qc4 + 1]),
              reads=[r_attno[qc4]], writes=[r_scrC, r_sqc[qc4]], haz=True)
            scol = stripC[:, 0:1]
            icol = stripC[:, 1:2]
            ncol = stripC[:, 2:3]
            E("V", lambda e, a=asl, scol=scol: e.tensor_reduce(scol, a, AX.X, ALU.add),
              reads=[r_attno[qc4]], writes=[r_stripC], haz=True)
            ln_tm_strips(scol, sqc_sb[:, qc4:qc4 + 1], icol, ncol, r_stripC,
                         [r_sqc[qc4]])
            x1t = x1b_sb[:, xslot * D:(xslot + 1) * D]
            rx1 = r_x1b[xslot]
            E("V", lambda e, a=asl, x1t=x1t, icol=icol, ncol=ncol: e.tensor_scalar(
                x1t, a, icol, ncol, ALU.mult, ALU.add),
              reads=[r_attno[qc4], r_stripC], writes=[rx1])
            E("V", lambda e, x1t=x1t, xx=xtile: e.tensor_tensor(x1t, x1t, xx, ALU.add),
              reads=[rx], writes=[rx1])
            E("S", lambda e, x1t=x1t, qc=qc: e.dma_start(
                x1d[qc * 128:(qc + 1) * 128, :], x1t), reads=[rx1], writes=[r_x1d],
              dma=True)

    if "attno" in dump:
        rd = R()
        E("S", lambda e: e.dma_start(dmp["attno"][:], attno_sb[:]),
          reads=r_attno, writes=[rd], dma=True)
        E("S", lambda e: e.dma_start(dmp["strip"][:], stripC[:]),
          reads=[r_stripC], writes=[rd], dma=True)
        E("S", lambda e: e.dma_start(dmp["sums"][:], sums_sb[:]),
          reads=r_sums, writes=[rd], dma=True)
        E("S", lambda e: e.dma_start(dmp["sqc"][:], sqc_sb[:]),
          reads=r_sqc, writes=[rd], dma=True)
    free("wo_sb")
    sch.gate()

    y_sb = sb("y_sb", [128, NFC * 512], BF16)
    r_y = R()
    xnf_sb = sb("xnf_sb", [128, 16 * 512], BF16)
    r_xnf = R()
    xnt_sb = sb("xnt_sb", [128, 2048], BF16)
    r_xnt = R()
    x1t_sb = sb("x1t_sb", [128, 2 * 2048], F32)
    r_x1t = [R(), R()]
    fsr_sb = sb("fsr_sb", [128, 2 * FS_COLS], BF16)
    r_fsr = [R(), R()]
    a13_sb = sb("a13_sb", [128, 16 * 128], BF16)
    r_a13 = R()
    ash_sb = sb("ash_sb", [128, 8 * 512], BF16)
    r_ash = R()
    m256_sb = sb("m256_sb", [128, 4 * 512], BF16)
    r_m256 = R()
    rwb_sb = sb("rwb_sb", [128, 2 * 512], F32)
    r_rwb = R()
    lgE_sb = sb("lgE_sb", [128, 32], F32)
    r_lgE = R()
    g01_sb = sb("g01_sb", [128, 16], BF16)
    r_g01 = R()
    rwp_sb = sb("rwp_sb", [128, 2], F32)
    r_rwp = R()
    maskf_sb = sb("maskf_sb", [8, 1024], BF16)
    r_maskf = R()
    rws_sb = sb("rws_sb", [1, 1024], F32)
    r_rws = R()
    stripD = sb("stripD", [128, 16], F32)
    r_stripD = R()
    scrD = sb("scrD", [128, 2048], F32)
    r_scrD = R()
    z_sb = scrD[:, 0:1024]
    r_z = [R(), R()]
    x2_sb = scrD[:, 1024:1536]
    r_x2 = R()
    y01_sb = sb("y01_sb", [128, 4 * 512], BF16)
    r_y01 = [R(), R(), R(), R()]
    a2e_sb = sb("a2e_sb", [128, 4 * 512], BF16)
    r_a2e = R()
    w2r_sb = sb("w2r_sb", [128, 2 * 2 * 512], BF16)
    r_w2r = [R(), R()]
    fout_sb = sb("fout_sb", [128, 4 * 2048], BF16)
    r_fout = [R() for _ in range(4)]
    sumd_sb = sb("sumd_sb", [128, 16], F32)
    r_sumd = [R() for _ in range(4)]
    sqd_sb = sb("sqd_sb", [128, 4], F32)
    r_sqd = [R() for _ in range(4)]
    ob_sb = sb("ob_sb", [128, 2 * 2048], F32)
    r_ob = [R(), R()]

    pD_rot, bD_rot = pp("pD_rot", [128, 1536])
    pD_tr = None
    rotv = [(pD_rot[:, i * 512:(i + 1) * 512], bD_rot[i]) for i in range(3)]
    trp_t, trp_b = pp("pD_tr", [128, 1024], BF16)
    trp_b = trp_b[0]


    rotc = [0]

    def rot_next():
        i = rotc[0] % 3
        rotc[0] += 1
        return rotv[i]

    for tbk in range(4):
        for tc in range(4):
            qc = tbk * 4 + tc
            xslot = tc % 2
            x1t = x1t_sb[:, xslot * 2048:(xslot + 1) * 2048]
            rx1 = r_x1t[xslot]
            E("Y", lambda e, qc=qc, xx=x1t: e.dma_start(
                xx, x1d[qc * 128:(qc + 1) * 128, :]), reads=[r_x1d], writes=[rx1],
              dma=True)
            scol = stripD[:, 0:1]
            sqcol = stripD[:, 1:2]
            icol = stripD[:, 2:3]
            ncol = stripD[:, 3:4]
            E("V", lambda e, xx=x1t, scol=scol: e.tensor_reduce(scol, xx, AX.X, ALU.add),
              reads=[rx1], writes=[r_stripD], haz=True)
            E("V", lambda e, xx=x1t, sqcol=sqcol: e.scalar_tensor_tensor(
                scrD[:], xx, 1.0, xx, ALU.mult, ALU.mult, accum_out=sqcol),
              reads=[rx1], writes=[r_scrD, r_stripD], haz=True)
            ln_tm_strips(scol, sqcol, icol, ncol, r_stripD, [])
            E("V", lambda e, xx=x1t, icol=icol, ncol=ncol: e.tensor_scalar(
                xnt_sb[:], xx, icol, ncol, ALU.mult, ALU.add),
              reads=[rx1, r_stripD], writes=[r_xnt])
            for b8 in range(2):
                def f_tr(e, b8=b8):
                    for i in range(8):
                        ins = e.transpose(trp_t[:, i * 128:(i + 1) * 128],
                                          xnt_sb[:, (b8 * 8 + i) * 128:(b8 * 8 + i) * 128 + 128],
                                          ident_sb[:])
                    return ins
                E("T", f_tr, reads=[r_xnt, rconst], writes=[trp_b])
                E("V", lambda e, b8=b8, tc=tc: e.tensor_copy(
                    xnf_sb[:].rearrange("p (i t) -> p i t", i=16)
                    [:, b8 * 8:(b8 + 1) * 8, tc * 128:(tc + 1) * 128],
                    trp_t[:].rearrange("p (i t) -> p i t", i=8)),
                  reads=[trp_b], writes=[r_xnf])
            pvl, bvl = rotv[0]
            lgp = pvl[:, 0:8]

            def f_lg(e, tc=tc, lgp=lgp):
                for i in range(16):
                    ins = e.matmul(lgp, xnf_sb[:, i * 512 + tc * 128:i * 512 + tc * 128 + 128],
                                   wg_sb[:, i * 8:(i + 1) * 8],
                                   start=(i == 0), stop=(i == 15))
                return ins
            E("T", f_lg, reads=[r_xnf, rconst], writes=[bvl])
            lg = lgE_sb[:, 0:8]
            Ee = lgE_sb[:, 8:16]
            E2 = lgE_sb[:, 16:24]
            c0 = lgE_sb[:, 24:25]
            c1 = lgE_sb[:, 25:26]
            E("V", lambda e, lgp=lgp: e.tensor_copy(lg, lgp), reads=[bvl], writes=[r_lgE], haz=True, tag=f"rt{tc}:cplg")
            E("V", lambda e: e.tensor_reduce(c0, lg, AX.X, ALU.max), writes=[r_lgE], haz=True, tag=f"rt{tc}:max")
            E("V", lambda e: e.tensor_scalar(c1, c0, -1.0, None, ALU.mult), writes=[r_lgE], haz=True, tag=f"rt{tc}:negc")
            E("S", lambda e: e.activation(Ee, lg, AF.Exp, bias=c1), reads=[r_lgE],
              writes=[r_lgE], tag=f"rt{tc}:exp")
            g0 = g01_sb[:, 0:8]
            g1 = g01_sb[:, 8:16]
            E("V", lambda e: e.tensor_scalar(g0, lg, c0, None, ALU.is_equal),
              reads=[r_lgE], writes=[r_g01], haz=True)
            E("V", lambda e: e.tensor_scalar(E2, g0, -1.0, 1.0, ALU.mult, ALU.add),
              reads=[r_g01], writes=[r_lgE], haz=True)
            E("V", lambda e: e.tensor_tensor(E2, E2, Ee, ALU.mult), writes=[r_lgE], haz=True)
            E("V", lambda e: e.tensor_reduce(c0, E2, AX.X, ALU.max), writes=[r_lgE], haz=True)
            E("V", lambda e: e.tensor_scalar(g1, E2, c0, None, ALU.is_equal),
              reads=[r_lgE], writes=[r_g01], haz=True)
            rw0 = rwp_sb[:, 0:1]
            rw1 = rwp_sb[:, 1:2]
            E("V", lambda e: e.tensor_scalar(c1, c0, 1.0, None, ALU.add), writes=[r_lgE], haz=True)
            E("V", lambda e: e.reciprocal(rw0, c1), reads=[r_lgE], writes=[r_rwp], haz=True, tag=f"rt{tc}:recip")
            E("V", lambda e: e.tensor_tensor(rw1, c0, rw0, ALU.mult),
              reads=[r_lgE], writes=[r_rwp], haz=True)
            def f_trm(e, tc=tc):
                e.transpose(trp_t[0:8, 0 * 512 + tc * 128:0 * 512 + tc * 128 + 128],
                            g01_sb[:, 0:8], ident_sb[:])
                return e.transpose(trp_t[0:8, 1 * 512 + tc * 128:1 * 512 + tc * 128 + 128],
                                   g01_sb[:, 8:16], ident_sb[:])
            E("T", f_trm, reads=[r_g01, rconst], writes=[trp_b])
            E("V", lambda e, tc=tc: e.tensor_copy(
                maskf_sb[:, 0 * 512 + tc * 128:0 * 512 + tc * 128 + 128],
                trp_t[0:8, 0 * 512 + tc * 128:0 * 512 + tc * 128 + 128]),
              reads=[trp_b], writes=[r_maskf])
            E("V", lambda e, tc=tc: e.tensor_copy(
                maskf_sb[:, 1 * 512 + tc * 128:1 * 512 + tc * 128 + 128],
                trp_t[0:8, 1 * 512 + tc * 128:1 * 512 + tc * 128 + 128]),
              reads=[trp_b], writes=[r_maskf])
            pvr, bvr = rotv[1]

            def f_trw(e, pvr=pvr):
                e.transpose(pvr[0:1, 0:128], rwp_sb[:, 0:1], id32_sb[:])
                return e.transpose(pvr[0:1, 128:256], rwp_sb[:, 1:2], id32_sb[:])
            E("T", f_trw, reads=[r_rwp, rconst], writes=[bvr])
            E("V", lambda e, tc=tc, pvr=pvr: e.tensor_copy(
                rws_sb[0:1, tc * 128:(tc + 1) * 128], pvr[0:1, 0:128]),
              reads=[bvr], writes=[r_rws])
            E("V", lambda e, tc=tc, pvr=pvr: e.tensor_copy(
                rws_sb[0:1, 512 + tc * 128:512 + (tc + 1) * 128], pvr[0:1, 128:256]),
              reads=[bvr], writes=[r_rws])
        for k in range(2):
            for j in range(2):
                pv, bv = rot_next()

                def f_ex(e, k=k, j=j, pv=pv):
                    return e.matmul(pv, e8_sb[:, j * 128:(j + 1) * 128],
                                    maskf_sb[0:8, k * 512:(k + 1) * 512],
                                    start=True, stop=True)
                E("T", f_ex, reads=[r_maskf, rconst], writes=[bv])
                E("V", lambda e, k=k, j=j, pv=pv: e.tensor_copy(
                    m256_sb[:, (k * 2 + j) * 512:(k * 2 + j) * 512 + 512], pv),
                  reads=[bv], writes=[r_m256])
            pv, bv = rot_next()
            E("T", lambda e, k=k, pv=pv: e.matmul(
                pv, ones_r32[:], rws_sb[0:1, k * 512:(k + 1) * 512], start=True, stop=True),
              reads=[r_rws, rconst], writes=[bv])
            E("V", lambda e, k=k, pv=pv: e.tensor_copy(
                rwb_sb[:, k * 512:(k + 1) * 512], pv), reads=[bv], writes=[r_rwb])
        for (m, asrc) in ((0, s_a1s), (1, s_a3s)):
            for j in range(2):
                E("Y", lambda e, asrc=asrc, j=j: e.dma_start(
                    a13_sb[:].rearrange("p (a c) -> p a c", a=16), asrc[:, :, j, :]),
                  reads=[r_sa13], writes=[r_a13], dma=True)
                pv, bv = rot_next()

                def f_a(e, j=j, pv=pv):
                    for i in range(16):
                        ins = e.matmul(pv, a13_sb[:, i * 128:(i + 1) * 128],
                                       xnf_sb[:, i * 512:(i + 1) * 512],
                                       start=(i == 0), stop=(i == 15))
                    return ins
                E("T", f_a, reads=[r_a13, r_xnf], writes=[bv])
                for k in range(2):
                    E("V", lambda e, m=m, k=k, j=j, pv=pv: e.tensor_tensor(
                        ash_sb[:, ((m * 2 + k) * 2 + j) * 512:((m * 2 + k) * 2 + j) * 512 + 512],
                        pv, m256_sb[:, (k * 2 + j) * 512:(k * 2 + j) * 512 + 512],
                        ALU.mult),
                      reads=[bv, r_m256], writes=[r_ash])
                E("V", lambda e, m=m, j=j: e.tensor_tensor(
                    ash_sb[:, ((m * 2 + 1) * 2 + j) * 512:((m * 2 + 1) * 2 + j) * 512 + 512],
                    ash_sb[:, ((m * 2 + 1) * 2 + j) * 512:((m * 2 + 1) * 2 + j) * 512 + 512],
                    ash_sb[:, ((m * 2 + 0) * 2 + j) * 512:((m * 2 + 0) * 2 + j) * 512 + 512],
                    ALU.subtract), writes=[r_ash])
        pD_a2a, bD_a2a = pp(f"pD_a2a{tbk}", [128, 1024])
        pD_a2b, bD_a2b = pp(f"pD_a2b{tbk}", [128, 1024])
        E("Y", lambda e: e.dma_start(
            fsr_sb[:, 0:FS_COLS], s_fs[0]), reads=[r_sfs], writes=[r_fsr[0]], dma=True)
        for fc in range(NFC):
            fslot = fc % 2
            fsl = fsr_sb[:, fslot * FS_COLS:(fslot + 1) * FS_COLS]
            rfs = r_fsr[fslot]
            pv1, bv1 = rot_next()

            def f_x1(e, fsl=fsl, pv1=pv1):
                for i in range(16):
                    e.matmul(pv1, fsl[:, FS_W1 + i * 128:FS_W1 + i * 128 + 128],
                             xnf_sb[:, i * 512:(i + 1) * 512],
                             start=(i == 0), stop=False)
                for j in range(2):
                    ins = e.matmul(pv1, fsl[:, FS_B1 + j * 128:FS_B1 + j * 128 + 128],
                                   ash_sb[:, (0 * 2 + j) * 512:(0 * 2 + j) * 512 + 512],
                                   start=False, stop=False)
                return ins
            E("T", f_x1, reads=[rfs, r_xnf, r_ash], writes=[bv1])
            pv3, bv3 = rot_next()

            def f_x3(e, fsl=fsl, pv3=pv3):
                for i in range(16):
                    e.matmul(pv3, fsl[:, FS_W3 + i * 128:FS_W3 + i * 128 + 128],
                             xnf_sb[:, i * 512:(i + 1) * 512],
                             start=(i == 0), stop=False)
                for j in range(2):
                    ins = e.matmul(pv3, fsl[:, FS_B3 + j * 128:FS_B3 + j * 128 + 128],
                                   ash_sb[:, ((2 + 0) * 2 + j) * 512:((2 + 0) * 2 + j) * 512 + 512],
                                   start=False, stop=False)
                return ins
            E("T", f_x3, reads=[rfs, r_xnf, r_ash], writes=[bv3])
            for k in range(2):
                if k == 1:
                    def f_x1d(e, fsl=fsl, pv1=pv1):
                        for j in range(2):
                            ins = e.matmul(pv1, fsl[:, FS_B1 + j * 128:FS_B1 + j * 128 + 128],
                                           ash_sb[:, (1 * 2 + j) * 512:(1 * 2 + j) * 512 + 512],
                                           start=False, stop=(j == 1))
                        return ins
                    E("T", f_x1d, reads=[rfs, r_ash], writes=[bv1])

                    def f_x3d(e, fsl=fsl, pv3=pv3):
                        for j in range(2):
                            ins = e.matmul(pv3, fsl[:, FS_B3 + j * 128:FS_B3 + j * 128 + 128],
                                           ash_sb[:, ((2 + 1) * 2 + j) * 512:((2 + 1) * 2 + j) * 512 + 512],
                                           start=False, stop=(j == 1))
                        return ins
                    E("T", f_x3d, reads=[rfs, r_ash], writes=[bv3])
                fpar = fc % 2
                zsl = z_sb[:, k * 512:(k + 1) * 512]
                rz = r_z[k]
                E("S", lambda e, pv1=pv1, z=zsl: e.activation(z, pv1, AF.Silu),
                  reads=[bv1], writes=[rz])
                E("V", lambda e, z=zsl, pv3=pv3: e.tensor_tensor(
                    x2_sb[:], z, pv3, ALU.mult), reads=[rz, bv3], writes=[r_x2])
                ysl = y01_sb[:, (fpar * 2 + k) * 512:(fpar * 2 + k) * 512 + 512]
                ry = r_y01[fpar * 2 + k]
                E("V", lambda e, k=k, y=ysl: e.tensor_tensor(
                    y, x2_sb[:], rwb_sb[:, k * 512:(k + 1) * 512], ALU.mult),
                  reads=[r_x2, r_rwb], writes=[ry])
                if fc > 0:
                    ppar = (fc - 1) % 2
                    ypsl = y01_sb[:, (ppar * 2 + k) * 512:(ppar * 2 + k) * 512 + 512]
                    rpy = r_y01[ppar * 2 + k]
                    a2b = (bD_a2a if k == 0 else bD_a2b)
                    a2t = (pD_a2a if k == 0 else pD_a2b)
                    pfsl = fsr_sb[:, ((fc - 1) % 2) * FS_COLS:((fc - 1) % 2) * FS_COLS + FS_COLS]
                    prfs = r_fsr[(fc - 1) % 2]

                    def f_a2(e, pfsl=pfsl, y=ypsl, a2t=a2t, fc=fc):
                        for j in range(2):
                            ins = e.matmul(a2t[:, j * 512:(j + 1) * 512],
                                           pfsl[:, FS_A2 + j * 128:FS_A2 + j * 128 + 128],
                                           y, start=(fc - 1 == 0), stop=False)
                        return ins
                    E("T", f_a2, reads=[prfs, rpy], writes=list(a2b))
            E("V", lambda e, fc=fc, fpar=fpar: e.tensor_tensor(
                y_sb[:, fc * 512:(fc + 1) * 512],
                y01_sb[:, fpar * 1024:fpar * 1024 + 512],
                y01_sb[:, fpar * 1024 + 512:fpar * 1024 + 1024], ALU.add),
              reads=[r_y01[fpar * 2], r_y01[fpar * 2 + 1]], writes=[r_y])
            if fc + 1 < NFC:
                nslot = (fc + 1) % 2
                E("Y", lambda e, fc=fc, ns=nslot: e.dma_start(
                    fsr_sb[:, ns * FS_COLS:(ns + 1) * FS_COLS], s_fs[fc + 1]),
                  reads=[r_sfs], writes=[r_fsr[nslot]], dma=True)
        for k in range(2):
            lpar = (NFC - 1) % 2
            ypsl = y01_sb[:, (lpar * 2 + k) * 512:(lpar * 2 + k) * 512 + 512]
            rpy = r_y01[lpar * 2 + k]
            a2b = (bD_a2a if k == 0 else bD_a2b)
            a2t = (pD_a2a if k == 0 else pD_a2b)
            lfsl = fsr_sb[:, lpar * FS_COLS:lpar * FS_COLS + FS_COLS]

            def f_a2f(e, lfsl=lfsl, y=ypsl, a2t=a2t):
                for j in range(2):
                    ins = e.matmul(a2t[:, j * 512:(j + 1) * 512],
                                   lfsl[:, FS_A2 + j * 128:FS_A2 + j * 128 + 128],
                                   y, start=False, stop=True)
                return ins
            E("T", f_a2f, reads=[r_fsr[lpar], rpy], writes=list(a2b))
        for k in range(2):
            a2t = (pD_a2a if k == 0 else pD_a2b)
            a2b = (bD_a2a if k == 0 else bD_a2b)
            for j in range(2):
                E("V", lambda e, k=k, j=j, a2t=a2t: e.tensor_tensor(
                    a2e_sb[:, (k * 2 + j) * 512:(k * 2 + j) * 512 + 512],
                    a2t[:, j * 512:(j + 1) * 512],
                    m256_sb[:, (k * 2 + j) * 512:(k * 2 + j) * 512 + 512],
                    ALU.mult),
                  reads=[a2b[j], r_m256], writes=[r_a2e])
        if tbk == 0 and "ffn" in dump:
            rd0 = R()
            E("S", lambda e: e.dma_start(dmp["lgE"][:], lgE_sb[:]), reads=[r_lgE],
              writes=[rd0], dma=True)
            E("S", lambda e: e.dma_start(dmp["g01"][:], g01_sb[:]), reads=[r_g01],
              writes=[rd0], dma=True)
            E("S", lambda e: e.dma_start(dmp["rwp"][:], rwp_sb[:]), reads=[r_rwp],
              writes=[rd0], dma=True)
            E("S", lambda e: e.dma_start(dmp["rws"][:], rws_sb[:]), reads=[r_rws],
              writes=[rd0], dma=True)
            E("S", lambda e: e.dma_start(dmp["xnf"][:], xnf_sb[:]), reads=[r_xnf],
              writes=[rd0], dma=True)
            E("S", lambda e: e.dma_start(dmp["maskf"][:], maskf_sb[:]), reads=[r_maskf],
              writes=[rd0], dma=True)
            E("S", lambda e: e.dma_start(dmp["rwb"][:], rwb_sb[:]), reads=[r_rwb],
              writes=[rd0], dma=True)
            E("S", lambda e: e.dma_start(dmp["ash"][:], ash_sb[:]), reads=[r_ash],
              writes=[rd0], dma=True)
            E("S", lambda e: e.dma_start(dmp["y"][:], y_sb[:]), reads=[r_y],
              writes=[rd0], dma=True)
            E("S", lambda e: e.dma_start(dmp["a2e"][:], a2e_sb[:]), reads=[r_a2e],
              writes=[rd0], dma=True)
        free(f"pD_a2a{tbk}")
        pD_dn, bD_dn = pp(f"pD_dn{tbk}", [128, 2048])
        for ob in range(4):
            for bi in range(NFC // 2):
                wslot = bi % 2
                w2t = w2r_sb[:, wslot * 1024:(wslot + 1) * 1024]
                rw2 = r_w2r[wslot]
                E("Y", lambda e, ob=ob, bi=bi, w=w2t: e.dma_start(
                    w.rearrange("p (a c) -> p a c", a=2),
                    s_w2[:, ob, bi * 2:(bi + 1) * 2, :]),
                  reads=[r_sw2], writes=[rw2], dma=True)

                def f_dn(e, bi=bi, w=w2t, pd=pD_dn):
                    for f2 in range(2):
                        fc = bi * 2 + f2
                        for tc in range(4):
                            ins = e.matmul(
                                pd[:, tc * 512:(tc + 1) * 512],
                                y_sb[:, fc * 512 + tc * 128:fc * 512 + tc * 128 + 128],
                                w[:, f2 * 512:(f2 + 1) * 512],
                                start=(bi == 0 and f2 == 0), stop=False)
                    return ins
                E("T", f_dn, reads=[rw2, r_y], writes=list(bD_dn))

            def f_l2(e, ob=ob, pd=pD_dn):
                for tc in range(4):
                    for k in range(2):
                        for j in range(2):
                            ins = e.matmul(
                                pd[:, tc * 512:(tc + 1) * 512],
                                a2e_sb[:, (k * 2 + j) * 512 + tc * 128:(k * 2 + j) * 512 + tc * 128 + 128],
                                b2s_sb[:, j * D + ob * 512:j * D + ob * 512 + 512],
                                start=False, stop=(k == 1 and j == 1))
                return ins
            E("T", f_l2, reads=[r_a2e, rconst], writes=list(bD_dn))
            for tc in range(4):
                E("S", lambda e, tc=tc, ob=ob, pd=pD_dn: e.activation(
                    fout_sb[:, tc * 2048 + ob * 512:tc * 2048 + ob * 512 + 512],
                    pd[:, tc * 512:(tc + 1) * 512], AF.Copy),
                  reads=[bD_dn[tc]], writes=[r_fout[tc]])
        for tc in range(4):
            qc = tbk * 4 + tc
            fsl = fout_sb[:, tc * 2048:(tc + 1) * 2048]
            E("V", lambda e, f=fsl, tc=tc: e.scalar_tensor_tensor(
                scrD[:], f, 1.0, f, ALU.mult, ALU.mult,
                accum_out=sqd_sb[:, tc:tc + 1]),
              reads=[r_fout[tc]], writes=[r_scrD, r_sqd[tc]], haz=True)
            scol = stripD[:, 4:5]
            icol = stripD[:, 5:6]
            ncol = stripD[:, 6:7]
            E("V", lambda e, f=fsl, scol=scol: e.tensor_reduce(scol, f, AX.X, ALU.add),
              reads=[r_fout[tc]], writes=[r_stripD], haz=True)
            ln_tm_strips(scol, sqd_sb[:, tc:tc + 1], icol, ncol, r_stripD,
                         [r_sqd[tc]])
            oslot = tc % 2
            obuf = ob_sb[:, oslot * 2048:(oslot + 1) * 2048]
            ro = r_ob[oslot]
            xslot = tc % 2
            x1t = x1t_sb[:, xslot * 2048:(xslot + 1) * 2048]
            rx1 = r_x1t[xslot]
            E("Y", lambda e, qc=qc, xx=x1t: e.dma_start(
                xx, x1d[qc * 128:(qc + 1) * 128, :]), reads=[r_x1d], writes=[rx1],
              dma=True)
            E("V", lambda e, f=fsl, o=obuf, icol=icol, ncol=ncol: e.tensor_scalar(
                o, f, icol, ncol, ALU.mult, ALU.add),
              reads=[r_fout[tc], r_stripD], writes=[ro])
            E("V", lambda e, o=obuf, xx=x1t: e.tensor_tensor(o, o, xx, ALU.add),
              reads=[rx1], writes=[ro])
            E("S", lambda e, o=obuf, qc=qc: e.dma_start(
                out[qc * 128:(qc + 1) * 128, :], o), reads=[ro], dma=True)
        if tbk == 0 and "ffn" in dump:
            rd1 = R()
            E("S", lambda e: e.dma_start(dmp["fout"][:], fout_sb[:]),
              reads=r_fout, writes=[rd1], dma=True)
        free(f"pD_dn{tbk}")

    sch.tail()
    with nc.Block() as block:
        sch.finalize(block)
    free_all = list(stack)
    while stack:
        stack.pop()[1].__exit__(None, None, None)
    sch.close()
    return nc


def kernel(**inputs):
    maps = _host_inputs(inputs)
    nc = build_nc()
    res = run_bass_kernel_spmd(nc, maps, core_ids=list(range(8)))
    x = np.asarray(inputs["L0_attn_x"])
    outp = np.empty((4, SEQ, D), dtype=np.float32)
    for c in range(8):
        rows = _own_rows(c % 2)
        outp[c // 2][rows] = res.results[c]["out"]
    return outp


# version 06
# speedup vs baseline: 1.1928x; 1.1928x over previous
import math
import numpy as np
import ml_dtypes
import concourse.bass as bass
import concourse.mybir as mybir
from concourse.bass_utils import run_bass_kernel_spmd

F32 = mybir.dt.float32
BF16 = mybir.dt.bfloat16
AF = mybir.ActivationFunctionType
ALU = mybir.AluOpType
AX = mybir.AxisListType

D = 2048
DL = 512
NH = 16
SEQ = 4096
FF = 7168
ER = 256
OWN = 2048
QT = 512
NQT = 4
KCS = (8, 16, 24, 32)
NFC = FF // 128
SCALE = 1.0 / math.sqrt(192.0)
EBIAS = -80.0 * SCALE
PLO = math.exp(2.0 * EBIAS)
THETA = 10000.0
EPS = 1e-5

POS_EVEN = (0, 3, 4, 7)
POS_ODD = (1, 2, 5, 6)

FS_W1 = 0
FS_W3 = 2048
FS_B1 = 4096
FS_B3 = 4096 + 256
FS_A2 = 4096 + 512
FS_COLS = 4096 + 768


def _own_rows(parity):
    pos = POS_EVEN if parity == 0 else POS_ODD
    r = []
    for p in pos:
        r.extend(range(p * QT, (p + 1) * QT))
    return np.array(r, dtype=np.int64)


def _lhsT4(wt, mj=128):
    din, dout = wt.shape
    ni, nj = din // 128, dout // mj
    return np.ascontiguousarray(wt.reshape(ni, 128, nj, mj).transpose(1, 0, 2, 3))


def _rope_perm():
    return np.concatenate([np.arange(0, 64, 2), np.arange(1, 64, 2)])


def _rope_tables(positions):
    inv = THETA ** (-np.arange(0, 64, 2, dtype=np.float64) / 64.0)
    ang = positions.astype(np.float64)[None, :] * inv[:, None]
    cos = np.cos(ang)
    sin = np.sin(ang)
    cos2 = np.concatenate([cos, cos], 0).astype(np.float32)
    sin2 = np.concatenate([-sin, sin], 0).astype(np.float32)
    return np.ascontiguousarray(cos2), np.ascontiguousarray(sin2)


def _host_inputs(inp):
    x = np.asarray(inp["L0_attn_x"], dtype=np.float32)
    perm = _rope_perm()
    sh = {}
    sh["wdq"] = _lhsT4(np.ascontiguousarray(np.asarray(inp["L0_attn_W_DQ"]).T))
    sh["wdkv"] = _lhsT4(np.ascontiguousarray(np.asarray(inp["L0_attn_W_DKV"]).T))
    sh["wkr"] = _lhsT4(np.ascontiguousarray(np.asarray(inp["L0_attn_W_KR"]).T[:, perm]), mj=64)
    sh["wuq"] = _lhsT4(np.ascontiguousarray(np.asarray(inp["L0_attn_W_UQ"]).T))
    wqrt = np.asarray(inp["L0_attn_W_QR"]).T.reshape(DL, NH, 64)[:, :, perm]
    sh["wqr"] = _lhsT4(np.ascontiguousarray(wqrt.reshape(DL, NH * 64)), mj=64)
    sh["wuk"] = _lhsT4(np.ascontiguousarray(np.asarray(inp["L0_attn_W_UK"]).T))
    sh["wuv"] = _lhsT4(np.ascontiguousarray(np.asarray(inp["L0_attn_W_UV"]).T))
    wot = np.asarray(inp["L0_attn_W_O"]).T
    sh["wo"] = np.ascontiguousarray(wot.reshape(NH, 128, 4, 512).transpose(1, 2, 0, 3))
    sh["wg"] = np.ascontiguousarray(
        np.asarray(inp["L0_ffn_Wg"]).T.reshape(16, 128, 8).transpose(1, 0, 2))

    w1t = np.asarray(inp["L0_ffn_W1"]).T
    w3t = np.asarray(inp["L0_ffn_W3"]).T
    b1st = np.asarray(inp["L0_ffn_B1"]).transpose(0, 2, 1).reshape(ER, FF)
    b3st = np.asarray(inp["L0_ffn_B3"]).transpose(0, 2, 1).reshape(ER, FF)
    a2st = np.ascontiguousarray(np.asarray(inp["L0_ffn_A2"]).reshape(ER, FF).T)
    fs = np.empty((NFC, 128, FS_COLS), dtype=np.float32)
    fs[:, :, FS_W1:FS_W1 + 2048] = w1t.reshape(16, 128, NFC, 128).transpose(2, 1, 0, 3).reshape(NFC, 128, 2048)
    fs[:, :, FS_W3:FS_W3 + 2048] = w3t.reshape(16, 128, NFC, 128).transpose(2, 1, 0, 3).reshape(NFC, 128, 2048)
    fs[:, :, FS_B1:FS_B1 + 256] = b1st.reshape(2, 128, NFC, 128).transpose(2, 1, 0, 3).reshape(NFC, 128, 256)
    fs[:, :, FS_B3:FS_B3 + 256] = b3st.reshape(2, 128, NFC, 128).transpose(2, 1, 0, 3).reshape(NFC, 128, 256)
    fs[:, :, FS_A2:FS_A2 + 256] = a2st.reshape(NFC, 128, 256)
    sh["fstream"] = fs
    sh["a1s"] = _lhsT4(np.ascontiguousarray(np.asarray(inp["L0_ffn_A1"]).reshape(ER, D).T))
    sh["a3s"] = _lhsT4(np.ascontiguousarray(np.asarray(inp["L0_ffn_A3"]).reshape(ER, D).T))
    b2st = np.asarray(inp["L0_ffn_B2"]).transpose(0, 2, 1).reshape(ER, D)
    sh["b2s"] = np.ascontiguousarray(b2st.reshape(2, 128, D).transpose(1, 0, 2))
    w2t = np.asarray(inp["L0_ffn_W2"]).T
    sh["w2"] = np.ascontiguousarray(w2t.reshape(NFC, 128, 4, 512).transpose(1, 2, 0, 3))

    sh["ident"] = np.eye(128, dtype=ml_dtypes.bfloat16)
    sh["ident32"] = np.eye(128, dtype=np.float32)
    sh["ones_col"] = np.ones((128, 1), dtype=ml_dtypes.bfloat16)
    sh["ones_row"] = np.ones((1, 128), dtype=ml_dtypes.bfloat16)
    sh["ones_row32"] = np.ones((1, 128), dtype=np.float32)
    sh["ebias"] = np.full((128, 1), EBIAS, dtype=np.float32)
    psm = np.zeros((64, 64), dtype=ml_dtypes.bfloat16)
    for i in range(32):
        psm[i, 32 + i] = 1
        psm[32 + i, i] = 1
    sh["ps64"] = psm
    e8 = np.zeros((8, 256), dtype=ml_dtypes.bfloat16)
    for e in range(8):
        e8[e, e * 32:(e + 1) * 32] = 1
    sh["e8"] = e8
    sh["kpos"] = np.ascontiguousarray(
        (np.arange(128)[:, None] + 128 * np.arange(32)[None, :]).astype(np.float32))
    kc2, ks2 = _rope_tables(np.arange(SEQ))
    sh["kcos"] = kc2
    sh["ksin"] = ks2

    maps = []
    for c in range(8):
        b, par = c // 2, c % 2
        rows = _own_rows(par)
        m = dict(sh)
        xb = x[b]
        xt = np.ascontiguousarray(xb.T)
        m["xT"] = np.ascontiguousarray(xt.reshape(16, 128, SEQ).transpose(1, 0, 2))
        xop = np.ascontiguousarray(xb[rows])
        m["xo"] = xop
        m["xTo"] = np.ascontiguousarray(xop.T.reshape(16, 128, OWN).transpose(1, 0, 2))
        qc2, qs2 = _rope_tables(rows)
        m["qcos"] = qc2
        m["qsin"] = qs2
        m["qposb"] = np.ascontiguousarray(
            np.broadcast_to(rows.astype(np.float32)[None, :], (128, OWN)).copy())
        maps.append(m)
    return maps


class Res:
    __slots__ = ("ready", "users")

    def __init__(self):
        self.ready = None
        self.users = []


class Sch:
    ENGS = ("T", "V", "S", "P", "Y")

    def __init__(self, nc):
        self.nc = nc
        self.q = {e: [] for e in self.ENGS}
        self.sem = {}
        self.cnt = {}
        self.hiw = {e: {} for e in self.ENGS}
        self._ctx = []
        for name in ("sT", "sV", "sS", "dY", "dS", "dP"):
            cm = nc.semaphore(name)
            self.sem[name] = cm.__enter__()
            self._ctx.append(cm)
            self.cnt[name] = 0
        self.esem = {"T": "sT", "V": "sV", "S": "sS"}
        self.qsem = {"P": "dP", "Y": "dY", "S": "dS"}
        self.trace = []

    def _wait(self, eng, toks):
        best = {}
        own = self.esem.get(eng)
        for t in toks:
            if t is None:
                continue
            s, v = t
            if s == own:
                continue
            if best.get(s, -1) < v:
                best[s] = v
        for s, v in best.items():
            if self.hiw[eng].get(s, 0) >= v:
                continue
            self.hiw[eng][s] = v
            h = self.sem[s]
            self.trace.append((eng, "wait", s, v))
            self.q[eng].append(lambda e, h=h, v=v: e.wait_ge(h, v))

    def emit(self, eng, fn, reads=(), writes=(), dma=False, ndma=1, haz=False, tag=""):
        toks = []
        for r in reads:
            if r is not None and r.ready is not None:
                toks.append(r.ready)
        for w in writes:
            toks.extend(w.users)
            if w.ready is not None:
                toks.append(w.ready)
        self._wait(eng, toks)
        sname = self.qsem[eng] if dma else self.esem[eng]
        amt = 16 * ndma if dma else 1
        self.cnt[sname] += amt
        tok = (sname, self.cnt[sname])
        h = self.sem[sname]
        inc = 16 if dma else 1
        if haz:
            vna = self.vnop_ap
            def fn2(e, fn=fn, vna=vna):
                fn(e)
                return e.tensor_scalar(vna, vna, 1.0, None, ALU.mult)
            fn = fn2
        self.trace.append((eng, "op:" + tag, sname, tok[1]))
        self.q[eng].append(lambda e, fn=fn, h=h, a=inc: fn(e).then_inc(h, a))
        for r in reads:
            if r is not None:
                r.users.append(tok)
        for w in writes:
            w.ready = tok
            w.users = []
        return tok

    def gate(self):
        toks = [(s, self.cnt[s]) for s in ("sT", "sV", "sS", "dS")]
        for e in self.ENGS:
            self._wait(e, toks)

    def tail(self):
        toks = [(s, self.cnt[s]) for s in self.cnt]
        self._wait("Y", toks)

    def finalize(self, block):
        for attr, eng in (("tensor", "T"), ("vector", "V"), ("scalar", "S"),
                          ("gpsimd", "P"), ("sync", "Y")):
            ops = self.q[eng]

            def body(e, ops=ops):
                for op in ops:
                    op(e)
            getattr(block, attr)(body)

    def close(self):
        for cm in reversed(self._ctx):
            cm.__exit__(None, None, None)


def build_nc(dump=()):
    nc = bass.Bass("TRN2", target_bir_lowering=False)

    def din(name, shape, dtype=F32):
        return nc.dram_tensor(name, list(shape), dtype, kind="ExternalInput")

    xT = din("xT", [128, 16, SEQ])
    xTo = din("xTo", [128, 16, OWN])
    xo = din("xo", [OWN, D])
    wdq = din("wdq", [128, 16, 4, 128])
    wdkv = din("wdkv", [128, 16, 4, 128])
    wkr = din("wkr", [128, 16, 1, 64])
    wuq = din("wuq", [128, 4, 16, 128])
    wqr = din("wqr", [128, 4, 16, 64])
    wuk = din("wuk", [128, 4, 16, 128])
    wuv = din("wuv", [128, 4, 16, 128])
    wo = din("wo", [128, 4, NH, 512])
    wg = din("wg", [128, 16, 8])
    fstream = din("fstream", [NFC, 128, FS_COLS])
    a1s = din("a1s", [128, 16, 2, 128])
    a3s = din("a3s", [128, 16, 2, 128])
    b2s = din("b2s", [128, 2, D])
    w2 = din("w2", [128, 4, NFC, 512])
    ident = din("ident", [128, 128], BF16)
    ident32 = din("ident32", [128, 128], F32)
    ones_col = din("ones_col", [128, 1], BF16)
    ones_row = din("ones_row", [1, 128], BF16)
    ones_row32 = din("ones_row32", [1, 128], F32)
    ebias = din("ebias", [128, 1], F32)
    ps64 = din("ps64", [64, 64], BF16)
    e8 = din("e8", [8, 256], BF16)
    kpos = din("kpos", [128, 32])
    kcos = din("kcos", [64, SEQ])
    ksin = din("ksin", [64, SEQ])
    qcos = din("qcos", [64, OWN])
    qsin = din("qsin", [64, OWN])
    qposb = din("qposb", [128, OWN])

    out = nc.dram_tensor("out", [OWN, D], F32, kind="ExternalOutput")
    x1d = nc.dram_tensor("x1d", [OWN, D], F32,
                         kind="ExternalOutput" if "x1" in dump else "Internal")
    attnd = nc.dram_tensor("attnd", [128, NH * OWN], BF16,
                           kind="ExternalOutput" if "attn" in dump else "Internal")
    s_wo = nc.dram_tensor("s_wo", [128, 4, NH, 512], BF16)
    s_fs = nc.dram_tensor("s_fs", [NFC, 128, FS_COLS], BF16)
    s_w2 = nc.dram_tensor("s_w2", [128, 4, NFC, 512], BF16)
    s_a1s = nc.dram_tensor("s_a1s", [128, 16, 2, 128], BF16)
    s_a3s = nc.dram_tensor("s_a3s", [128, 16, 2, 128], BF16)

    dmp = {}
    if "ffn" in dump:
        dmp["xnf"] = nc.dram_tensor("d_xnf", [128, 16 * 512], BF16, kind="ExternalOutput")
        dmp["maskf"] = nc.dram_tensor("d_maskf", [8, 1024], BF16, kind="ExternalOutput")
        dmp["rwb"] = nc.dram_tensor("d_rwb", [128, 1024], F32, kind="ExternalOutput")
        dmp["ash"] = nc.dram_tensor("d_ash", [128, 8 * 512], BF16, kind="ExternalOutput")
        dmp["y"] = nc.dram_tensor("d_y", [128, NFC * 512], BF16, kind="ExternalOutput")
        dmp["a2e"] = nc.dram_tensor("d_a2e", [128, 4 * 512], BF16, kind="ExternalOutput")
        dmp["fout"] = nc.dram_tensor("d_fout", [128, 4 * 2048], BF16, kind="ExternalOutput")
        dmp["lgE"] = nc.dram_tensor("d_lgE", [128, 32], F32, kind="ExternalOutput")
        dmp["g01"] = nc.dram_tensor("d_g01", [128, 16], BF16, kind="ExternalOutput")
        dmp["rwp"] = nc.dram_tensor("d_rwp", [128, 2], F32, kind="ExternalOutput")
        dmp["rws"] = nc.dram_tensor("d_rws", [1, 1024], F32, kind="ExternalOutput")
    if "attno" in dump:
        dmp["attno"] = nc.dram_tensor("d_attno", [128, 4 * D], F32, kind="ExternalOutput")
        dmp["strip"] = nc.dram_tensor("d_strip", [128, 8], F32, kind="ExternalOutput")
        dmp["sums"] = nc.dram_tensor("d_sums", [128, 16], F32, kind="ExternalOutput")
        dmp["sqc"] = nc.dram_tensor("d_sqc", [128, 4], F32, kind="ExternalOutput")
    if "ckv" in dump:
        dmp["ckv"] = nc.dram_tensor("d_ckv", [128, 4 * SEQ], BF16, kind="ExternalOutput")
        dmp["cq"] = nc.dram_tensor("d_cq", [128, 4 * OWN], BF16, kind="ExternalOutput")
        dmp["kr"] = nc.dram_tensor("d_kr", [64, SEQ], BF16, kind="ExternalOutput")

    sch = Sch(nc)
    E = sch.emit
    stack = []

    def sb(name, shape, dtype):
        cm = nc.sbuf_tensor(name, list(shape), dtype)
        t = cm.__enter__()
        stack.append((name, cm))
        return t

    nbank_res = [Res() for _ in range(8)]

    def pp(name, shape, dtype=F32):
        b0 = nc.psum_base
        cm = nc.psum_tensor(name, list(shape), dtype)
        t = cm.__enter__()
        b1 = nc.psum_base
        stack.append((name, cm))
        return t, nbank_res[b0:b1]

    def free(upto):
        while stack:
            name, cm = stack.pop()
            cm.__exit__(None, None, None)
            if name == upto:
                return
        raise RuntimeError(f"free: {upto} not found")

    R = Res

    vnop_sb = sb("vnop_sb", [128, 2], F32)
    sch.vnop_ap = vnop_sb[:, 0:2]
    ident_sb = sb("ident_sb", [128, 128], BF16)
    id32_sb = sb("id32_sb", [128, 128], F32)
    ones_c = sb("ones_c", [128, 1], BF16)
    ones_r = sb("ones_r", [1, 128], BF16)
    ones_r32 = sb("ones_r32", [1, 128], F32)
    ebias_sb = sb("ebias_sb", [128, 1], F32)
    ps64_sb = sb("ps64_sb", [64, 64], BF16)
    e8_sb = sb("e8_sb", [8, 256], BF16)
    kpos_sb = sb("kpos_sb", [128, 32], F32)
    wg_sb = sb("wg_sb", [128, 16 * 8], BF16)
    b2s_sb = sb("b2s_sb", [128, 2 * D], BF16)
    rconst = R()
    for dst, src in ((ident_sb, ident), (id32_sb, ident32), (ones_c, ones_col),
                     (ones_r, ones_row), (ones_r32, ones_row32), (ebias_sb, ebias),
                     (ps64_sb, ps64), (e8_sb, e8), (kpos_sb, kpos)):
        E("Y", lambda e, d=dst, s=src: e.dma_start(d[:], s[:]), writes=[rconst], dma=True)
    E("P", lambda e: e.dma_start(wg_sb[:].rearrange("p (a b) -> p a b", a=16), wg[:]),
      writes=[rconst], dma=True)
    E("P", lambda e: e.dma_start(b2s_sb[:].rearrange("p (a b) -> p a b", a=2), b2s[:]),
      writes=[rconst], dma=True)

    r_swo, r_sfs, r_sw2, r_sa13 = R(), R(), R(), R()
    cast_jobs = [(s_wo, wo, r_swo), (s_a1s, a1s, r_sa13), (s_a3s, a3s, r_sa13)]
    for i in range(0, NFC, 7):
        cast_jobs.append((s_fs[i:i + 7], fstream[i:i + 7], r_sfs))
    for i in range(4):
        cast_jobs.append((s_w2[:, i], w2[:, i], r_sw2))

    def emit_casts(k=1):
        for _ in range(k):
            if not cast_jobs:
                return
            dst, src, res = cast_jobs.pop(0)
            E("P", lambda e, d=dst, s=src: e.dma_start(d[:], s[:]), writes=[res], dma=True)

    ckv_sb = sb("ckv_sb", [128, 4 * SEQ], BF16)
    cq_sb = sb("cq_sb", [128, 4 * OWN], BF16)
    kr_sb = sb("kr_sb", [64, SEQ], BF16)
    r_ckv, r_cq, r_kr = R(), R(), R()
    wdq_sb = sb("wdq_sb", [128, 16 * 4 * 128], BF16)
    wdkv_sb = sb("wdkv_sb", [128, 16 * 4 * 128], BF16)
    wkr_sb = sb("wkr_sb", [128, 16 * 64], BF16)
    r_wdq, r_wdkv, r_wkr = R(), R(), R()
    E("P", lambda e: e.dma_start(
        wdq_sb[:].rearrange("p (a b c) -> p a b c", a=16, b=4), wdq[:]),
      writes=[r_wdq], dma=True)
    E("P", lambda e: e.dma_start(
        wdkv_sb[:].rearrange("p (a b c) -> p a b c", a=16, b=4), wdkv[:]),
      writes=[r_wdkv], dma=True)
    E("P", lambda e: e.dma_start(
        wkr_sb[:].rearrange("p (a c) -> p a c", a=16),
        wkr[:].rearrange("p a b c -> p a (b c)")),
      writes=[r_wkr], dma=True)
    kcos_sb = sb("kcos_sb", [64, SEQ], F32)
    ksin_sb = sb("ksin_sb", [64, SEQ], F32)
    r_ktab = R()
    E("Y", lambda e: e.dma_start(kcos_sb[:], kcos[:]), writes=[r_ktab], dma=True)
    E("Y", lambda e: e.dma_start(ksin_sb[:], ksin[:]), writes=[r_ktab], dma=True)

    xb_sb = sb("xb_sb", [128, 3 * 16 * 512], BF16)
    r_xb = [R(), R(), R()]
    xsq_sb = sb("xsq_sb", [128, 2048], BF16)
    r_xsq = R()
    stripA = sb("stripA", [1, 4 * 512], F32)
    r_stripA = R()
    mb_sb = sb("mb_sb", [128, 1024], F32)
    r_mb = R()
    kraw_sb = sb("kraw_sb", [64, 512], BF16)
    r_kraw = R()
    kt_sb = sb("kt_sb", [64, 1024], F32)
    r_kt = R()

    pA_sum, bA_sum = pp("pA_sum", [1, 512])
    pA_sq, bA_sq = pp("pA_sq", [1, 512])
    pA_bc, bA_bc = pp("pA_bc", [128, 1024])
    pA_p1, bA_p1 = pp("pA_p1", [128, 1024])
    pA_p2, bA_p2 = pp("pA_p2", [128, 1024])
    bA_sum, bA_sq = bA_sum[0], bA_sq[0]

    def ln_fm_stats(xs, rx):
        def f_sum(e, x=xs):
            for i in range(16):
                ins = e.matmul(pA_sum[:], ones_c[:], x[:, i * 512:(i + 1) * 512],
                               start=(i == 0), stop=(i == 15))
            return ins
        E("T", f_sum, reads=[rx, rconst], writes=[bA_sum])
        for g in range(4):
            E("V", lambda e, x=xs, g=g: e.tensor_tensor(
                xsq_sb[:], x[:, g * 2048:(g + 1) * 2048],
                x[:, g * 2048:(g + 1) * 2048], ALU.mult),
              reads=[rx], writes=[r_xsq])

            def f_sq(e, g=g):
                for i in range(4):
                    ins = e.matmul(pA_sq[:], ones_c[:], xsq_sb[:, i * 512:(i + 1) * 512],
                                   start=(g == 0 and i == 0), stop=(g == 3 and i == 3))
                return ins
            E("T", f_sq, reads=[r_xsq, rconst], writes=[bA_sq] if g == 3 else [])
        mS = stripA[:, 0:512]
        vS = stripA[:, 512:1024]
        tS = stripA[:, 1024:1536]
        E("V", lambda e: e.tensor_scalar(mS, pA_sum[:], 1.0 / D, None, ALU.mult),
          reads=[bA_sum], writes=[r_stripA])
        E("V", lambda e: e.tensor_scalar(tS, pA_sq[:], 1.0 / D, EPS, ALU.mult, ALU.add),
          reads=[bA_sq], writes=[r_stripA])
        E("V", lambda e: e.tensor_tensor(vS, mS, mS, ALU.mult), writes=[r_stripA])
        E("V", lambda e: e.tensor_tensor(vS, tS, vS, ALU.subtract), writes=[r_stripA])
        E("S", lambda e: e.activation(vS, vS, AF.Ln), reads=[r_stripA], writes=[r_stripA])
        E("S", lambda e: e.activation(vS, vS, AF.Exp, scale=-0.5),
          reads=[r_stripA], writes=[r_stripA])
        E("T", lambda e: e.matmul(pA_bc[:, 0:512], ones_r32[:], mS, start=True, stop=True),
          reads=[r_stripA, rconst], writes=[bA_bc[0]])
        E("T", lambda e: e.matmul(pA_bc[:, 512:1024], ones_r32[:], vS, start=True, stop=True),
          reads=[r_stripA], writes=[bA_bc[1]])
        E("V", lambda e: e.tensor_copy(mb_sb[:], pA_bc[:]),
          reads=bA_bc, writes=[r_mb])
        for i in range(16):
            E("V", lambda e, x=xs, i=i: e.tensor_tensor(
                x[:, i * 512:(i + 1) * 512], x[:, i * 512:(i + 1) * 512],
                mb_sb[:, 0:512], ALU.subtract), reads=[r_mb], writes=[rx])
        for i in range(16):
            E("V", lambda e, x=xs, i=i: e.tensor_tensor(
                x[:, i * 512:(i + 1) * 512], x[:, i * 512:(i + 1) * 512],
                mb_sb[:, 512:1024], ALU.mult), reads=[r_mb], writes=[rx])

    prjL = [(pA_p1[:, 0:512], bA_p1[0]), (pA_p1[:, 512:1024], bA_p1[1]),
            (pA_p2[:, 0:512], bA_p2[0]), (pA_p2[:, 512:1024], bA_p2[1])]

    def phaseA_load(idx, src):
        slot = idx % 3
        xs = xb_sb[:, slot * 8192:(slot + 1) * 8192]
        rx = r_xb[slot]
        E("P", lambda e, s=src, x=xs: e.dma_start(
            x.rearrange("p (i t) -> p i t", i=16), s), writes=[rx], dma=True)

    def phaseA_block(idx, tb, own):
        slot = idx % 3
        xs = xb_sb[:, slot * 8192:(slot + 1) * 8192]
        rx = r_xb[slot]
        ln_fm_stats(xs, rx)
        if own:
            wsb, rw, dst, rdst, blk = wdq_sb, r_wdq, cq_sb, r_cq, OWN
        else:
            wsb, rw, dst, rdst, blk = wdkv_sb, r_wdkv, ckv_sb, r_ckv, SEQ
        for j in range(4):
            pv, br = prjL[j]

            def f_prj(e, x=xs, w=wsb, j=j, pv=pv):
                for i in range(16):
                    ins = e.matmul(pv, w[:, (i * 4 + j) * 128:(i * 4 + j) * 128 + 128],
                                   x[:, i * 512:(i + 1) * 512],
                                   start=(i == 0), stop=(i == 15))
                return ins
            E("T", f_prj, reads=[rx, rw], writes=[br])
            E("V", lambda e, pv=pv, d=dst, j=j, tb=tb, blk=blk: e.tensor_copy(
                d[:, j * blk + tb * 512: j * blk + tb * 512 + 512], pv),
              reads=[br], writes=[rdst])
        if not own:
            pv, br = prjL[0]

            def f_kr(e, x=xs, pv=pv):
                pv64 = pv[0:64, :]
                for i in range(16):
                    ins = e.matmul(pv64, wkr_sb[:, i * 64:(i + 1) * 64],
                                   x[:, i * 512:(i + 1) * 512],
                                   start=(i == 0), stop=(i == 15))
                return ins
            E("T", f_kr, reads=[rx, r_wkr], writes=[br])
            E("V", lambda e, pv=pv: e.tensor_copy(kraw_sb[:], pv[0:64, :]),
              reads=[br], writes=[r_kraw])
            pv2, br2 = prjL[1]
            E("T", lambda e, pv2=pv2: e.matmul(pv2[0:64, :], ps64_sb[:], kraw_sb[:],
                                               start=True, stop=True),
              reads=[r_kraw, rconst], writes=[br2])
            t1 = kt_sb[:, 0:512]
            t2 = kt_sb[:, 512:1024]
            E("V", lambda e, tb=tb, t1=t1: e.tensor_tensor(
                t1, kraw_sb[:], kcos_sb[:, tb * 512:(tb + 1) * 512], ALU.mult),
              reads=[r_kraw, r_ktab], writes=[r_kt])
            E("V", lambda e, pv2=pv2, tb=tb, t2=t2: e.tensor_tensor(
                t2, pv2[0:64, :], ksin_sb[:, tb * 512:(tb + 1) * 512], ALU.mult),
              reads=[br2, r_ktab], writes=[r_kt])
            E("V", lambda e, tb=tb, t1=t1, t2=t2: e.tensor_tensor(
                kr_sb[:, tb * 512:(tb + 1) * 512], t1, t2, ALU.add),
              reads=[r_kt], writes=[r_kr])

    A_jobs = [(tb, xT[:, :, tb * 512:(tb + 1) * 512], False) for tb in range(8)]
    A_jobs += [(tb, xTo[:, :, tb * 512:(tb + 1) * 512], True) for tb in range(4)]
    phaseA_load(0, A_jobs[0][1])
    phaseA_load(1, A_jobs[1][1])
    for idx, (tb, src, own) in enumerate(A_jobs):
        if idx + 2 < len(A_jobs):
            phaseA_load(idx + 2, A_jobs[idx + 2][1])
        phaseA_block(idx, tb, own)

    if "ckv" in dump:
        rd = R()
        E("S", lambda e: e.dma_start(dmp["ckv"][:], ckv_sb[:]), reads=[r_ckv],
          writes=[rd], dma=True)
        E("S", lambda e: e.dma_start(dmp["cq"][:], cq_sb[:]), reads=[r_cq],
          writes=[rd], dma=True)
        E("S", lambda e: e.dma_start(dmp["kr"][:], kr_sb[:]), reads=[r_kr],
          writes=[rd], dma=True)

    free("wdq_sb")
    sch.gate()

    qcos_sb = sb("qcos_sb", [64, OWN], F32)
    qsin_sb = sb("qsin_sb", [64, OWN], F32)
    qpos_sb = sb("qpos_sb", [128, OWN], F32)
    r_qtab = R()
    E("Y", lambda e: e.dma_start(qcos_sb[:], qcos[:]), writes=[r_qtab], dma=True)
    E("Y", lambda e: e.dma_start(qsin_sb[:], qsin[:]), writes=[r_qtab], dma=True)
    E("Y", lambda e: e.dma_start(qpos_sb[:], qposb[:]), writes=[r_qtab], dma=True)

    WSL = 4 * 128 * 3 + 4 * 64
    wslf = sb("wslf", [128, WSL], F32)
    wslb = sb("wslb", [128, 2 * WSL], BF16)
    r_wslf = R()
    r_wslb = [R(), R()]
    khi_sb = sb("khi_sb", [128, SEQ], BF16)
    v_sb = sb("v_sb", [128, SEQ], BF16)
    qhi_sb = sb("qhi_sb", [128, OWN], BF16)
    qlo_sb = sb("qlo_sb", [64, OWN], BF16)
    qraw_sb = sb("qraw_sb", [64, OWN], BF16)
    qt_sb = sb("qt_sb", [64, 1024], F32)
    r_khi, r_v, r_qhi, r_qlo, r_qraw, r_qt = R(), R(), R(), R(), R(), R()
    p_sb = sb("p_sb", [128, 3 * 1024], BF16)
    r_p = [R(), R(), R()]
    rl_sb = sb("rl_sb", [1, 512], F32)
    r_rl = R()
    atmp_sb = sb("atmp_sb", [128, 2 * 512], BF16)
    r_atmp = [R(), R()]

    pB_sc, bB_sc6 = pp("pB_sc", [128, 3072])
    pB_acc, bB_acc = pp("pB_acc", [128, 512])
    pB_l, bB_l = pp("pB_l", [1, 512])
    bB_sc = [bB_sc6[0:2], bB_sc6[2:4], bB_sc6[4:6]]
    pB_pr = pB_sc[:, 2048:3072]
    bB_pr = bB_sc6[4:6]
    bB_acc, bB_l = bB_acc[0], bB_l[0]

    def load_wslice(h):
        tok = None
        first = True
        for off, src, w in ((0, wuq, 512), (512, wuk, 512), (1024, wuv, 512),
                            (1536, wqr, 256)):
            tok = E("Y", lambda e, o=off, s=src, w=w, h=h: e.dma_start(
                wslf[:, o:o + w].rearrange("p (i c) -> p i c", i=4), s[:, :, h, :]),
                writes=[r_wslf] if first else [], dma=True)
            first = False
        r_wslf.ready = tok
        r_wslf.users = []
        slot = h % 2
        E("V", lambda e, slot=slot: e.tensor_copy(
            wslb[:, slot * WSL:(slot + 1) * WSL], wslf[:]),
          reads=[r_wslf], writes=[r_wslb[slot]])

    load_wslice(0)
    pending_evac = [None]
    for h in range(NH):
        emit_casts(3)
        slot = h % 2
        base = slot * WSL
        rb = r_wslb[slot]
        if h + 1 < NH:
            load_wslice(h + 1)
        for kb in range(8):
            half = kb % 2
            pv = pB_pr[:, half * 512:(half + 1) * 512]
            br = bB_pr[half]

            def f_khi(e, kb=kb, base=base, pv=pv):
                for i in range(4):
                    ins = e.matmul(pv, wslb[:, base + 512 + i * 128:base + 512 + i * 128 + 128],
                                   ckv_sb[:, i * SEQ + kb * 512:i * SEQ + kb * 512 + 512],
                                   start=(i == 0), stop=(i == 3))
                return ins
            E("T", f_khi, reads=[rb, r_ckv], writes=[br])
            E("V", lambda e, pv=pv, kb=kb: e.tensor_copy(
                khi_sb[:, kb * 512:(kb + 1) * 512], pv), reads=[br], writes=[r_khi])
        for g in range(8):
            half = g % 2
            pv = pB_pr[:, half * 512:(half + 1) * 512]
            br = bB_pr[half]

            def f_v(e, g=g, base=base, pv=pv):
                for c in range(4):
                    kc = g * 4 + c
                    for i in range(4):
                        ins = e.matmul(
                            pv[:, c * 128:(c + 1) * 128],
                            ckv_sb[:, i * SEQ + kc * 128:i * SEQ + kc * 128 + 128],
                            wslb[:, base + 1024 + i * 128:base + 1024 + i * 128 + 128],
                            start=(i == 0), stop=(i == 3))
                return ins
            E("T", f_v, reads=[rb, r_ckv], writes=[br])
            E("V", lambda e, pv=pv, g=g: e.tensor_copy(
                v_sb[:, g * 512:(g + 1) * 512], pv), reads=[br], writes=[r_v])
        for qb in range(4):
            half = qb % 2
            pv = pB_pr[:, half * 512:(half + 1) * 512]
            br = bB_pr[half]

            def f_qhi(e, qb=qb, base=base, pv=pv):
                for i in range(4):
                    ins = e.matmul(pv, wslb[:, base + i * 128:base + i * 128 + 128],
                                   cq_sb[:, i * OWN + qb * 512:i * OWN + qb * 512 + 512],
                                   start=(i == 0), stop=(i == 3))
                return ins
            E("T", f_qhi, reads=[rb, r_cq], writes=[br])
            E("V", lambda e, pv=pv, qb=qb: e.tensor_copy(
                qhi_sb[:, qb * 512:(qb + 1) * 512], pv), reads=[br], writes=[r_qhi])
        for qb in range(4):
            half = qb % 2
            pv = pB_pr[:, half * 512:(half + 1) * 512]
            br = bB_pr[half]

            def f_qr(e, qb=qb, base=base, pv=pv):
                pv64 = pv[0:64, :]
                for i in range(4):
                    ins = e.matmul(pv64, wslb[:, base + 1536 + i * 64:base + 1536 + i * 64 + 64],
                                   cq_sb[:, i * OWN + qb * 512:i * OWN + qb * 512 + 512],
                                   start=(i == 0), stop=(i == 3))
                return ins
            E("T", f_qr, reads=[rb, r_cq], writes=[br])
            E("V", lambda e, pv=pv, qb=qb: e.tensor_copy(
                qraw_sb[:, qb * 512:(qb + 1) * 512], pv[0:64, :]),
              reads=[br], writes=[r_qraw])
            pv2 = pB_pr[:, (1 - half) * 512:(2 - half) * 512]
            br2 = bB_pr[1 - half]
            E("T", lambda e, pv2=pv2, qb=qb: e.matmul(
                pv2[0:64, :], ps64_sb[:], qraw_sb[:, qb * 512:(qb + 1) * 512],
                start=True, stop=True),
              reads=[r_qraw, rconst], writes=[br2])
            t1 = qt_sb[:, 0:512]
            t2 = qt_sb[:, 512:1024]
            E("V", lambda e, qb=qb, t1=t1: e.tensor_tensor(
                t1, qraw_sb[:, qb * 512:(qb + 1) * 512],
                qcos_sb[:, qb * 512:(qb + 1) * 512], ALU.mult),
              reads=[r_qraw, r_qtab], writes=[r_qt])
            E("V", lambda e, pv2=pv2, qb=qb, t2=t2: e.tensor_tensor(
                t2, pv2[0:64, :], qsin_sb[:, qb * 512:(qb + 1) * 512], ALU.mult),
              reads=[br2, r_qtab], writes=[r_qt])
            E("V", lambda e, qb=qb, t1=t1, t2=t2: e.tensor_tensor(
                qlo_sb[:, qb * 512:(qb + 1) * 512], t1, t2, ALU.add),
              reads=[r_qt], writes=[r_qlo])
        for qt in range(NQT):
            np2 = KCS[qt] // 2

            def emit_scores(j2, qt=qt, np2=np2):
                sslot = j2 % 3
                sc = pB_sc[:, sslot * 1024:(sslot + 1) * 1024]
                bsc = bB_sc[sslot]
                kc0 = 2 * j2

                def f_sc(e, kc0=kc0, qt=qt, sc=sc):
                    for half in range(2):
                        kc = kc0 + half
                        e.matmul(sc[:, half * 512:(half + 1) * 512],
                                 khi_sb[:, kc * 128:(kc + 1) * 128],
                                 qhi_sb[:, qt * 512:(qt + 1) * 512],
                                 start=True, stop=False)
                        ins = e.matmul(sc[:, half * 512:(half + 1) * 512],
                                       kr_sb[:, kc * 128:(kc + 1) * 128],
                                       qlo_sb[:, qt * 512:(qt + 1) * 512],
                                       start=False, stop=True)
                    return ins
                E("T", f_sc, reads=[r_khi, r_qhi, r_qlo, r_kr], writes=list(bsc))
                pslice = p_sb[:, sslot * 1024:(sslot + 1) * 1024]
                rp = r_p[sslot]
                E("S", lambda e, sc=sc, pq=pslice: e.activation(
                    pq, sc, AF.Exp, bias=ebias_sb[:, 0:1], scale=SCALE),
                  reads=list(bsc), writes=[rp])
                E("V", lambda e, pq=pslice: e.tensor_scalar(
                    pq, pq, 1.0, PLO, ALU.min, ALU.max), writes=[rp])
                if j2 >= np2 - 4:
                    kc0b = 2 * j2
                    for half in range(2):
                        kc = kc0b + half
                        E("V", lambda e, pq=pslice, half=half, qt=qt, kc=kc:
                          e.scalar_tensor_tensor(
                              pq[:, half * 512:(half + 1) * 512],
                              qpos_sb[:, qt * 512:(qt + 1) * 512],
                              kpos_sb[:, kc:kc + 1],
                              pq[:, half * 512:(half + 1) * 512],
                              ALU.is_ge, ALU.mult),
                          reads=[rconst, r_qtab], writes=[rp], haz=True)

            def emit_pv(j2, qt=qt):
                sslot = j2 % 3
                pslice = p_sb[:, sslot * 1024:(sslot + 1) * 1024]
                rp = r_p[sslot]
                kc0 = 2 * j2

                def f_pv(e, kc0=kc0, j2=j2, pq=pslice):
                    for half in range(2):
                        kc = kc0 + half
                        st = (j2 == 0 and half == 0)
                        e.matmul(pB_l[:], ones_c[:], pq[:, half * 512:(half + 1) * 512],
                                 start=st, stop=False)
                        ins = e.matmul(pB_acc[:], v_sb[:, kc * 128:(kc + 1) * 128],
                                       pq[:, half * 512:(half + 1) * 512],
                                       start=st, stop=False)
                    return ins
                E("T", f_pv, reads=[rp, r_v, rconst], writes=[bB_acc, bB_l])

            emit_scores(0)
            if np2 > 1:
                emit_scores(1)
            if pending_evac[0] is not None:
                pending_evac[0]()
                pending_evac[0] = None
            for j2 in range(np2):
                if j2 + 2 < np2:
                    emit_scores(j2 + 2)
                emit_pv(j2)

            def evac(h=h, qt=qt):
                aslot = qt % 2
                atile = atmp_sb[:, aslot * 512:(aslot + 1) * 512]
                ratile = r_atmp[aslot]
                E("V", lambda e, a=atile: e.tensor_copy(a, pB_acc[:]),
                  reads=[bB_acc], writes=[ratile])
                E("V", lambda e: e.reciprocal(rl_sb[:], pB_l[:]), reads=[bB_l],
                  writes=[r_rl], haz=True)
                E("T", lambda e: e.matmul(pB_acc[:], ones_r32[:], rl_sb[:],
                                          start=True, stop=True),
                  reads=[r_rl, rconst], writes=[bB_acc])
                E("V", lambda e, a=atile: e.tensor_tensor(a, a, pB_acc[:], ALU.mult),
                  reads=[bB_acc], writes=[ratile])
                E("S", lambda e, a=atile, h=h, qt=qt: e.dma_start(
                    attnd[:, h * OWN + qt * 512: h * OWN + qt * 512 + 512], a),
                  reads=[ratile], dma=True)
            pending_evac[0] = evac

    if pending_evac[0] is not None:
        pending_evac[0]()
        pending_evac[0] = None
    emit_casts(100)
    r_attnd = R()
    r_attnd.ready = ("dS", sch.cnt["dS"])
    free("qcos_sb")
    free("ckv_sb")
    sch.gate()

    wo_sb = sb("wo_sb", [128, 2 * NH * 512], BF16)
    r_wo = [R(), R()]
    attc_sb = sb("attc_sb", [128, 2 * NH * 512], BF16)
    r_attc = [R(), R()]
    attno_sb = sb("attno_sb", [128, 4 * D], F32)
    r_attno = [R() for _ in range(4)]
    sums_sb = sb("sums_sb", [128, 16], F32)
    r_sums = [R() for _ in range(4)]
    sqc_sb = sb("sqc_sb", [128, 4], F32)
    r_sqc = [R() for _ in range(4)]
    scrC = sb("scrC", [128, 2048], F32)
    r_scrC = R()
    stripC = sb("stripC", [128, 8], F32)
    r_stripC = R()
    xtm_sb = sb("xtm_sb", [128, 2 * D], F32)
    r_xtm = [R(), R()]
    x1b_sb = sb("x1b_sb", [128, 2 * D], F32)
    r_x1b = [R(), R()]
    pC, bC = pp("pC", [128, 2048])
    r_x1d = R()

    def ln_tm_strips(sumcol, sqcol, icol, ncol, rs, extra_reads):
        E("V", lambda e: e.tensor_scalar(ncol, sumcol, 1.0 / D, None, ALU.mult),
          reads=extra_reads, writes=[rs])
        E("V", lambda e: e.tensor_scalar(icol, sqcol, 1.0 / D, EPS, ALU.mult, ALU.add),
          reads=extra_reads, writes=[rs])
        E("V", lambda e: e.scalar_tensor_tensor(
            icol, ncol, ncol, icol, ALU.mult, ALU.subtract),
          writes=[rs], haz=True)
        E("V", lambda e: e.tensor_scalar(icol, icol, -1.0, None, ALU.mult),
          writes=[rs])
        E("S", lambda e: e.activation(icol, icol, AF.Ln), reads=[rs], writes=[rs])
        E("S", lambda e: e.activation(icol, icol, AF.Exp, scale=-0.5),
          reads=[rs], writes=[rs])
        E("V", lambda e: e.scalar_tensor_tensor(
            ncol, ncol, -1.0, icol, ALU.mult, ALU.mult),
          writes=[rs], haz=True)

    for qs in range(4):
        aslot = qs % 2
        attc = attc_sb[:, aslot * NH * 512:(aslot + 1) * NH * 512]
        rattc = r_attc[aslot]
        E("Y", lambda e, qs=qs, a=attc: e.dma_start(
            a.rearrange("p (h c) -> p h c", h=NH),
            attnd[:].rearrange("p (h t) -> p h t", h=NH)[:, :, qs * 512:(qs + 1) * 512]),
          reads=[r_attnd], writes=[rattc], dma=True)
        for ob in range(4):
            wslot = ob % 2
            wdst = wo_sb[:, wslot * NH * 512:(wslot + 1) * NH * 512]
            rwo = r_wo[wslot]
            E("Y", lambda e, ob=ob, w=wdst: e.dma_start(
                w.rearrange("p (h c) -> p h c", h=NH), s_wo[:, ob]),
              reads=[r_swo], writes=[rwo], dma=True)
            for qc4 in range(4):
                pv = pC[:, qc4 * 512:(qc4 + 1) * 512]
                bb = bC[qc4]

                def f_op(e, a=attc, qc4=qc4, w=wdst, pv=pv):
                    for hh in range(NH):
                        ins = e.matmul(pv, a[:, hh * 512 + qc4 * 128:hh * 512 + qc4 * 128 + 128],
                                       w[:, hh * 512:(hh + 1) * 512],
                                       start=(hh == 0), stop=(hh == NH - 1))
                    return ins
                E("T", f_op, reads=[rwo, rattc], writes=[bb])
                E("S", lambda e, pv=pv, qc4=qc4, ob=ob: e.activation(
                    attno_sb[:, qc4 * D + ob * 512:qc4 * D + ob * 512 + 512], pv,
                    AF.Copy),
                  reads=[bb], writes=[r_attno[qc4]])
        for qc4 in range(4):
            qc = qs * 4 + qc4
            asl = attno_sb[:, qc4 * D:(qc4 + 1) * D]
            xslot = qc4 % 2
            xtile = xtm_sb[:, xslot * D:(xslot + 1) * D]
            rx = r_xtm[xslot]
            E("Y", lambda e, qc=qc, xx=xtile: e.dma_start(xx, xo[qc * 128:(qc + 1) * 128, :]),
              writes=[rx], dma=True)
            E("V", lambda e, a=asl, qc4=qc4: e.scalar_tensor_tensor(
                scrC[:], a, 1.0, a, ALU.mult, ALU.mult,
                accum_out=sqc_sb[:, qc4:qc4 + 1]),
              reads=[r_attno[qc4]], writes=[r_scrC, r_sqc[qc4]], haz=True)
            scol = stripC[:, 0:1]
            icol = stripC[:, 1:2]
            ncol = stripC[:, 2:3]
            E("V", lambda e, a=asl, scol=scol: e.tensor_reduce(scol, a, AX.X, ALU.add),
              reads=[r_attno[qc4]], writes=[r_stripC], haz=True)
            ln_tm_strips(scol, sqc_sb[:, qc4:qc4 + 1], icol, ncol, r_stripC,
                         [r_sqc[qc4]])
            x1t = x1b_sb[:, xslot * D:(xslot + 1) * D]
            rx1 = r_x1b[xslot]
            E("V", lambda e, a=asl, x1t=x1t, icol=icol, ncol=ncol: e.tensor_scalar(
                x1t, a, icol, ncol, ALU.mult, ALU.add),
              reads=[r_attno[qc4], r_stripC], writes=[rx1])
            E("V", lambda e, x1t=x1t, xx=xtile: e.tensor_tensor(x1t, x1t, xx, ALU.add),
              reads=[rx], writes=[rx1])
            E("S", lambda e, x1t=x1t, qc=qc: e.dma_start(
                x1d[qc * 128:(qc + 1) * 128, :], x1t), reads=[rx1], writes=[r_x1d],
              dma=True)

    if "attno" in dump:
        rd = R()
        E("S", lambda e: e.dma_start(dmp["attno"][:], attno_sb[:]),
          reads=r_attno, writes=[rd], dma=True)
        E("S", lambda e: e.dma_start(dmp["strip"][:], stripC[:]),
          reads=[r_stripC], writes=[rd], dma=True)
        E("S", lambda e: e.dma_start(dmp["sums"][:], sums_sb[:]),
          reads=r_sums, writes=[rd], dma=True)
        E("S", lambda e: e.dma_start(dmp["sqc"][:], sqc_sb[:]),
          reads=r_sqc, writes=[rd], dma=True)
    free("wo_sb")
    sch.gate()

    y_sb = sb("y_sb", [128, NFC * 512], BF16)
    r_y = R()
    xnf_sb = sb("xnf_sb", [128, 16 * 512], BF16)
    r_xnf = R()
    xnt_sb = sb("xnt_sb", [128, 2048], BF16)
    r_xnt = R()
    x1t_sb = sb("x1t_sb", [128, 2 * 2048], F32)
    r_x1t = [R(), R()]
    fsr_sb = sb("fsr_sb", [128, 2 * FS_COLS], BF16)
    r_fsr = [R(), R()]
    a2r_sb = sb("a2r_sb", [128, 3 * 256], BF16)
    r_a2r = [R(), R(), R()]
    a13_sb = sb("a13_sb", [128, 16 * 128], BF16)
    r_a13 = R()
    ash_sb = sb("ash_sb", [128, 8 * 512], BF16)
    r_ash = R()
    m256_sb = sb("m256_sb", [128, 4 * 512], BF16)
    r_m256 = R()
    rwb_sb = sb("rwb_sb", [128, 2 * 512], F32)
    r_rwb = R()
    lgE_sb = sb("lgE_sb", [128, 32], F32)
    r_lgE = R()
    g01_sb = sb("g01_sb", [128, 16], BF16)
    r_g01 = R()
    rwp_sb = sb("rwp_sb", [128, 2], F32)
    r_rwp = R()
    maskf_sb = sb("maskf_sb", [8, 1024], BF16)
    r_maskf = R()
    rws_sb = sb("rws_sb", [1, 1024], F32)
    r_rws = R()
    stripD = sb("stripD", [128, 16], F32)
    r_stripD = R()
    scrD = sb("scrD", [128, 2048], F32)
    r_scrD = R()
    z_sb = scrD[:, 0:1024]
    r_z = [R(), R()]
    x2_sb = scrD[:, 1024:1536]
    r_x2 = R()
    y01_sb = sb("y01_sb", [128, 4 * 512], BF16)
    r_y01 = [R(), R(), R(), R()]
    a2e_sb = sb("a2e_sb", [128, 4 * 512], BF16)
    r_a2e = R()
    w2r_sb = sb("w2r_sb", [128, 3 * 512], BF16)
    r_w2r = [R(), R(), R()]
    fout_sb = sb("fout_sb", [128, 4 * 2048], BF16)
    r_fout = [R() for _ in range(4)]
    sumd_sb = sb("sumd_sb", [128, 16], F32)
    r_sumd = [R() for _ in range(4)]
    sqd_sb = sb("sqd_sb", [128, 4], F32)
    r_sqd = [R() for _ in range(4)]
    ob_sb = sb("ob_sb", [128, 2 * 2048], F32)
    r_ob = [R(), R()]

    pD_rot, bD_rot = pp("pD_rot", [128, 1536])
    pD_tr = None
    rotv = [(pD_rot[:, i * 512:(i + 1) * 512], bD_rot[i]) for i in range(3)]
    trp_t, trp_b = pp("pD_tr", [128, 1024], BF16)
    trp_b = trp_b[0]


    rotc = [0]

    def rot_next():
        i = rotc[0] % 3
        rotc[0] += 1
        return rotv[i]

    for tbk in range(4):
        for tc in range(4):
            qc = tbk * 4 + tc
            xslot = tc % 2
            x1t = x1t_sb[:, xslot * 2048:(xslot + 1) * 2048]
            rx1 = r_x1t[xslot]
            E("Y", lambda e, qc=qc, xx=x1t: e.dma_start(
                xx, x1d[qc * 128:(qc + 1) * 128, :]), reads=[r_x1d], writes=[rx1],
              dma=True)
            scol = stripD[:, 0:1]
            sqcol = stripD[:, 1:2]
            icol = stripD[:, 2:3]
            ncol = stripD[:, 3:4]
            E("V", lambda e, xx=x1t, scol=scol: e.tensor_reduce(scol, xx, AX.X, ALU.add),
              reads=[rx1], writes=[r_stripD], haz=True)
            E("V", lambda e, xx=x1t, sqcol=sqcol: e.scalar_tensor_tensor(
                scrD[:], xx, 1.0, xx, ALU.mult, ALU.mult, accum_out=sqcol),
              reads=[rx1], writes=[r_scrD, r_stripD], haz=True)
            ln_tm_strips(scol, sqcol, icol, ncol, r_stripD, [])
            E("V", lambda e, xx=x1t, icol=icol, ncol=ncol: e.tensor_scalar(
                xnt_sb[:], xx, icol, ncol, ALU.mult, ALU.add),
              reads=[rx1, r_stripD], writes=[r_xnt])
            for b8 in range(2):
                def f_tr(e, b8=b8):
                    for i in range(8):
                        ins = e.transpose(trp_t[:, i * 128:(i + 1) * 128],
                                          xnt_sb[:, (b8 * 8 + i) * 128:(b8 * 8 + i) * 128 + 128],
                                          ident_sb[:])
                    return ins
                E("T", f_tr, reads=[r_xnt, rconst], writes=[trp_b])
                E("V", lambda e, b8=b8, tc=tc: e.tensor_copy(
                    xnf_sb[:].rearrange("p (i t) -> p i t", i=16)
                    [:, b8 * 8:(b8 + 1) * 8, tc * 128:(tc + 1) * 128],
                    trp_t[:].rearrange("p (i t) -> p i t", i=8)),
                  reads=[trp_b], writes=[r_xnf])
            pvl, bvl = rotv[0]
            lgp = pvl[:, 0:8]

            def f_lg(e, tc=tc, lgp=lgp):
                for i in range(16):
                    ins = e.matmul(lgp, xnf_sb[:, i * 512 + tc * 128:i * 512 + tc * 128 + 128],
                                   wg_sb[:, i * 8:(i + 1) * 8],
                                   start=(i == 0), stop=(i == 15))
                return ins
            E("T", f_lg, reads=[r_xnf, rconst], writes=[bvl])
            lg = lgE_sb[:, 0:8]
            Ee = lgE_sb[:, 8:16]
            E2 = lgE_sb[:, 16:24]
            c0 = lgE_sb[:, 24:25]
            c1 = lgE_sb[:, 25:26]
            E("V", lambda e, lgp=lgp: e.tensor_copy(lg, lgp), reads=[bvl], writes=[r_lgE], haz=True, tag=f"rt{tc}:cplg")
            E("V", lambda e: e.tensor_reduce(c0, lg, AX.X, ALU.max), writes=[r_lgE], haz=True, tag=f"rt{tc}:max")
            E("V", lambda e: e.tensor_scalar(c1, c0, -1.0, None, ALU.mult), writes=[r_lgE], haz=True, tag=f"rt{tc}:negc")
            E("S", lambda e: e.activation(Ee, lg, AF.Exp, bias=c1), reads=[r_lgE],
              writes=[r_lgE], tag=f"rt{tc}:exp")
            g0 = g01_sb[:, 0:8]
            g1 = g01_sb[:, 8:16]
            E("V", lambda e: e.tensor_scalar(g0, lg, c0, None, ALU.is_equal),
              reads=[r_lgE], writes=[r_g01], haz=True)
            E("V", lambda e: e.tensor_scalar(E2, g0, -1.0, 1.0, ALU.mult, ALU.add),
              reads=[r_g01], writes=[r_lgE], haz=True)
            E("V", lambda e: e.tensor_tensor(E2, E2, Ee, ALU.mult), writes=[r_lgE], haz=True)
            E("V", lambda e: e.tensor_reduce(c0, E2, AX.X, ALU.max), writes=[r_lgE], haz=True)
            E("V", lambda e: e.tensor_scalar(g1, E2, c0, None, ALU.is_equal),
              reads=[r_lgE], writes=[r_g01], haz=True)
            rw0 = rwp_sb[:, 0:1]
            rw1 = rwp_sb[:, 1:2]
            E("V", lambda e: e.tensor_scalar(c1, c0, 1.0, None, ALU.add), writes=[r_lgE], haz=True)
            E("V", lambda e: e.reciprocal(rw0, c1), reads=[r_lgE], writes=[r_rwp], haz=True, tag=f"rt{tc}:recip")
            E("V", lambda e: e.tensor_tensor(rw1, c0, rw0, ALU.mult),
              reads=[r_lgE], writes=[r_rwp], haz=True)
            def f_trm(e, tc=tc):
                e.transpose(trp_t[0:8, 0 * 512 + tc * 128:0 * 512 + tc * 128 + 128],
                            g01_sb[:, 0:8], ident_sb[:])
                return e.transpose(trp_t[0:8, 1 * 512 + tc * 128:1 * 512 + tc * 128 + 128],
                                   g01_sb[:, 8:16], ident_sb[:])
            E("T", f_trm, reads=[r_g01, rconst], writes=[trp_b])
            E("V", lambda e, tc=tc: e.tensor_copy(
                maskf_sb[:, 0 * 512 + tc * 128:0 * 512 + tc * 128 + 128],
                trp_t[0:8, 0 * 512 + tc * 128:0 * 512 + tc * 128 + 128]),
              reads=[trp_b], writes=[r_maskf])
            E("V", lambda e, tc=tc: e.tensor_copy(
                maskf_sb[:, 1 * 512 + tc * 128:1 * 512 + tc * 128 + 128],
                trp_t[0:8, 1 * 512 + tc * 128:1 * 512 + tc * 128 + 128]),
              reads=[trp_b], writes=[r_maskf])
            pvr, bvr = rotv[1]

            def f_trw(e, pvr=pvr):
                e.transpose(pvr[0:1, 0:128], rwp_sb[:, 0:1], id32_sb[:])
                return e.transpose(pvr[0:1, 128:256], rwp_sb[:, 1:2], id32_sb[:])
            E("T", f_trw, reads=[r_rwp, rconst], writes=[bvr])
            E("V", lambda e, tc=tc, pvr=pvr: e.tensor_copy(
                rws_sb[0:1, tc * 128:(tc + 1) * 128], pvr[0:1, 0:128]),
              reads=[bvr], writes=[r_rws])
            E("V", lambda e, tc=tc, pvr=pvr: e.tensor_copy(
                rws_sb[0:1, 512 + tc * 128:512 + (tc + 1) * 128], pvr[0:1, 128:256]),
              reads=[bvr], writes=[r_rws])
        for k in range(2):
            for j in range(2):
                pv, bv = rot_next()

                def f_ex(e, k=k, j=j, pv=pv):
                    return e.matmul(pv, e8_sb[:, j * 128:(j + 1) * 128],
                                    maskf_sb[0:8, k * 512:(k + 1) * 512],
                                    start=True, stop=True)
                E("T", f_ex, reads=[r_maskf, rconst], writes=[bv])
                E("V", lambda e, k=k, j=j, pv=pv: e.tensor_copy(
                    m256_sb[:, (k * 2 + j) * 512:(k * 2 + j) * 512 + 512], pv),
                  reads=[bv], writes=[r_m256])
            pv, bv = rot_next()
            E("T", lambda e, k=k, pv=pv: e.matmul(
                pv, ones_r32[:], rws_sb[0:1, k * 512:(k + 1) * 512], start=True, stop=True),
              reads=[r_rws, rconst], writes=[bv])
            E("V", lambda e, k=k, pv=pv: e.tensor_copy(
                rwb_sb[:, k * 512:(k + 1) * 512], pv), reads=[bv], writes=[r_rwb])
        for (m, asrc) in ((0, s_a1s), (1, s_a3s)):
            for j in range(2):
                E("Y", lambda e, asrc=asrc, j=j: e.dma_start(
                    a13_sb[:].rearrange("p (a c) -> p a c", a=16), asrc[:, :, j, :]),
                  reads=[r_sa13], writes=[r_a13], dma=True)
                pv, bv = rot_next()

                def f_a(e, j=j, pv=pv):
                    for i in range(16):
                        ins = e.matmul(pv, a13_sb[:, i * 128:(i + 1) * 128],
                                       xnf_sb[:, i * 512:(i + 1) * 512],
                                       start=(i == 0), stop=(i == 15))
                    return ins
                E("T", f_a, reads=[r_a13, r_xnf], writes=[bv])
                for k in range(2):
                    E("V", lambda e, m=m, k=k, j=j, pv=pv: e.tensor_tensor(
                        ash_sb[:, ((m * 2 + k) * 2 + j) * 512:((m * 2 + k) * 2 + j) * 512 + 512],
                        pv, m256_sb[:, (k * 2 + j) * 512:(k * 2 + j) * 512 + 512],
                        ALU.mult),
                      reads=[bv, r_m256], writes=[r_ash])
                E("V", lambda e, m=m, j=j: e.tensor_tensor(
                    ash_sb[:, ((m * 2 + 1) * 2 + j) * 512:((m * 2 + 1) * 2 + j) * 512 + 512],
                    ash_sb[:, ((m * 2 + 1) * 2 + j) * 512:((m * 2 + 1) * 2 + j) * 512 + 512],
                    ash_sb[:, ((m * 2 + 0) * 2 + j) * 512:((m * 2 + 0) * 2 + j) * 512 + 512],
                    ALU.subtract), writes=[r_ash])
        pD_a2a, bD_a2a = pp(f"pD_a2a{tbk}", [128, 1024])
        pD_a2b, bD_a2b = pp(f"pD_a2b{tbk}", [128, 1024])
        E("Y", lambda e: e.dma_start(
            fsr_sb[:, 0:FS_COLS], s_fs[0]), reads=[r_sfs], writes=[r_fsr[0]], dma=True)
        E("Y", lambda e: e.dma_start(
            a2r_sb[:, 0:256], s_fs[0][:, FS_A2:FS_COLS]),
          reads=[r_sfs], writes=[r_a2r[0]], dma=True)
        for fc in range(NFC):
            fslot = fc % 2
            fsl = fsr_sb[:, fslot * FS_COLS:(fslot + 1) * FS_COLS]
            rfs = r_fsr[fslot]
            if fc + 1 < NFC:
                nslot = (fc + 1) % 2
                E("Y", lambda e, fc=fc, ns=nslot: e.dma_start(
                    fsr_sb[:, ns * FS_COLS:(ns + 1) * FS_COLS], s_fs[fc + 1]),
                  reads=[r_sfs], writes=[r_fsr[nslot]], dma=True)
                E("Y", lambda e, fc=fc, ns=(fc + 1) % 3: e.dma_start(
                    a2r_sb[:, ns * 256:(ns + 1) * 256], s_fs[fc + 1][:, FS_A2:FS_COLS]),
                  reads=[r_sfs], writes=[r_a2r[(fc + 1) % 3]], dma=True)
            pv1, bv1 = rot_next()

            def f_x1(e, fsl=fsl, pv1=pv1):
                for i in range(16):
                    e.matmul(pv1, fsl[:, FS_W1 + i * 128:FS_W1 + i * 128 + 128],
                             xnf_sb[:, i * 512:(i + 1) * 512],
                             start=(i == 0), stop=False)
                for j in range(2):
                    ins = e.matmul(pv1, fsl[:, FS_B1 + j * 128:FS_B1 + j * 128 + 128],
                                   ash_sb[:, (0 * 2 + j) * 512:(0 * 2 + j) * 512 + 512],
                                   start=False, stop=False)
                return ins
            E("T", f_x1, reads=[rfs, r_xnf, r_ash], writes=[bv1])
            pv3, bv3 = rot_next()

            def f_x3(e, fsl=fsl, pv3=pv3):
                for i in range(16):
                    e.matmul(pv3, fsl[:, FS_W3 + i * 128:FS_W3 + i * 128 + 128],
                             xnf_sb[:, i * 512:(i + 1) * 512],
                             start=(i == 0), stop=False)
                for j in range(2):
                    ins = e.matmul(pv3, fsl[:, FS_B3 + j * 128:FS_B3 + j * 128 + 128],
                                   ash_sb[:, ((2 + 0) * 2 + j) * 512:((2 + 0) * 2 + j) * 512 + 512],
                                   start=False, stop=False)
                return ins
            E("T", f_x3, reads=[rfs, r_xnf, r_ash], writes=[bv3])
            for k in range(2):
                if k == 1:
                    def f_x1d(e, fsl=fsl, pv1=pv1):
                        for j in range(2):
                            ins = e.matmul(pv1, fsl[:, FS_B1 + j * 128:FS_B1 + j * 128 + 128],
                                           ash_sb[:, (1 * 2 + j) * 512:(1 * 2 + j) * 512 + 512],
                                           start=False, stop=(j == 1))
                        return ins
                    E("T", f_x1d, reads=[rfs, r_ash], writes=[bv1])

                    def f_x3d(e, fsl=fsl, pv3=pv3):
                        for j in range(2):
                            ins = e.matmul(pv3, fsl[:, FS_B3 + j * 128:FS_B3 + j * 128 + 128],
                                           ash_sb[:, ((2 + 1) * 2 + j) * 512:((2 + 1) * 2 + j) * 512 + 512],
                                           start=False, stop=(j == 1))
                        return ins
                    E("T", f_x3d, reads=[rfs, r_ash], writes=[bv3])
                fpar = fc % 2
                zsl = z_sb[:, k * 512:(k + 1) * 512]
                rz = r_z[k]
                E("S", lambda e, pv1=pv1, z=zsl: e.activation(z, pv1, AF.Silu),
                  reads=[bv1], writes=[rz])
                E("V", lambda e, z=zsl, pv3=pv3: e.tensor_tensor(
                    x2_sb[:], z, pv3, ALU.mult), reads=[rz, bv3], writes=[r_x2])
                ysl = y01_sb[:, (fpar * 2 + k) * 512:(fpar * 2 + k) * 512 + 512]
                ry = r_y01[fpar * 2 + k]
                E("V", lambda e, k=k, y=ysl: e.tensor_tensor(
                    y, x2_sb[:], rwb_sb[:, k * 512:(k + 1) * 512], ALU.mult),
                  reads=[r_x2, r_rwb], writes=[ry])
                if fc > 0:
                    ppar = (fc - 1) % 2
                    ypsl = y01_sb[:, (ppar * 2 + k) * 512:(ppar * 2 + k) * 512 + 512]
                    rpy = r_y01[ppar * 2 + k]
                    a2b = (bD_a2a if k == 0 else bD_a2b)
                    a2t = (pD_a2a if k == 0 else pD_a2b)
                    pslot = (fc - 1) % 3
                    pa2 = a2r_sb[:, pslot * 256:(pslot + 1) * 256]
                    prfs = r_a2r[pslot]

                    def f_a2(e, pa2=pa2, y=ypsl, a2t=a2t, fc=fc):
                        for j in range(2):
                            ins = e.matmul(a2t[:, j * 512:(j + 1) * 512],
                                           pa2[:, j * 128:(j + 1) * 128],
                                           y, start=(fc - 1 == 0), stop=False)
                        return ins
                    E("T", f_a2, reads=[prfs, rpy], writes=list(a2b))
            E("V", lambda e, fc=fc, fpar=fpar: e.tensor_tensor(
                y_sb[:, fc * 512:(fc + 1) * 512],
                y01_sb[:, fpar * 1024:fpar * 1024 + 512],
                y01_sb[:, fpar * 1024 + 512:fpar * 1024 + 1024], ALU.add),
              reads=[r_y01[fpar * 2], r_y01[fpar * 2 + 1]], writes=[r_y])

        for k in range(2):
            lpar = (NFC - 1) % 2
            lslot = (NFC - 1) % 3
            ypsl = y01_sb[:, (lpar * 2 + k) * 512:(lpar * 2 + k) * 512 + 512]
            rpy = r_y01[lpar * 2 + k]
            a2b = (bD_a2a if k == 0 else bD_a2b)
            a2t = (pD_a2a if k == 0 else pD_a2b)
            la2 = a2r_sb[:, lslot * 256:(lslot + 1) * 256]

            def f_a2f(e, la2=la2, y=ypsl, a2t=a2t):
                for j in range(2):
                    ins = e.matmul(a2t[:, j * 512:(j + 1) * 512],
                                   la2[:, j * 128:(j + 1) * 128],
                                   y, start=False, stop=True)
                return ins
            E("T", f_a2f, reads=[r_a2r[lslot], rpy], writes=list(a2b))
        for k in range(2):
            a2t = (pD_a2a if k == 0 else pD_a2b)
            a2b = (bD_a2a if k == 0 else bD_a2b)
            for j in range(2):
                E("V", lambda e, k=k, j=j, a2t=a2t: e.tensor_tensor(
                    a2e_sb[:, (k * 2 + j) * 512:(k * 2 + j) * 512 + 512],
                    a2t[:, j * 512:(j + 1) * 512],
                    m256_sb[:, (k * 2 + j) * 512:(k * 2 + j) * 512 + 512],
                    ALU.mult),
                  reads=[a2b[j], r_m256], writes=[r_a2e])
        if tbk == 0 and "ffn" in dump:
            rd0 = R()
            E("S", lambda e: e.dma_start(dmp["lgE"][:], lgE_sb[:]), reads=[r_lgE],
              writes=[rd0], dma=True)
            E("S", lambda e: e.dma_start(dmp["g01"][:], g01_sb[:]), reads=[r_g01],
              writes=[rd0], dma=True)
            E("S", lambda e: e.dma_start(dmp["rwp"][:], rwp_sb[:]), reads=[r_rwp],
              writes=[rd0], dma=True)
            E("S", lambda e: e.dma_start(dmp["rws"][:], rws_sb[:]), reads=[r_rws],
              writes=[rd0], dma=True)
            E("S", lambda e: e.dma_start(dmp["xnf"][:], xnf_sb[:]), reads=[r_xnf],
              writes=[rd0], dma=True)
            E("S", lambda e: e.dma_start(dmp["maskf"][:], maskf_sb[:]), reads=[r_maskf],
              writes=[rd0], dma=True)
            E("S", lambda e: e.dma_start(dmp["rwb"][:], rwb_sb[:]), reads=[r_rwb],
              writes=[rd0], dma=True)
            E("S", lambda e: e.dma_start(dmp["ash"][:], ash_sb[:]), reads=[r_ash],
              writes=[rd0], dma=True)
            E("S", lambda e: e.dma_start(dmp["y"][:], y_sb[:]), reads=[r_y],
              writes=[rd0], dma=True)
            E("S", lambda e: e.dma_start(dmp["a2e"][:], a2e_sb[:]), reads=[r_a2e],
              writes=[rd0], dma=True)
        free(f"pD_a2a{tbk}")
        pD_dn, bD_dn = pp(f"pD_dn{tbk}", [128, 2048])
        for ob in range(4):
            for fc in range(NFC):
                wslot = fc % 3
                w2t = w2r_sb[:, wslot * 512:(wslot + 1) * 512]
                rw2 = r_w2r[wslot]
                E("Y", lambda e, ob=ob, fc=fc, w=w2t: e.dma_start(
                    w, s_w2[:, ob, fc, :]),
                  reads=[r_sw2], writes=[rw2], dma=True)

                def f_dn(e, fc=fc, w=w2t, pd=pD_dn):
                    for tc in range(4):
                        ins = e.matmul(
                            pd[:, tc * 512:(tc + 1) * 512],
                            y_sb[:, fc * 512 + tc * 128:fc * 512 + tc * 128 + 128],
                            w, start=(fc == 0), stop=False)
                    return ins
                E("T", f_dn, reads=[rw2, r_y], writes=list(bD_dn))

            def f_l2(e, ob=ob, pd=pD_dn):
                for tc in range(4):
                    for k in range(2):
                        for j in range(2):
                            ins = e.matmul(
                                pd[:, tc * 512:(tc + 1) * 512],
                                a2e_sb[:, (k * 2 + j) * 512 + tc * 128:(k * 2 + j) * 512 + tc * 128 + 128],
                                b2s_sb[:, j * D + ob * 512:j * D + ob * 512 + 512],
                                start=False, stop=(k == 1 and j == 1))
                return ins
            E("T", f_l2, reads=[r_a2e, rconst], writes=list(bD_dn))
            for tc in range(4):
                E("S", lambda e, tc=tc, ob=ob, pd=pD_dn: e.activation(
                    fout_sb[:, tc * 2048 + ob * 512:tc * 2048 + ob * 512 + 512],
                    pd[:, tc * 512:(tc + 1) * 512], AF.Copy),
                  reads=[bD_dn[tc]], writes=[r_fout[tc]])
        for tc in range(4):
            qc = tbk * 4 + tc
            fsl = fout_sb[:, tc * 2048:(tc + 1) * 2048]
            E("V", lambda e, f=fsl, tc=tc: e.scalar_tensor_tensor(
                scrD[:], f, 1.0, f, ALU.mult, ALU.mult,
                accum_out=sqd_sb[:, tc:tc + 1]),
              reads=[r_fout[tc]], writes=[r_scrD, r_sqd[tc]], haz=True)
            scol = stripD[:, 4:5]
            icol = stripD[:, 5:6]
            ncol = stripD[:, 6:7]
            E("V", lambda e, f=fsl, scol=scol: e.tensor_reduce(scol, f, AX.X, ALU.add),
              reads=[r_fout[tc]], writes=[r_stripD], haz=True)
            ln_tm_strips(scol, sqd_sb[:, tc:tc + 1], icol, ncol, r_stripD,
                         [r_sqd[tc]])
            oslot = tc % 2
            obuf = ob_sb[:, oslot * 2048:(oslot + 1) * 2048]
            ro = r_ob[oslot]
            xslot = tc % 2
            x1t = x1t_sb[:, xslot * 2048:(xslot + 1) * 2048]
            rx1 = r_x1t[xslot]
            E("Y", lambda e, qc=qc, xx=x1t: e.dma_start(
                xx, x1d[qc * 128:(qc + 1) * 128, :]), reads=[r_x1d], writes=[rx1],
              dma=True)
            E("V", lambda e, f=fsl, o=obuf, icol=icol, ncol=ncol: e.tensor_scalar(
                o, f, icol, ncol, ALU.mult, ALU.add),
              reads=[r_fout[tc], r_stripD], writes=[ro])
            E("V", lambda e, o=obuf, xx=x1t: e.tensor_tensor(o, o, xx, ALU.add),
              reads=[rx1], writes=[ro])
            E("S", lambda e, o=obuf, qc=qc: e.dma_start(
                out[qc * 128:(qc + 1) * 128, :], o), reads=[ro], dma=True)
        if tbk == 0 and "ffn" in dump:
            rd1 = R()
            E("S", lambda e: e.dma_start(dmp["fout"][:], fout_sb[:]),
              reads=r_fout, writes=[rd1], dma=True)
        free(f"pD_dn{tbk}")

    sch.tail()
    with nc.Block() as block:
        sch.finalize(block)
    free_all = list(stack)
    while stack:
        stack.pop()[1].__exit__(None, None, None)
    sch.close()
    return nc


def kernel(**inputs):
    maps = _host_inputs(inputs)
    nc = build_nc()
    res = run_bass_kernel_spmd(nc, maps, core_ids=list(range(8)))
    x = np.asarray(inputs["L0_attn_x"])
    outp = np.empty((4, SEQ, D), dtype=np.float32)
    for c in range(8):
        rows = _own_rows(c % 2)
        outp[c // 2][rows] = res.results[c]["out"]
    return outp


# version 07
# speedup vs baseline: 1.2835x; 1.0761x over previous
import math
import numpy as np
import ml_dtypes
import concourse.bass as bass
import concourse.mybir as mybir
from concourse.bass_utils import run_bass_kernel_spmd

F32 = mybir.dt.float32
BF16 = mybir.dt.bfloat16
AF = mybir.ActivationFunctionType
ALU = mybir.AluOpType
AX = mybir.AxisListType

D = 2048
DL = 512
NH = 16
SEQ = 4096
FF = 7168
ER = 256
OWN = 2048
QT = 512
NQT = 4
KCS = (8, 16, 24, 32)
NFC = FF // 128
SCALE = 1.0 / math.sqrt(192.0)
EBIAS = -80.0 * SCALE
PLO = math.exp(2.0 * EBIAS)
THETA = 10000.0
EPS = 1e-5

POS_EVEN = (0, 3, 4, 7)
POS_ODD = (1, 2, 5, 6)

FS_W1 = 0
FS_W3 = 2048
FS_B1 = 4096
FS_B3 = 4096 + 256
FS_A2 = 4096 + 512
FS_COLS = 4096 + 768


def _own_rows(parity):
    pos = POS_EVEN if parity == 0 else POS_ODD
    r = []
    for p in pos:
        r.extend(range(p * QT, (p + 1) * QT))
    return np.array(r, dtype=np.int64)


def _lhsT4(wt, mj=128):
    din, dout = wt.shape
    ni, nj = din // 128, dout // mj
    return np.ascontiguousarray(wt.reshape(ni, 128, nj, mj).transpose(1, 0, 2, 3))


def _rope_perm():
    return np.concatenate([np.arange(0, 64, 2), np.arange(1, 64, 2)])


def _rope_tables(positions):
    inv = THETA ** (-np.arange(0, 64, 2, dtype=np.float64) / 64.0)
    ang = positions.astype(np.float64)[None, :] * inv[:, None]
    cos = np.cos(ang)
    sin = np.sin(ang)
    cos2 = np.concatenate([cos, cos], 0).astype(np.float32)
    sin2 = np.concatenate([-sin, sin], 0).astype(np.float32)
    return np.ascontiguousarray(cos2), np.ascontiguousarray(sin2)


def _host_inputs(inp):
    x = np.asarray(inp["L0_attn_x"], dtype=np.float32)
    perm = _rope_perm()
    sh = {}
    sh["wdq"] = _lhsT4(np.ascontiguousarray(np.asarray(inp["L0_attn_W_DQ"]).T))
    sh["wdkv"] = _lhsT4(np.ascontiguousarray(np.asarray(inp["L0_attn_W_DKV"]).T))
    sh["wkr"] = _lhsT4(np.ascontiguousarray(np.asarray(inp["L0_attn_W_KR"]).T[:, perm]), mj=64)
    sh["wuq"] = _lhsT4(np.ascontiguousarray(np.asarray(inp["L0_attn_W_UQ"]).T))
    wqrt = np.asarray(inp["L0_attn_W_QR"]).T.reshape(DL, NH, 64)[:, :, perm]
    sh["wqr"] = _lhsT4(np.ascontiguousarray(wqrt.reshape(DL, NH * 64)), mj=64)
    sh["wuk"] = _lhsT4(np.ascontiguousarray(np.asarray(inp["L0_attn_W_UK"]).T))
    sh["wuv"] = _lhsT4(np.ascontiguousarray(np.asarray(inp["L0_attn_W_UV"]).T))
    wot = np.asarray(inp["L0_attn_W_O"]).T
    sh["wo"] = np.ascontiguousarray(wot.reshape(NH, 128, 4, 512).transpose(1, 2, 0, 3))
    sh["wg"] = np.ascontiguousarray(
        np.asarray(inp["L0_ffn_Wg"]).T.reshape(16, 128, 8).transpose(1, 0, 2))

    w1t = np.asarray(inp["L0_ffn_W1"]).T
    w3t = np.asarray(inp["L0_ffn_W3"]).T
    b1st = np.asarray(inp["L0_ffn_B1"]).transpose(0, 2, 1).reshape(ER, FF)
    b3st = np.asarray(inp["L0_ffn_B3"]).transpose(0, 2, 1).reshape(ER, FF)
    a2st = np.ascontiguousarray(np.asarray(inp["L0_ffn_A2"]).reshape(ER, FF).T)
    fs = np.empty((NFC, 128, FS_COLS), dtype=np.float32)
    fs[:, :, FS_W1:FS_W1 + 2048] = w1t.reshape(16, 128, NFC, 128).transpose(2, 1, 0, 3).reshape(NFC, 128, 2048)
    fs[:, :, FS_W3:FS_W3 + 2048] = w3t.reshape(16, 128, NFC, 128).transpose(2, 1, 0, 3).reshape(NFC, 128, 2048)
    fs[:, :, FS_B1:FS_B1 + 256] = b1st.reshape(2, 128, NFC, 128).transpose(2, 1, 0, 3).reshape(NFC, 128, 256)
    fs[:, :, FS_B3:FS_B3 + 256] = b3st.reshape(2, 128, NFC, 128).transpose(2, 1, 0, 3).reshape(NFC, 128, 256)
    fs[:, :, FS_A2:FS_A2 + 256] = a2st.reshape(NFC, 128, 256)
    sh["fstream"] = fs
    sh["a1s"] = _lhsT4(np.ascontiguousarray(np.asarray(inp["L0_ffn_A1"]).reshape(ER, D).T))
    sh["a3s"] = _lhsT4(np.ascontiguousarray(np.asarray(inp["L0_ffn_A3"]).reshape(ER, D).T))
    b2st = np.asarray(inp["L0_ffn_B2"]).transpose(0, 2, 1).reshape(ER, D)
    sh["b2s"] = np.ascontiguousarray(b2st.reshape(2, 128, D).transpose(1, 0, 2))
    w2t = np.asarray(inp["L0_ffn_W2"]).T
    sh["w2"] = np.ascontiguousarray(w2t.reshape(NFC, 128, 4, 512).transpose(1, 2, 0, 3))

    sh["ident"] = np.eye(128, dtype=ml_dtypes.bfloat16)
    sh["ident32"] = np.eye(128, dtype=np.float32)
    sh["ones_col"] = np.ones((128, 1), dtype=ml_dtypes.bfloat16)
    sh["ones_row"] = np.ones((1, 128), dtype=ml_dtypes.bfloat16)
    sh["ones_row32"] = np.ones((1, 128), dtype=np.float32)
    sh["ebias"] = np.full((128, 1), EBIAS, dtype=np.float32)
    psm = np.zeros((64, 64), dtype=ml_dtypes.bfloat16)
    for i in range(32):
        psm[i, 32 + i] = 1
        psm[32 + i, i] = 1
    sh["ps64"] = psm
    e8 = np.zeros((8, 256), dtype=ml_dtypes.bfloat16)
    for e in range(8):
        e8[e, e * 32:(e + 1) * 32] = 1
    sh["e8"] = e8
    sh["kpos"] = np.ascontiguousarray(
        (np.arange(128)[:, None] + 128 * np.arange(32)[None, :]).astype(np.float32))
    kc2, ks2 = _rope_tables(np.arange(SEQ))
    sh["kcos"] = kc2
    sh["ksin"] = ks2

    maps = []
    for c in range(8):
        b, par = c // 2, c % 2
        rows = _own_rows(par)
        m = dict(sh)
        xb = x[b]
        xt = np.ascontiguousarray(xb.T)
        m["xT"] = np.ascontiguousarray(xt.reshape(16, 128, SEQ).transpose(1, 0, 2))
        xop = np.ascontiguousarray(xb[rows])
        m["xo"] = xop
        m["xTo"] = np.ascontiguousarray(xop.T.reshape(16, 128, OWN).transpose(1, 0, 2))
        qc2, qs2 = _rope_tables(rows)
        m["qcos"] = qc2
        m["qsin"] = qs2
        m["qposb"] = np.ascontiguousarray(
            np.broadcast_to(rows.astype(np.float32)[None, :], (128, OWN)).copy())
        maps.append(m)
    return maps


class Res:
    __slots__ = ("ready", "users")

    def __init__(self):
        self.ready = None
        self.users = []


class Sch:
    ENGS = ("T", "V", "S", "P", "Y")

    def __init__(self, nc):
        self.nc = nc
        self.q = {e: [] for e in self.ENGS}
        self.sem = {}
        self.cnt = {}
        self.hiw = {e: {} for e in self.ENGS}
        self._ctx = []
        for name in ("sT", "sV", "sS", "dY", "dS", "dP"):
            cm = nc.semaphore(name)
            self.sem[name] = cm.__enter__()
            self._ctx.append(cm)
            self.cnt[name] = 0
        self.esem = {"T": "sT", "V": "sV", "S": "sS"}
        self.qsem = {"P": "dP", "Y": "dY", "S": "dS"}
        self.trace = []

    def _wait(self, eng, toks):
        best = {}
        own = self.esem.get(eng)
        for t in toks:
            if t is None:
                continue
            s, v = t
            if s == own:
                continue
            if best.get(s, -1) < v:
                best[s] = v
        for s, v in best.items():
            if self.hiw[eng].get(s, 0) >= v:
                continue
            self.hiw[eng][s] = v
            h = self.sem[s]
            self.trace.append((eng, "wait", s, v))
            self.q[eng].append(lambda e, h=h, v=v: e.wait_ge(h, v))

    def emit(self, eng, fn, reads=(), writes=(), dma=False, ndma=1, haz=False, tag=""):
        toks = []
        for r in reads:
            if r is not None and r.ready is not None:
                toks.append(r.ready)
        for w in writes:
            toks.extend(w.users)
            if w.ready is not None:
                toks.append(w.ready)
        self._wait(eng, toks)
        sname = self.qsem[eng] if dma else self.esem[eng]
        amt = 16 * ndma if dma else 1
        self.cnt[sname] += amt
        tok = (sname, self.cnt[sname])
        h = self.sem[sname]
        inc = 16 if dma else 1
        if haz:
            vna = self.vnop_ap
            def fn2(e, fn=fn, vna=vna):
                fn(e)
                return e.tensor_scalar(vna, vna, 1.0, None, ALU.mult)
            fn = fn2
        self.trace.append((eng, "op:" + tag, sname, tok[1]))
        self.q[eng].append(lambda e, fn=fn, h=h, a=inc: fn(e).then_inc(h, a))
        for r in reads:
            if r is not None:
                r.users.append(tok)
        for w in writes:
            w.ready = tok
            w.users = []
        return tok

    def gate(self):
        toks = [(s, self.cnt[s]) for s in ("sT", "sV", "sS", "dS")]
        for e in self.ENGS:
            self._wait(e, toks)

    def tail(self):
        toks = [(s, self.cnt[s]) for s in self.cnt]
        self._wait("Y", toks)

    def finalize(self, block):
        for attr, eng in (("tensor", "T"), ("vector", "V"), ("scalar", "S"),
                          ("gpsimd", "P"), ("sync", "Y")):
            ops = self.q[eng]

            def body(e, ops=ops):
                for op in ops:
                    op(e)
            getattr(block, attr)(body)

    def close(self):
        for cm in reversed(self._ctx):
            cm.__exit__(None, None, None)


def build_nc(dump=()):
    nc = bass.Bass("TRN2", target_bir_lowering=False)

    def din(name, shape, dtype=F32):
        return nc.dram_tensor(name, list(shape), dtype, kind="ExternalInput")

    xT = din("xT", [128, 16, SEQ])
    xTo = din("xTo", [128, 16, OWN])
    xo = din("xo", [OWN, D])
    wdq = din("wdq", [128, 16, 4, 128])
    wdkv = din("wdkv", [128, 16, 4, 128])
    wkr = din("wkr", [128, 16, 1, 64])
    wuq = din("wuq", [128, 4, 16, 128])
    wqr = din("wqr", [128, 4, 16, 64])
    wuk = din("wuk", [128, 4, 16, 128])
    wuv = din("wuv", [128, 4, 16, 128])
    wo = din("wo", [128, 4, NH, 512])
    wg = din("wg", [128, 16, 8])
    fstream = din("fstream", [NFC, 128, FS_COLS])
    a1s = din("a1s", [128, 16, 2, 128])
    a3s = din("a3s", [128, 16, 2, 128])
    b2s = din("b2s", [128, 2, D])
    w2 = din("w2", [128, 4, NFC, 512])
    ident = din("ident", [128, 128], BF16)
    ident32 = din("ident32", [128, 128], F32)
    ones_col = din("ones_col", [128, 1], BF16)
    ones_row = din("ones_row", [1, 128], BF16)
    ones_row32 = din("ones_row32", [1, 128], F32)
    ebias = din("ebias", [128, 1], F32)
    ps64 = din("ps64", [64, 64], BF16)
    e8 = din("e8", [8, 256], BF16)
    kpos = din("kpos", [128, 32])
    kcos = din("kcos", [64, SEQ])
    ksin = din("ksin", [64, SEQ])
    qcos = din("qcos", [64, OWN])
    qsin = din("qsin", [64, OWN])
    qposb = din("qposb", [128, OWN])

    out = nc.dram_tensor("out", [OWN, D], F32, kind="ExternalOutput")
    x1d = nc.dram_tensor("x1d", [OWN, D], F32,
                         kind="ExternalOutput" if "x1" in dump else "Internal")
    attnd = nc.dram_tensor("attnd", [128, NH * OWN], BF16,
                           kind="ExternalOutput" if "attn" in dump else "Internal")
    s_wo = nc.dram_tensor("s_wo", [128, 4, NH, 512], BF16)
    s_fs = nc.dram_tensor("s_fs", [NFC, 128, FS_COLS], BF16)
    s_w2 = nc.dram_tensor("s_w2", [128, 4, NFC, 512], BF16)
    s_a1s = nc.dram_tensor("s_a1s", [128, 16, 2, 128], BF16)
    s_a3s = nc.dram_tensor("s_a3s", [128, 16, 2, 128], BF16)

    dmp = {}
    if "ffn" in dump:
        dmp["xnf"] = nc.dram_tensor("d_xnf", [128, 16 * 512], BF16, kind="ExternalOutput")
        dmp["maskf"] = nc.dram_tensor("d_maskf", [8, 1024], BF16, kind="ExternalOutput")
        dmp["rwb"] = nc.dram_tensor("d_rwb", [128, 1024], F32, kind="ExternalOutput")
        dmp["ash"] = nc.dram_tensor("d_ash", [128, 8 * 512], BF16, kind="ExternalOutput")
        dmp["y"] = nc.dram_tensor("d_y", [128, NFC * 512], BF16, kind="ExternalOutput")
        dmp["a2e"] = nc.dram_tensor("d_a2e", [128, 4 * 512], BF16, kind="ExternalOutput")
        dmp["fout"] = nc.dram_tensor("d_fout", [128, 4 * 2048], BF16, kind="ExternalOutput")
        dmp["lgE"] = nc.dram_tensor("d_lgE", [128, 32], F32, kind="ExternalOutput")
        dmp["g01"] = nc.dram_tensor("d_g01", [128, 16], BF16, kind="ExternalOutput")
        dmp["rwp"] = nc.dram_tensor("d_rwp", [128, 2], F32, kind="ExternalOutput")
        dmp["rws"] = nc.dram_tensor("d_rws", [1, 1024], F32, kind="ExternalOutput")
    if "attno" in dump:
        dmp["attno"] = nc.dram_tensor("d_attno", [128, 4 * D], F32, kind="ExternalOutput")
        dmp["strip"] = nc.dram_tensor("d_strip", [128, 8], F32, kind="ExternalOutput")
        dmp["sums"] = nc.dram_tensor("d_sums", [128, 16], F32, kind="ExternalOutput")
        dmp["sqc"] = nc.dram_tensor("d_sqc", [128, 4], F32, kind="ExternalOutput")
    if "ckv" in dump:
        dmp["ckv"] = nc.dram_tensor("d_ckv", [128, 4 * SEQ], BF16, kind="ExternalOutput")
        dmp["cq"] = nc.dram_tensor("d_cq", [128, 4 * OWN], BF16, kind="ExternalOutput")
        dmp["kr"] = nc.dram_tensor("d_kr", [64, SEQ], BF16, kind="ExternalOutput")

    sch = Sch(nc)
    E = sch.emit
    stack = []

    def sb(name, shape, dtype):
        cm = nc.sbuf_tensor(name, list(shape), dtype)
        t = cm.__enter__()
        stack.append((name, cm))
        return t

    nbank_res = [Res() for _ in range(8)]

    def pp(name, shape, dtype=F32):
        b0 = nc.psum_base
        cm = nc.psum_tensor(name, list(shape), dtype)
        t = cm.__enter__()
        b1 = nc.psum_base
        stack.append((name, cm))
        return t, nbank_res[b0:b1]

    def free(upto):
        while stack:
            name, cm = stack.pop()
            cm.__exit__(None, None, None)
            if name == upto:
                return
        raise RuntimeError(f"free: {upto} not found")

    R = Res

    vnop_sb = sb("vnop_sb", [128, 2], F32)
    sch.vnop_ap = vnop_sb[:, 0:2]
    ident_sb = sb("ident_sb", [128, 128], BF16)
    id32_sb = sb("id32_sb", [128, 128], F32)
    ones_c = sb("ones_c", [128, 1], BF16)
    ones_r = sb("ones_r", [1, 128], BF16)
    ones_r32 = sb("ones_r32", [1, 128], F32)
    ebias_sb = sb("ebias_sb", [128, 1], F32)
    ps64_sb = sb("ps64_sb", [64, 64], BF16)
    e8_sb = sb("e8_sb", [8, 256], BF16)
    kpos_sb = sb("kpos_sb", [128, 32], F32)
    wg_sb = sb("wg_sb", [128, 16 * 8], BF16)
    b2s_sb = sb("b2s_sb", [128, 2 * D], BF16)
    rconst = R()
    for dst, src in ((ident_sb, ident), (id32_sb, ident32), (ones_c, ones_col),
                     (ones_r, ones_row), (ones_r32, ones_row32), (ebias_sb, ebias),
                     (ps64_sb, ps64), (e8_sb, e8), (kpos_sb, kpos)):
        E("Y", lambda e, d=dst, s=src: e.dma_start(d[:], s[:]), writes=[rconst], dma=True)
    E("P", lambda e: e.dma_start(wg_sb[:].rearrange("p (a b) -> p a b", a=16), wg[:]),
      writes=[rconst], dma=True)
    E("P", lambda e: e.dma_start(b2s_sb[:].rearrange("p (a b) -> p a b", a=2), b2s[:]),
      writes=[rconst], dma=True)

    r_swo, r_sfs, r_sw2, r_sa13 = R(), R(), R(), R()
    cast_jobs = [(s_wo, wo, r_swo), (s_a1s, a1s, r_sa13), (s_a3s, a3s, r_sa13)]
    for i in range(0, NFC, 7):
        cast_jobs.append((s_fs[i:i + 7], fstream[i:i + 7], r_sfs))
    for i in range(4):
        cast_jobs.append((s_w2[:, i], w2[:, i], r_sw2))

    def emit_casts(k=1):
        for _ in range(k):
            if not cast_jobs:
                return
            dst, src, res = cast_jobs.pop(0)
            E("P", lambda e, d=dst, s=src: e.dma_start(d[:], s[:]), writes=[res], dma=True)

    ckv_sb = sb("ckv_sb", [128, 4 * SEQ], BF16)
    cq_sb = sb("cq_sb", [128, 4 * OWN], BF16)
    kr_sb = sb("kr_sb", [64, SEQ], BF16)
    r_ckv, r_cq, r_kr = R(), R(), R()
    wdq_sb = sb("wdq_sb", [128, 16 * 4 * 128], BF16)
    wdkv_sb = sb("wdkv_sb", [128, 16 * 4 * 128], BF16)
    wkr_sb = sb("wkr_sb", [128, 16 * 64], BF16)
    r_wdq, r_wdkv, r_wkr = R(), R(), R()
    E("P", lambda e: e.dma_start(
        wdq_sb[:].rearrange("p (a b c) -> p a b c", a=16, b=4), wdq[:]),
      writes=[r_wdq], dma=True)
    E("P", lambda e: e.dma_start(
        wdkv_sb[:].rearrange("p (a b c) -> p a b c", a=16, b=4), wdkv[:]),
      writes=[r_wdkv], dma=True)
    E("P", lambda e: e.dma_start(
        wkr_sb[:].rearrange("p (a c) -> p a c", a=16),
        wkr[:].rearrange("p a b c -> p a (b c)")),
      writes=[r_wkr], dma=True)
    kcos_sb = sb("kcos_sb", [64, SEQ], F32)
    ksin_sb = sb("ksin_sb", [64, SEQ], F32)
    r_ktab = R()
    E("Y", lambda e: e.dma_start(kcos_sb[:], kcos[:]), writes=[r_ktab], dma=True)
    E("Y", lambda e: e.dma_start(ksin_sb[:], ksin[:]), writes=[r_ktab], dma=True)

    xb_sb = sb("xb_sb", [128, 3 * 16 * 512], BF16)
    r_xb = [R(), R(), R()]
    xsq_sb = sb("xsq_sb", [128, 2048], BF16)
    r_xsq = R()
    stripA = sb("stripA", [1, 4 * 512], F32)
    r_stripA = R()
    mb_sb = sb("mb_sb", [128, 1024], F32)
    r_mb = R()
    kraw_sb = sb("kraw_sb", [64, 512], BF16)
    r_kraw = R()
    kt_sb = sb("kt_sb", [64, 1024], F32)
    r_kt = R()

    pA_sum, bA_sum = pp("pA_sum", [1, 512])
    pA_sq, bA_sq = pp("pA_sq", [1, 512])
    pA_bc, bA_bc = pp("pA_bc", [128, 1024])
    pA_p1, bA_p1 = pp("pA_p1", [128, 1024])
    pA_p2, bA_p2 = pp("pA_p2", [128, 1024])
    bA_sum, bA_sq = bA_sum[0], bA_sq[0]

    def ln_fm_stats(xs, rx):
        def f_sum(e, x=xs):
            for i in range(16):
                ins = e.matmul(pA_sum[:], ones_c[:], x[:, i * 512:(i + 1) * 512],
                               start=(i == 0), stop=(i == 15))
            return ins
        E("T", f_sum, reads=[rx, rconst], writes=[bA_sum])
        for g in range(4):
            E("V", lambda e, x=xs, g=g: e.tensor_tensor(
                xsq_sb[:], x[:, g * 2048:(g + 1) * 2048],
                x[:, g * 2048:(g + 1) * 2048], ALU.mult),
              reads=[rx], writes=[r_xsq])

            def f_sq(e, g=g):
                for i in range(4):
                    ins = e.matmul(pA_sq[:], ones_c[:], xsq_sb[:, i * 512:(i + 1) * 512],
                                   start=(g == 0 and i == 0), stop=(g == 3 and i == 3))
                return ins
            E("T", f_sq, reads=[r_xsq, rconst], writes=[bA_sq] if g == 3 else [])
        mS = stripA[:, 0:512]
        vS = stripA[:, 512:1024]
        tS = stripA[:, 1024:1536]
        E("V", lambda e: e.tensor_scalar(mS, pA_sum[:], 1.0 / D, None, ALU.mult),
          reads=[bA_sum], writes=[r_stripA])
        E("V", lambda e: e.tensor_scalar(tS, pA_sq[:], 1.0 / D, EPS, ALU.mult, ALU.add),
          reads=[bA_sq], writes=[r_stripA])
        E("V", lambda e: e.tensor_tensor(vS, mS, mS, ALU.mult), writes=[r_stripA])
        E("V", lambda e: e.tensor_tensor(vS, tS, vS, ALU.subtract), writes=[r_stripA])
        E("S", lambda e: e.activation(vS, vS, AF.Ln), reads=[r_stripA], writes=[r_stripA])
        E("S", lambda e: e.activation(vS, vS, AF.Exp, scale=-0.5),
          reads=[r_stripA], writes=[r_stripA])
        E("T", lambda e: e.matmul(pA_bc[:, 0:512], ones_r32[:], mS, start=True, stop=True),
          reads=[r_stripA, rconst], writes=[bA_bc[0]])
        E("T", lambda e: e.matmul(pA_bc[:, 512:1024], ones_r32[:], vS, start=True, stop=True),
          reads=[r_stripA], writes=[bA_bc[1]])
        E("V", lambda e: e.tensor_copy(mb_sb[:], pA_bc[:]),
          reads=bA_bc, writes=[r_mb])
        for i in range(16):
            E("V", lambda e, x=xs, i=i: e.tensor_tensor(
                x[:, i * 512:(i + 1) * 512], x[:, i * 512:(i + 1) * 512],
                mb_sb[:, 0:512], ALU.subtract), reads=[r_mb], writes=[rx])
        for i in range(16):
            E("V", lambda e, x=xs, i=i: e.tensor_tensor(
                x[:, i * 512:(i + 1) * 512], x[:, i * 512:(i + 1) * 512],
                mb_sb[:, 512:1024], ALU.mult), reads=[r_mb], writes=[rx])

    prjL = [(pA_p1[:, 0:512], bA_p1[0]), (pA_p1[:, 512:1024], bA_p1[1]),
            (pA_p2[:, 0:512], bA_p2[0]), (pA_p2[:, 512:1024], bA_p2[1])]

    def phaseA_load(idx, src):
        slot = idx % 3
        xs = xb_sb[:, slot * 8192:(slot + 1) * 8192]
        rx = r_xb[slot]
        E("P", lambda e, s=src, x=xs: e.dma_start(
            x.rearrange("p (i t) -> p i t", i=16), s), writes=[rx], dma=True)

    def phaseA_block(idx, tb, own):
        slot = idx % 3
        xs = xb_sb[:, slot * 8192:(slot + 1) * 8192]
        rx = r_xb[slot]
        ln_fm_stats(xs, rx)
        if own:
            wsb, rw, dst, rdst, blk = wdq_sb, r_wdq, cq_sb, r_cq, OWN
        else:
            wsb, rw, dst, rdst, blk = wdkv_sb, r_wdkv, ckv_sb, r_ckv, SEQ
        for j in range(4):
            pv, br = prjL[j]

            def f_prj(e, x=xs, w=wsb, j=j, pv=pv):
                for i in range(16):
                    ins = e.matmul(pv, w[:, (i * 4 + j) * 128:(i * 4 + j) * 128 + 128],
                                   x[:, i * 512:(i + 1) * 512],
                                   start=(i == 0), stop=(i == 15))
                return ins
            E("T", f_prj, reads=[rx, rw], writes=[br])
            E("V", lambda e, pv=pv, d=dst, j=j, tb=tb, blk=blk: e.tensor_copy(
                d[:, j * blk + tb * 512: j * blk + tb * 512 + 512], pv),
              reads=[br], writes=[rdst])
        if not own:
            pv, br = prjL[0]

            def f_kr(e, x=xs, pv=pv):
                pv64 = pv[0:64, :]
                for i in range(16):
                    ins = e.matmul(pv64, wkr_sb[:, i * 64:(i + 1) * 64],
                                   x[:, i * 512:(i + 1) * 512],
                                   start=(i == 0), stop=(i == 15))
                return ins
            E("T", f_kr, reads=[rx, r_wkr], writes=[br])
            E("V", lambda e, pv=pv: e.tensor_copy(kraw_sb[:], pv[0:64, :]),
              reads=[br], writes=[r_kraw])
            pv2, br2 = prjL[1]
            E("T", lambda e, pv2=pv2: e.matmul(pv2[0:64, :], ps64_sb[:], kraw_sb[:],
                                               start=True, stop=True),
              reads=[r_kraw, rconst], writes=[br2])
            t1 = kt_sb[:, 0:512]
            t2 = kt_sb[:, 512:1024]
            E("V", lambda e, tb=tb, t1=t1: e.tensor_tensor(
                t1, kraw_sb[:], kcos_sb[:, tb * 512:(tb + 1) * 512], ALU.mult),
              reads=[r_kraw, r_ktab], writes=[r_kt])
            E("V", lambda e, pv2=pv2, tb=tb, t2=t2: e.tensor_tensor(
                t2, pv2[0:64, :], ksin_sb[:, tb * 512:(tb + 1) * 512], ALU.mult),
              reads=[br2, r_ktab], writes=[r_kt])
            E("V", lambda e, tb=tb, t1=t1, t2=t2: e.tensor_tensor(
                kr_sb[:, tb * 512:(tb + 1) * 512], t1, t2, ALU.add),
              reads=[r_kt], writes=[r_kr])

    A_jobs = [(tb, xT[:, :, tb * 512:(tb + 1) * 512], False) for tb in range(8)]
    A_jobs += [(tb, xTo[:, :, tb * 512:(tb + 1) * 512], True) for tb in range(4)]
    phaseA_load(0, A_jobs[0][1])
    phaseA_load(1, A_jobs[1][1])
    for idx, (tb, src, own) in enumerate(A_jobs):
        if idx + 2 < len(A_jobs):
            phaseA_load(idx + 2, A_jobs[idx + 2][1])
        phaseA_block(idx, tb, own)

    if "ckv" in dump:
        rd = R()
        E("S", lambda e: e.dma_start(dmp["ckv"][:], ckv_sb[:]), reads=[r_ckv],
          writes=[rd], dma=True)
        E("S", lambda e: e.dma_start(dmp["cq"][:], cq_sb[:]), reads=[r_cq],
          writes=[rd], dma=True)
        E("S", lambda e: e.dma_start(dmp["kr"][:], kr_sb[:]), reads=[r_kr],
          writes=[rd], dma=True)

    free("wdq_sb")
    sch.gate()

    qcos_sb = sb("qcos_sb", [64, OWN], F32)
    qsin_sb = sb("qsin_sb", [64, OWN], F32)
    qpos_sb = sb("qpos_sb", [128, OWN], F32)
    r_qtab = R()
    E("Y", lambda e: e.dma_start(qcos_sb[:], qcos[:]), writes=[r_qtab], dma=True)
    E("Y", lambda e: e.dma_start(qsin_sb[:], qsin[:]), writes=[r_qtab], dma=True)
    E("Y", lambda e: e.dma_start(qpos_sb[:], qposb[:]), writes=[r_qtab], dma=True)

    WSL = 4 * 128 * 3 + 4 * 64
    wslf = sb("wslf", [128, WSL], F32)
    wslb = sb("wslb", [128, 2 * WSL], BF16)
    r_wslf = R()
    r_wslb = [R(), R()]
    khi_sb = sb("khi_sb", [128, SEQ], BF16)
    v_sb = sb("v_sb", [128, SEQ], BF16)
    qhi_sb = sb("qhi_sb", [128, OWN], BF16)
    qlo_sb = sb("qlo_sb", [64, OWN], BF16)
    qraw_sb = sb("qraw_sb", [64, OWN], BF16)
    qt_sb = sb("qt_sb", [64, 1024], F32)
    r_khi, r_v, r_qhi, r_qlo, r_qraw, r_qt = R(), R(), R(), R(), R(), R()
    p_sb = sb("p_sb", [128, 3 * 1024], BF16)
    r_p = [[R(), R()], [R(), R()], [R(), R()]]
    rl_sb = sb("rl_sb", [1, 512], F32)
    r_rl = R()
    atmp_sb = sb("atmp_sb", [128, 2 * 512], BF16)
    r_atmp = [R(), R()]

    pB_sc, bB_sc6 = pp("pB_sc", [128, 3072])
    pB_acc, bB_acc = pp("pB_acc", [128, 512])
    pB_l, bB_l = pp("pB_l", [1, 512])
    bB_sc = [bB_sc6[0:2], bB_sc6[2:4], bB_sc6[4:6]]
    pB_pr = pB_sc[:, 2048:3072]
    bB_pr = bB_sc6[4:6]
    bB_acc, bB_l = bB_acc[0], bB_l[0]

    def load_wslice(h):
        tok = None
        first = True
        for off, src, w in ((0, wuq, 512), (512, wuk, 512), (1024, wuv, 512),
                            (1536, wqr, 256)):
            tok = E("Y", lambda e, o=off, s=src, w=w, h=h: e.dma_start(
                wslf[:, o:o + w].rearrange("p (i c) -> p i c", i=4), s[:, :, h, :]),
                writes=[r_wslf] if first else [], dma=True)
            first = False
        r_wslf.ready = tok
        r_wslf.users = []
        slot = h % 2
        E("V", lambda e, slot=slot: e.tensor_copy(
            wslb[:, slot * WSL:(slot + 1) * WSL], wslf[:]),
          reads=[r_wslf], writes=[r_wslb[slot]])

    load_wslice(0)
    pending_evac = [None]
    for h in range(NH):
        emit_casts(3)
        slot = h % 2
        base = slot * WSL
        rb = r_wslb[slot]
        if h + 1 < NH:
            load_wslice(h + 1)
        for kb in range(8):
            half = kb % 2
            pv = pB_pr[:, half * 512:(half + 1) * 512]
            br = bB_pr[half]

            def f_khi(e, kb=kb, base=base, pv=pv):
                for i in range(4):
                    ins = e.matmul(pv, wslb[:, base + 512 + i * 128:base + 512 + i * 128 + 128],
                                   ckv_sb[:, i * SEQ + kb * 512:i * SEQ + kb * 512 + 512],
                                   start=(i == 0), stop=(i == 3))
                return ins
            E("T", f_khi, reads=[rb, r_ckv], writes=[br])
            E("V", lambda e, pv=pv, kb=kb: e.tensor_copy(
                khi_sb[:, kb * 512:(kb + 1) * 512], pv), reads=[br], writes=[r_khi])
        for g in range(8):
            half = g % 2
            pv = pB_pr[:, half * 512:(half + 1) * 512]
            br = bB_pr[half]

            def f_v(e, g=g, base=base, pv=pv):
                for c in range(4):
                    kc = g * 4 + c
                    for i in range(4):
                        ins = e.matmul(
                            pv[:, c * 128:(c + 1) * 128],
                            ckv_sb[:, i * SEQ + kc * 128:i * SEQ + kc * 128 + 128],
                            wslb[:, base + 1024 + i * 128:base + 1024 + i * 128 + 128],
                            start=(i == 0), stop=(i == 3))
                return ins
            E("T", f_v, reads=[rb, r_ckv], writes=[br])
            E("V", lambda e, pv=pv, g=g: e.tensor_copy(
                v_sb[:, g * 512:(g + 1) * 512], pv), reads=[br], writes=[r_v])
        for qb in range(4):
            half = qb % 2
            pv = pB_pr[:, half * 512:(half + 1) * 512]
            br = bB_pr[half]

            def f_qhi(e, qb=qb, base=base, pv=pv):
                for i in range(4):
                    ins = e.matmul(pv, wslb[:, base + i * 128:base + i * 128 + 128],
                                   cq_sb[:, i * OWN + qb * 512:i * OWN + qb * 512 + 512],
                                   start=(i == 0), stop=(i == 3))
                return ins
            E("T", f_qhi, reads=[rb, r_cq], writes=[br])
            E("V", lambda e, pv=pv, qb=qb: e.tensor_copy(
                qhi_sb[:, qb * 512:(qb + 1) * 512], pv), reads=[br], writes=[r_qhi])
        for qb in range(4):
            half = qb % 2
            pv = pB_pr[:, half * 512:(half + 1) * 512]
            br = bB_pr[half]

            def f_qr(e, qb=qb, base=base, pv=pv):
                pv64 = pv[0:64, :]
                for i in range(4):
                    ins = e.matmul(pv64, wslb[:, base + 1536 + i * 64:base + 1536 + i * 64 + 64],
                                   cq_sb[:, i * OWN + qb * 512:i * OWN + qb * 512 + 512],
                                   start=(i == 0), stop=(i == 3))
                return ins
            E("T", f_qr, reads=[rb, r_cq], writes=[br])
            E("V", lambda e, pv=pv, qb=qb: e.tensor_copy(
                qraw_sb[:, qb * 512:(qb + 1) * 512], pv[0:64, :]),
              reads=[br], writes=[r_qraw])
            pv2 = pB_pr[:, (1 - half) * 512:(2 - half) * 512]
            br2 = bB_pr[1 - half]
            E("T", lambda e, pv2=pv2, qb=qb: e.matmul(
                pv2[0:64, :], ps64_sb[:], qraw_sb[:, qb * 512:(qb + 1) * 512],
                start=True, stop=True),
              reads=[r_qraw, rconst], writes=[br2])
            t1 = qt_sb[:, 0:512]
            t2 = qt_sb[:, 512:1024]
            E("V", lambda e, qb=qb, t1=t1: e.tensor_tensor(
                t1, qraw_sb[:, qb * 512:(qb + 1) * 512],
                qcos_sb[:, qb * 512:(qb + 1) * 512], ALU.mult),
              reads=[r_qraw, r_qtab], writes=[r_qt])
            E("V", lambda e, pv2=pv2, qb=qb, t2=t2: e.tensor_tensor(
                t2, pv2[0:64, :], qsin_sb[:, qb * 512:(qb + 1) * 512], ALU.mult),
              reads=[br2, r_qtab], writes=[r_qt])
            E("V", lambda e, qb=qb, t1=t1, t2=t2: e.tensor_tensor(
                qlo_sb[:, qb * 512:(qb + 1) * 512], t1, t2, ALU.add),
              reads=[r_qt], writes=[r_qlo])
        for qt in range(NQT):
            np2 = KCS[qt] // 2

            def emit_scores(j2, qt=qt, np2=np2):
                sslot = j2 % 3
                sc = pB_sc[:, sslot * 1024:(sslot + 1) * 1024]
                bsc = bB_sc[sslot]
                kc0 = 2 * j2
                for half in range(2):
                    kc = kc0 + half
                    schalf = sc[:, half * 512:(half + 1) * 512]

                    def f_sc(e, kc=kc, qt=qt, schalf=schalf):
                        e.matmul(schalf,
                                 khi_sb[:, kc * 128:(kc + 1) * 128],
                                 qhi_sb[:, qt * 512:(qt + 1) * 512],
                                 start=True, stop=False)
                        return e.matmul(schalf,
                                        kr_sb[:, kc * 128:(kc + 1) * 128],
                                        qlo_sb[:, qt * 512:(qt + 1) * 512],
                                        start=False, stop=True)
                    E("T", f_sc, reads=[r_khi, r_qhi, r_qlo, r_kr],
                      writes=[bsc[half]])
                    ph = p_sb[:, sslot * 1024 + half * 512:sslot * 1024 + half * 512 + 512]
                    rp = r_p[sslot][half]
                    E("S", lambda e, sc=schalf, pq=ph: e.activation(
                        pq, sc, AF.Exp, bias=ebias_sb[:, 0:1], scale=SCALE),
                      reads=[bsc[half]], writes=[rp])
                    E("V", lambda e, pq=ph: e.tensor_scalar(
                        pq, pq, 1.0, PLO, ALU.min, ALU.max), writes=[rp])
                    if j2 >= np2 - 4:
                        E("V", lambda e, pq=ph, qt=qt, kc=kc:
                          e.scalar_tensor_tensor(
                              pq, qpos_sb[:, qt * 512:(qt + 1) * 512],
                              kpos_sb[:, kc:kc + 1], pq,
                              ALU.is_ge, ALU.mult),
                          reads=[rconst, r_qtab], writes=[rp], haz=True)

            def emit_pv(j2, qt=qt):
                sslot = j2 % 3
                kc0 = 2 * j2
                for half in range(2):
                    kc = kc0 + half
                    ph = p_sb[:, sslot * 1024 + half * 512:sslot * 1024 + half * 512 + 512]
                    rp = r_p[sslot][half]
                    st = (j2 == 0 and half == 0)

                    def f_pv(e, kc=kc, st=st, pq=ph):
                        e.matmul(pB_l[:], ones_c[:], pq, start=st, stop=False)
                        return e.matmul(pB_acc[:], v_sb[:, kc * 128:(kc + 1) * 128],
                                        pq, start=st, stop=False)
                    E("T", f_pv, reads=[rp, r_v, rconst], writes=[bB_acc, bB_l])

            emit_scores(0)
            if np2 > 1:
                emit_scores(1)
            if pending_evac[0] is not None:
                pending_evac[0]()
                pending_evac[0] = None
            for j2 in range(np2):
                if j2 + 2 < np2:
                    emit_scores(j2 + 2)
                emit_pv(j2)

            def evac(h=h, qt=qt):
                aslot = qt % 2
                atile = atmp_sb[:, aslot * 512:(aslot + 1) * 512]
                ratile = r_atmp[aslot]
                E("V", lambda e, a=atile: e.tensor_copy(a, pB_acc[:]),
                  reads=[bB_acc], writes=[ratile])
                E("V", lambda e: e.reciprocal(rl_sb[:], pB_l[:]), reads=[bB_l],
                  writes=[r_rl], haz=True)
                E("T", lambda e: e.matmul(pB_acc[:], ones_r32[:], rl_sb[:],
                                          start=True, stop=True),
                  reads=[r_rl, rconst], writes=[bB_acc])
                E("V", lambda e, a=atile: e.tensor_tensor(a, a, pB_acc[:], ALU.mult),
                  reads=[bB_acc], writes=[ratile])
                E("S", lambda e, a=atile, h=h, qt=qt: e.dma_start(
                    attnd[:, h * OWN + qt * 512: h * OWN + qt * 512 + 512], a),
                  reads=[ratile], dma=True)
            pending_evac[0] = evac

    if pending_evac[0] is not None:
        pending_evac[0]()
        pending_evac[0] = None
    emit_casts(100)
    r_attnd = R()
    r_attnd.ready = ("dS", sch.cnt["dS"])
    free("qcos_sb")
    free("ckv_sb")
    sch.gate()

    wo_sb = sb("wo_sb", [128, 2 * NH * 512], BF16)
    r_wo = [R(), R()]
    attc_sb = sb("attc_sb", [128, 2 * NH * 512], BF16)
    r_attc = [R(), R()]
    attno_sb = sb("attno_sb", [128, 4 * D], F32)
    r_attno = [R() for _ in range(4)]
    sums_sb = sb("sums_sb", [128, 16], F32)
    r_sums = [R() for _ in range(4)]
    sqc_sb = sb("sqc_sb", [128, 4], F32)
    r_sqc = [R() for _ in range(4)]
    scrC = sb("scrC", [128, 2048], F32)
    r_scrC = R()
    stripC = sb("stripC", [128, 8], F32)
    r_stripC = R()
    xtm_sb = sb("xtm_sb", [128, 2 * D], F32)
    r_xtm = [R(), R()]
    x1b_sb = sb("x1b_sb", [128, 2 * D], F32)
    r_x1b = [R(), R()]
    pC, bC = pp("pC", [128, 2048])
    r_x1d = R()

    def ln_tm_strips(sumcol, sqcol, icol, ncol, rs, extra_reads):
        E("V", lambda e: e.tensor_scalar(ncol, sumcol, 1.0 / D, None, ALU.mult),
          reads=extra_reads, writes=[rs])
        E("V", lambda e: e.tensor_scalar(icol, sqcol, 1.0 / D, EPS, ALU.mult, ALU.add),
          reads=extra_reads, writes=[rs])
        E("V", lambda e: e.scalar_tensor_tensor(
            icol, ncol, ncol, icol, ALU.mult, ALU.subtract),
          writes=[rs], haz=True)
        E("V", lambda e: e.tensor_scalar(icol, icol, -1.0, None, ALU.mult),
          writes=[rs])
        E("S", lambda e: e.activation(icol, icol, AF.Ln), reads=[rs], writes=[rs])
        E("S", lambda e: e.activation(icol, icol, AF.Exp, scale=-0.5),
          reads=[rs], writes=[rs])
        E("V", lambda e: e.scalar_tensor_tensor(
            ncol, ncol, -1.0, icol, ALU.mult, ALU.mult),
          writes=[rs], haz=True)

    for qs in range(4):
        aslot = qs % 2
        attc = attc_sb[:, aslot * NH * 512:(aslot + 1) * NH * 512]
        rattc = r_attc[aslot]
        E("Y", lambda e, qs=qs, a=attc: e.dma_start(
            a.rearrange("p (h c) -> p h c", h=NH),
            attnd[:].rearrange("p (h t) -> p h t", h=NH)[:, :, qs * 512:(qs + 1) * 512]),
          reads=[r_attnd], writes=[rattc], dma=True)
        for ob in range(4):
            wslot = ob % 2
            wdst = wo_sb[:, wslot * NH * 512:(wslot + 1) * NH * 512]
            rwo = r_wo[wslot]
            E("Y", lambda e, ob=ob, w=wdst: e.dma_start(
                w.rearrange("p (h c) -> p h c", h=NH), s_wo[:, ob]),
              reads=[r_swo], writes=[rwo], dma=True)
            for qc4 in range(4):
                pv = pC[:, qc4 * 512:(qc4 + 1) * 512]
                bb = bC[qc4]

                def f_op(e, a=attc, qc4=qc4, w=wdst, pv=pv):
                    for hh in range(NH):
                        ins = e.matmul(pv, a[:, hh * 512 + qc4 * 128:hh * 512 + qc4 * 128 + 128],
                                       w[:, hh * 512:(hh + 1) * 512],
                                       start=(hh == 0), stop=(hh == NH - 1))
                    return ins
                E("T", f_op, reads=[rwo, rattc], writes=[bb])
                E("S", lambda e, pv=pv, qc4=qc4, ob=ob: e.activation(
                    attno_sb[:, qc4 * D + ob * 512:qc4 * D + ob * 512 + 512], pv,
                    AF.Copy),
                  reads=[bb], writes=[r_attno[qc4]])
        for qc4 in range(4):
            qc = qs * 4 + qc4
            asl = attno_sb[:, qc4 * D:(qc4 + 1) * D]
            xslot = qc4 % 2
            xtile = xtm_sb[:, xslot * D:(xslot + 1) * D]
            rx = r_xtm[xslot]
            E("Y", lambda e, qc=qc, xx=xtile: e.dma_start(xx, xo[qc * 128:(qc + 1) * 128, :]),
              writes=[rx], dma=True)
            E("V", lambda e, a=asl, qc4=qc4: e.scalar_tensor_tensor(
                scrC[:], a, 1.0, a, ALU.mult, ALU.mult,
                accum_out=sqc_sb[:, qc4:qc4 + 1]),
              reads=[r_attno[qc4]], writes=[r_scrC, r_sqc[qc4]], haz=True)
            scol = stripC[:, 0:1]
            icol = stripC[:, 1:2]
            ncol = stripC[:, 2:3]
            E("V", lambda e, a=asl, scol=scol: e.tensor_reduce(scol, a, AX.X, ALU.add),
              reads=[r_attno[qc4]], writes=[r_stripC], haz=True)
            ln_tm_strips(scol, sqc_sb[:, qc4:qc4 + 1], icol, ncol, r_stripC,
                         [r_sqc[qc4]])
            x1t = x1b_sb[:, xslot * D:(xslot + 1) * D]
            rx1 = r_x1b[xslot]
            E("V", lambda e, a=asl, x1t=x1t, icol=icol, ncol=ncol: e.tensor_scalar(
                x1t, a, icol, ncol, ALU.mult, ALU.add),
              reads=[r_attno[qc4], r_stripC], writes=[rx1])
            E("V", lambda e, x1t=x1t, xx=xtile: e.tensor_tensor(x1t, x1t, xx, ALU.add),
              reads=[rx], writes=[rx1])
            E("S", lambda e, x1t=x1t, qc=qc: e.dma_start(
                x1d[qc * 128:(qc + 1) * 128, :], x1t), reads=[rx1], writes=[r_x1d],
              dma=True)

    if "attno" in dump:
        rd = R()
        E("S", lambda e: e.dma_start(dmp["attno"][:], attno_sb[:]),
          reads=r_attno, writes=[rd], dma=True)
        E("S", lambda e: e.dma_start(dmp["strip"][:], stripC[:]),
          reads=[r_stripC], writes=[rd], dma=True)
        E("S", lambda e: e.dma_start(dmp["sums"][:], sums_sb[:]),
          reads=r_sums, writes=[rd], dma=True)
        E("S", lambda e: e.dma_start(dmp["sqc"][:], sqc_sb[:]),
          reads=r_sqc, writes=[rd], dma=True)
    free("wo_sb")
    sch.gate()

    y_sb = sb("y_sb", [128, NFC * 512], BF16)
    r_y = R()
    xnf_sb = sb("xnf_sb", [128, 16 * 512], BF16)
    r_xnf = R()
    xnt_sb = sb("xnt_sb", [128, 2048], BF16)
    r_xnt = R()
    x1t_sb = sb("x1t_sb", [128, 2 * 2048], F32)
    r_x1t = [R(), R()]
    fsr_sb = sb("fsr_sb", [128, 2 * FS_COLS], BF16)
    r_fsr = [R(), R()]
    a2r_sb = sb("a2r_sb", [128, 3 * 256], BF16)
    r_a2r = [R(), R(), R()]
    a13_sb = sb("a13_sb", [128, 16 * 128], BF16)
    r_a13 = R()
    ash_sb = sb("ash_sb", [128, 8 * 512], BF16)
    r_ash = R()
    m256_sb = sb("m256_sb", [128, 4 * 512], BF16)
    r_m256 = R()
    rwb_sb = sb("rwb_sb", [128, 2 * 512], F32)
    r_rwb = R()
    lgE_sb = sb("lgE_sb", [128, 32], F32)
    r_lgE = R()
    g01_sb = sb("g01_sb", [128, 16], BF16)
    r_g01 = R()
    rwp_sb = sb("rwp_sb", [128, 2], F32)
    r_rwp = R()
    maskf_sb = sb("maskf_sb", [8, 1024], BF16)
    r_maskf = R()
    rws_sb = sb("rws_sb", [1, 1024], F32)
    r_rws = R()
    stripD = sb("stripD", [128, 16], F32)
    r_stripD = R()
    scrD = sb("scrD", [128, 2048], F32)
    r_scrD = R()
    z_sb = scrD[:, 0:1024]
    r_z = [R(), R()]
    x2_sb = scrD[:, 1024:1536]
    r_x2 = R()
    y01_sb = sb("y01_sb", [128, 4 * 512], BF16)
    r_y01 = [R(), R(), R(), R()]
    a2e_sb = sb("a2e_sb", [128, 4 * 512], BF16)
    r_a2e = R()
    w2r_sb = sb("w2r_sb", [128, 2 * 1024], BF16)
    r_w2r = [R(), R()]
    fout_sb = sb("fout_sb", [128, 4 * 2048], BF16)
    r_fout = [R() for _ in range(4)]
    sumd_sb = sb("sumd_sb", [128, 16], F32)
    r_sumd = [R() for _ in range(4)]
    sqd_sb = sb("sqd_sb", [128, 4], F32)
    r_sqd = [R() for _ in range(4)]
    ob_sb = sb("ob_sb", [128, 2 * 2048], F32)
    r_ob = [R(), R()]

    pD_rot, bD_rot = pp("pD_rot", [128, 1536])
    pD_tr = None
    rotv = [(pD_rot[:, i * 512:(i + 1) * 512], bD_rot[i]) for i in range(3)]
    trp_t, trp_b = pp("pD_tr", [128, 1024], BF16)
    trp_b = trp_b[0]


    rotc = [0]

    def rot_next():
        i = rotc[0] % 3
        rotc[0] += 1
        return rotv[i]

    def emit_prologue(tbk):
        for tc in range(4):
            qc = tbk * 4 + tc
            xslot = tc % 2
            x1t = x1t_sb[:, xslot * 2048:(xslot + 1) * 2048]
            rx1 = r_x1t[xslot]
            E("Y", lambda e, qc=qc, xx=x1t: e.dma_start(
                xx, x1d[qc * 128:(qc + 1) * 128, :]), reads=[r_x1d], writes=[rx1],
              dma=True)
            scol = stripD[:, 0:1]
            sqcol = stripD[:, 1:2]
            icol = stripD[:, 2:3]
            ncol = stripD[:, 3:4]
            E("V", lambda e, xx=x1t, scol=scol: e.tensor_reduce(scol, xx, AX.X, ALU.add),
              reads=[rx1], writes=[r_stripD], haz=True)
            E("V", lambda e, xx=x1t, sqcol=sqcol: e.scalar_tensor_tensor(
                scrD[:], xx, 1.0, xx, ALU.mult, ALU.mult, accum_out=sqcol),
              reads=[rx1], writes=[r_scrD, r_stripD], haz=True)
            ln_tm_strips(scol, sqcol, icol, ncol, r_stripD, [])
            E("V", lambda e, xx=x1t, icol=icol, ncol=ncol: e.tensor_scalar(
                xnt_sb[:], xx, icol, ncol, ALU.mult, ALU.add),
              reads=[rx1, r_stripD], writes=[r_xnt])
            for b8 in range(2):
                def f_tr(e, b8=b8):
                    for i in range(8):
                        ins = e.transpose(trp_t[:, i * 128:(i + 1) * 128],
                                          xnt_sb[:, (b8 * 8 + i) * 128:(b8 * 8 + i) * 128 + 128],
                                          ident_sb[:])
                    return ins
                E("T", f_tr, reads=[r_xnt, rconst], writes=[trp_b])
                E("V", lambda e, b8=b8, tc=tc: e.tensor_copy(
                    xnf_sb[:].rearrange("p (i t) -> p i t", i=16)
                    [:, b8 * 8:(b8 + 1) * 8, tc * 128:(tc + 1) * 128],
                    trp_t[:].rearrange("p (i t) -> p i t", i=8)),
                  reads=[trp_b], writes=[r_xnf])
            pvl, bvl = rotv[0]
            lgp = pvl[:, 0:8]

            def f_lg(e, tc=tc, lgp=lgp):
                for i in range(16):
                    ins = e.matmul(lgp, xnf_sb[:, i * 512 + tc * 128:i * 512 + tc * 128 + 128],
                                   wg_sb[:, i * 8:(i + 1) * 8],
                                   start=(i == 0), stop=(i == 15))
                return ins
            E("T", f_lg, reads=[r_xnf, rconst], writes=[bvl])
            lg = lgE_sb[:, 0:8]
            Ee = lgE_sb[:, 8:16]
            E2 = lgE_sb[:, 16:24]
            c0 = lgE_sb[:, 24:25]
            c1 = lgE_sb[:, 25:26]
            E("V", lambda e, lgp=lgp: e.tensor_copy(lg, lgp), reads=[bvl], writes=[r_lgE], haz=True, tag=f"rt{tc}:cplg")
            E("V", lambda e: e.tensor_reduce(c0, lg, AX.X, ALU.max), writes=[r_lgE], haz=True, tag=f"rt{tc}:max")
            E("V", lambda e: e.tensor_scalar(c1, c0, -1.0, None, ALU.mult), writes=[r_lgE], haz=True, tag=f"rt{tc}:negc")
            E("S", lambda e: e.activation(Ee, lg, AF.Exp, bias=c1), reads=[r_lgE],
              writes=[r_lgE], tag=f"rt{tc}:exp")
            g0 = g01_sb[:, 0:8]
            g1 = g01_sb[:, 8:16]
            E("V", lambda e: e.tensor_scalar(g0, lg, c0, None, ALU.is_equal),
              reads=[r_lgE], writes=[r_g01], haz=True)
            E("V", lambda e: e.tensor_scalar(E2, g0, -1.0, 1.0, ALU.mult, ALU.add),
              reads=[r_g01], writes=[r_lgE], haz=True)
            E("V", lambda e: e.tensor_tensor(E2, E2, Ee, ALU.mult), writes=[r_lgE], haz=True)
            E("V", lambda e: e.tensor_reduce(c0, E2, AX.X, ALU.max), writes=[r_lgE], haz=True)
            E("V", lambda e: e.tensor_scalar(g1, E2, c0, None, ALU.is_equal),
              reads=[r_lgE], writes=[r_g01], haz=True)
            rw0 = rwp_sb[:, 0:1]
            rw1 = rwp_sb[:, 1:2]
            E("V", lambda e: e.tensor_scalar(c1, c0, 1.0, None, ALU.add), writes=[r_lgE], haz=True)
            E("V", lambda e: e.reciprocal(rw0, c1), reads=[r_lgE], writes=[r_rwp], haz=True, tag=f"rt{tc}:recip")
            E("V", lambda e: e.tensor_tensor(rw1, c0, rw0, ALU.mult),
              reads=[r_lgE], writes=[r_rwp], haz=True)
            def f_trm(e, tc=tc):
                e.transpose(trp_t[0:8, 0 * 512 + tc * 128:0 * 512 + tc * 128 + 128],
                            g01_sb[:, 0:8], ident_sb[:])
                return e.transpose(trp_t[0:8, 1 * 512 + tc * 128:1 * 512 + tc * 128 + 128],
                                   g01_sb[:, 8:16], ident_sb[:])
            E("T", f_trm, reads=[r_g01, rconst], writes=[trp_b])
            E("V", lambda e, tc=tc: e.tensor_copy(
                maskf_sb[:, 0 * 512 + tc * 128:0 * 512 + tc * 128 + 128],
                trp_t[0:8, 0 * 512 + tc * 128:0 * 512 + tc * 128 + 128]),
              reads=[trp_b], writes=[r_maskf])
            E("V", lambda e, tc=tc: e.tensor_copy(
                maskf_sb[:, 1 * 512 + tc * 128:1 * 512 + tc * 128 + 128],
                trp_t[0:8, 1 * 512 + tc * 128:1 * 512 + tc * 128 + 128]),
              reads=[trp_b], writes=[r_maskf])
            pvr, bvr = rotv[1]

            def f_trw(e, pvr=pvr):
                e.transpose(pvr[0:1, 0:128], rwp_sb[:, 0:1], id32_sb[:])
                return e.transpose(pvr[0:1, 128:256], rwp_sb[:, 1:2], id32_sb[:])
            E("T", f_trw, reads=[r_rwp, rconst], writes=[bvr])
            E("V", lambda e, tc=tc, pvr=pvr: e.tensor_copy(
                rws_sb[0:1, tc * 128:(tc + 1) * 128], pvr[0:1, 0:128]),
              reads=[bvr], writes=[r_rws])
            E("V", lambda e, tc=tc, pvr=pvr: e.tensor_copy(
                rws_sb[0:1, 512 + tc * 128:512 + (tc + 1) * 128], pvr[0:1, 128:256]),
              reads=[bvr], writes=[r_rws])
        for k in range(2):
            for j in range(2):
                pv, bv = rot_next()

                def f_ex(e, k=k, j=j, pv=pv):
                    return e.matmul(pv, e8_sb[:, j * 128:(j + 1) * 128],
                                    maskf_sb[0:8, k * 512:(k + 1) * 512],
                                    start=True, stop=True)
                E("T", f_ex, reads=[r_maskf, rconst], writes=[bv])
                E("V", lambda e, k=k, j=j, pv=pv: e.tensor_copy(
                    m256_sb[:, (k * 2 + j) * 512:(k * 2 + j) * 512 + 512], pv),
                  reads=[bv], writes=[r_m256])
            pv, bv = rot_next()
            E("T", lambda e, k=k, pv=pv: e.matmul(
                pv, ones_r32[:], rws_sb[0:1, k * 512:(k + 1) * 512], start=True, stop=True),
              reads=[r_rws, rconst], writes=[bv])
            E("V", lambda e, k=k, pv=pv: e.tensor_copy(
                rwb_sb[:, k * 512:(k + 1) * 512], pv), reads=[bv], writes=[r_rwb])
        for (m, asrc) in ((0, s_a1s), (1, s_a3s)):
            for j in range(2):
                E("Y", lambda e, asrc=asrc, j=j: e.dma_start(
                    a13_sb[:].rearrange("p (a c) -> p a c", a=16), asrc[:, :, j, :]),
                  reads=[r_sa13], writes=[r_a13], dma=True)
                pv, bv = rot_next()

                def f_a(e, j=j, pv=pv):
                    for i in range(16):
                        ins = e.matmul(pv, a13_sb[:, i * 128:(i + 1) * 128],
                                       xnf_sb[:, i * 512:(i + 1) * 512],
                                       start=(i == 0), stop=(i == 15))
                    return ins
                E("T", f_a, reads=[r_a13, r_xnf], writes=[bv])
                for k in range(2):
                    E("V", lambda e, m=m, k=k, j=j, pv=pv: e.tensor_tensor(
                        ash_sb[:, ((m * 2 + k) * 2 + j) * 512:((m * 2 + k) * 2 + j) * 512 + 512],
                        pv, m256_sb[:, (k * 2 + j) * 512:(k * 2 + j) * 512 + 512],
                        ALU.mult),
                      reads=[bv, r_m256], writes=[r_ash])
                E("V", lambda e, m=m, j=j: e.tensor_tensor(
                    ash_sb[:, ((m * 2 + 1) * 2 + j) * 512:((m * 2 + 1) * 2 + j) * 512 + 512],
                    ash_sb[:, ((m * 2 + 1) * 2 + j) * 512:((m * 2 + 1) * 2 + j) * 512 + 512],
                    ash_sb[:, ((m * 2 + 0) * 2 + j) * 512:((m * 2 + 0) * 2 + j) * 512 + 512],
                    ALU.subtract), writes=[r_ash])

    emit_prologue(0)
    for tbk in range(4):
        pD_a2a, bD_a2a = pp(f"pD_a2a{tbk}", [128, 1024])
        pD_a2b, bD_a2b = pp(f"pD_a2b{tbk}", [128, 1024])
        E("Y", lambda e: e.dma_start(
            fsr_sb[:, 0:FS_COLS], s_fs[0]), reads=[r_sfs], writes=[r_fsr[0]], dma=True)
        E("Y", lambda e: e.dma_start(
            a2r_sb[:, 0:256], s_fs[0][:, FS_A2:FS_COLS]),
          reads=[r_sfs], writes=[r_a2r[0]], dma=True)
        for fc in range(NFC):
            fslot = fc % 2
            fsl = fsr_sb[:, fslot * FS_COLS:(fslot + 1) * FS_COLS]
            rfs = r_fsr[fslot]
            if fc + 1 < NFC:
                nslot = (fc + 1) % 2
                E("Y", lambda e, fc=fc, ns=nslot: e.dma_start(
                    fsr_sb[:, ns * FS_COLS:(ns + 1) * FS_COLS], s_fs[fc + 1]),
                  reads=[r_sfs], writes=[r_fsr[nslot]], dma=True)
                E("Y", lambda e, fc=fc, ns=(fc + 1) % 3: e.dma_start(
                    a2r_sb[:, ns * 256:(ns + 1) * 256], s_fs[fc + 1][:, FS_A2:FS_COLS]),
                  reads=[r_sfs], writes=[r_a2r[(fc + 1) % 3]], dma=True)
            pv1, bv1 = rot_next()

            def f_x1(e, fsl=fsl, pv1=pv1):
                for i in range(16):
                    e.matmul(pv1, fsl[:, FS_W1 + i * 128:FS_W1 + i * 128 + 128],
                             xnf_sb[:, i * 512:(i + 1) * 512],
                             start=(i == 0), stop=False)
                for j in range(2):
                    ins = e.matmul(pv1, fsl[:, FS_B1 + j * 128:FS_B1 + j * 128 + 128],
                                   ash_sb[:, (0 * 2 + j) * 512:(0 * 2 + j) * 512 + 512],
                                   start=False, stop=False)
                return ins
            E("T", f_x1, reads=[rfs, r_xnf, r_ash], writes=[bv1])
            pv3, bv3 = rot_next()

            def f_x3(e, fsl=fsl, pv3=pv3):
                for i in range(16):
                    e.matmul(pv3, fsl[:, FS_W3 + i * 128:FS_W3 + i * 128 + 128],
                             xnf_sb[:, i * 512:(i + 1) * 512],
                             start=(i == 0), stop=False)
                for j in range(2):
                    ins = e.matmul(pv3, fsl[:, FS_B3 + j * 128:FS_B3 + j * 128 + 128],
                                   ash_sb[:, ((2 + 0) * 2 + j) * 512:((2 + 0) * 2 + j) * 512 + 512],
                                   start=False, stop=False)
                return ins
            E("T", f_x3, reads=[rfs, r_xnf, r_ash], writes=[bv3])
            for k in range(2):
                if k == 1:
                    def f_x1d(e, fsl=fsl, pv1=pv1):
                        for j in range(2):
                            ins = e.matmul(pv1, fsl[:, FS_B1 + j * 128:FS_B1 + j * 128 + 128],
                                           ash_sb[:, (1 * 2 + j) * 512:(1 * 2 + j) * 512 + 512],
                                           start=False, stop=(j == 1))
                        return ins
                    E("T", f_x1d, reads=[rfs, r_ash], writes=[bv1])

                    def f_x3d(e, fsl=fsl, pv3=pv3):
                        for j in range(2):
                            ins = e.matmul(pv3, fsl[:, FS_B3 + j * 128:FS_B3 + j * 128 + 128],
                                           ash_sb[:, ((2 + 1) * 2 + j) * 512:((2 + 1) * 2 + j) * 512 + 512],
                                           start=False, stop=(j == 1))
                        return ins
                    E("T", f_x3d, reads=[rfs, r_ash], writes=[bv3])
                fpar = fc % 2
                zsl = z_sb[:, k * 512:(k + 1) * 512]
                rz = r_z[k]
                E("S", lambda e, pv1=pv1, z=zsl: e.activation(z, pv1, AF.Silu),
                  reads=[bv1], writes=[rz])
                E("V", lambda e, z=zsl, pv3=pv3: e.tensor_tensor(
                    x2_sb[:], z, pv3, ALU.mult), reads=[rz, bv3], writes=[r_x2])
                ysl = y01_sb[:, (fpar * 2 + k) * 512:(fpar * 2 + k) * 512 + 512]
                ry = r_y01[fpar * 2 + k]
                E("V", lambda e, k=k, y=ysl: e.tensor_tensor(
                    y, x2_sb[:], rwb_sb[:, k * 512:(k + 1) * 512], ALU.mult),
                  reads=[r_x2, r_rwb], writes=[ry])
                if fc > 0:
                    ppar = (fc - 1) % 2
                    ypsl = y01_sb[:, (ppar * 2 + k) * 512:(ppar * 2 + k) * 512 + 512]
                    rpy = r_y01[ppar * 2 + k]
                    a2b = (bD_a2a if k == 0 else bD_a2b)
                    a2t = (pD_a2a if k == 0 else pD_a2b)
                    pslot = (fc - 1) % 3
                    pa2 = a2r_sb[:, pslot * 256:(pslot + 1) * 256]
                    prfs = r_a2r[pslot]

                    def f_a2(e, pa2=pa2, y=ypsl, a2t=a2t, fc=fc):
                        for j in range(2):
                            ins = e.matmul(a2t[:, j * 512:(j + 1) * 512],
                                           pa2[:, j * 128:(j + 1) * 128],
                                           y, start=(fc - 1 == 0), stop=False)
                        return ins
                    E("T", f_a2, reads=[prfs, rpy], writes=list(a2b))
            E("V", lambda e, fc=fc, fpar=fpar: e.tensor_tensor(
                y_sb[:, fc * 512:(fc + 1) * 512],
                y01_sb[:, fpar * 1024:fpar * 1024 + 512],
                y01_sb[:, fpar * 1024 + 512:fpar * 1024 + 1024], ALU.add),
              reads=[r_y01[fpar * 2], r_y01[fpar * 2 + 1]], writes=[r_y])

        for k in range(2):
            lpar = (NFC - 1) % 2
            lslot = (NFC - 1) % 3
            ypsl = y01_sb[:, (lpar * 2 + k) * 512:(lpar * 2 + k) * 512 + 512]
            rpy = r_y01[lpar * 2 + k]
            a2b = (bD_a2a if k == 0 else bD_a2b)
            a2t = (pD_a2a if k == 0 else pD_a2b)
            la2 = a2r_sb[:, lslot * 256:(lslot + 1) * 256]

            def f_a2f(e, la2=la2, y=ypsl, a2t=a2t):
                for j in range(2):
                    ins = e.matmul(a2t[:, j * 512:(j + 1) * 512],
                                   la2[:, j * 128:(j + 1) * 128],
                                   y, start=False, stop=True)
                return ins
            E("T", f_a2f, reads=[r_a2r[lslot], rpy], writes=list(a2b))
        for k in range(2):
            a2t = (pD_a2a if k == 0 else pD_a2b)
            a2b = (bD_a2a if k == 0 else bD_a2b)
            for j in range(2):
                E("V", lambda e, k=k, j=j, a2t=a2t: e.tensor_tensor(
                    a2e_sb[:, (k * 2 + j) * 512:(k * 2 + j) * 512 + 512],
                    a2t[:, j * 512:(j + 1) * 512],
                    m256_sb[:, (k * 2 + j) * 512:(k * 2 + j) * 512 + 512],
                    ALU.mult),
                  reads=[a2b[j], r_m256], writes=[r_a2e])
        if tbk == 0 and "ffn" in dump:
            rd0 = R()
            E("S", lambda e: e.dma_start(dmp["lgE"][:], lgE_sb[:]), reads=[r_lgE],
              writes=[rd0], dma=True)
            E("S", lambda e: e.dma_start(dmp["g01"][:], g01_sb[:]), reads=[r_g01],
              writes=[rd0], dma=True)
            E("S", lambda e: e.dma_start(dmp["rwp"][:], rwp_sb[:]), reads=[r_rwp],
              writes=[rd0], dma=True)
            E("S", lambda e: e.dma_start(dmp["rws"][:], rws_sb[:]), reads=[r_rws],
              writes=[rd0], dma=True)
            E("S", lambda e: e.dma_start(dmp["xnf"][:], xnf_sb[:]), reads=[r_xnf],
              writes=[rd0], dma=True)
            E("S", lambda e: e.dma_start(dmp["maskf"][:], maskf_sb[:]), reads=[r_maskf],
              writes=[rd0], dma=True)
            E("S", lambda e: e.dma_start(dmp["rwb"][:], rwb_sb[:]), reads=[r_rwb],
              writes=[rd0], dma=True)
            E("S", lambda e: e.dma_start(dmp["ash"][:], ash_sb[:]), reads=[r_ash],
              writes=[rd0], dma=True)
            E("S", lambda e: e.dma_start(dmp["y"][:], y_sb[:]), reads=[r_y],
              writes=[rd0], dma=True)
            E("S", lambda e: e.dma_start(dmp["a2e"][:], a2e_sb[:]), reads=[r_a2e],
              writes=[rd0], dma=True)
        free(f"pD_a2a{tbk}")
        if tbk + 1 < 4:
            emit_prologue(tbk + 1)
        pD_dn, bD_dn = pp(f"pD_dn{tbk}", [128, 2048])
        for ob in range(4):
            for bi in range(NFC // 2):
                wslot = bi % 2
                w2t = w2r_sb[:, wslot * 1024:(wslot + 1) * 1024]
                rw2 = r_w2r[wslot]
                E("Y", lambda e, ob=ob, bi=bi, w=w2t: e.dma_start(
                    w.rearrange("p (a c) -> p a c", a=2),
                    s_w2[:, ob, bi * 2:(bi + 1) * 2, :]),
                  reads=[r_sw2], writes=[rw2], dma=True)

                def f_dn(e, bi=bi, w=w2t, pd=pD_dn):
                    for f2 in range(2):
                        fc = bi * 2 + f2
                        for tc in range(4):
                            ins = e.matmul(
                                pd[:, tc * 512:(tc + 1) * 512],
                                y_sb[:, fc * 512 + tc * 128:fc * 512 + tc * 128 + 128],
                                w[:, f2 * 512:(f2 + 1) * 512],
                                start=(bi == 0 and f2 == 0), stop=False)
                    return ins
                E("T", f_dn, reads=[rw2, r_y], writes=list(bD_dn))

            def f_l2(e, ob=ob, pd=pD_dn):
                for tc in range(4):
                    for k in range(2):
                        for j in range(2):
                            ins = e.matmul(
                                pd[:, tc * 512:(tc + 1) * 512],
                                a2e_sb[:, (k * 2 + j) * 512 + tc * 128:(k * 2 + j) * 512 + tc * 128 + 128],
                                b2s_sb[:, j * D + ob * 512:j * D + ob * 512 + 512],
                                start=False, stop=(k == 1 and j == 1))
                return ins
            E("T", f_l2, reads=[r_a2e, rconst], writes=list(bD_dn))
            for tc in range(4):
                E("S", lambda e, tc=tc, ob=ob, pd=pD_dn: e.activation(
                    fout_sb[:, tc * 2048 + ob * 512:tc * 2048 + ob * 512 + 512],
                    pd[:, tc * 512:(tc + 1) * 512], AF.Copy),
                  reads=[bD_dn[tc]], writes=[r_fout[tc]])
        for tc in range(4):
            qc = tbk * 4 + tc
            fsl = fout_sb[:, tc * 2048:(tc + 1) * 2048]
            E("V", lambda e, f=fsl, tc=tc: e.scalar_tensor_tensor(
                scrD[:], f, 1.0, f, ALU.mult, ALU.mult,
                accum_out=sqd_sb[:, tc:tc + 1]),
              reads=[r_fout[tc]], writes=[r_scrD, r_sqd[tc]], haz=True)
            scol = stripD[:, 4:5]
            icol = stripD[:, 5:6]
            ncol = stripD[:, 6:7]
            E("V", lambda e, f=fsl, scol=scol: e.tensor_reduce(scol, f, AX.X, ALU.add),
              reads=[r_fout[tc]], writes=[r_stripD], haz=True)
            ln_tm_strips(scol, sqd_sb[:, tc:tc + 1], icol, ncol, r_stripD,
                         [r_sqd[tc]])
            oslot = tc % 2
            obuf = ob_sb[:, oslot * 2048:(oslot + 1) * 2048]
            ro = r_ob[oslot]
            xslot = tc % 2
            x1t = x1t_sb[:, xslot * 2048:(xslot + 1) * 2048]
            rx1 = r_x1t[xslot]
            E("Y", lambda e, qc=qc, xx=x1t: e.dma_start(
                xx, x1d[qc * 128:(qc + 1) * 128, :]), reads=[r_x1d], writes=[rx1],
              dma=True)
            E("V", lambda e, f=fsl, o=obuf, icol=icol, ncol=ncol: e.tensor_scalar(
                o, f, icol, ncol, ALU.mult, ALU.add),
              reads=[r_fout[tc], r_stripD], writes=[ro])
            E("V", lambda e, o=obuf, xx=x1t: e.tensor_tensor(o, o, xx, ALU.add),
              reads=[rx1], writes=[ro])
            E("S", lambda e, o=obuf, qc=qc: e.dma_start(
                out[qc * 128:(qc + 1) * 128, :], o), reads=[ro], dma=True)
        if tbk == 0 and "ffn" in dump:
            rd1 = R()
            E("S", lambda e: e.dma_start(dmp["fout"][:], fout_sb[:]),
              reads=r_fout, writes=[rd1], dma=True)
        free(f"pD_dn{tbk}")

    sch.tail()
    with nc.Block() as block:
        sch.finalize(block)
    free_all = list(stack)
    while stack:
        stack.pop()[1].__exit__(None, None, None)
    sch.close()
    return nc


def kernel(**inputs):
    maps = _host_inputs(inputs)
    nc = build_nc()
    res = run_bass_kernel_spmd(nc, maps, core_ids=list(range(8)))
    x = np.asarray(inputs["L0_attn_x"])
    outp = np.empty((4, SEQ, D), dtype=np.float32)
    for c in range(8):
        rows = _own_rows(c % 2)
        outp[c // 2][rows] = res.results[c]["out"]
    return outp


# version 08
# speedup vs baseline: 1.3844x; 1.0786x over previous
import math
import numpy as np
import ml_dtypes
import concourse.bass as bass
import concourse.mybir as mybir
from concourse.bass_utils import run_bass_kernel_spmd

F32 = mybir.dt.float32
BF16 = mybir.dt.bfloat16
AF = mybir.ActivationFunctionType
ALU = mybir.AluOpType
AX = mybir.AxisListType

D = 2048
DL = 512
NH = 16
SEQ = 4096
FF = 7168
ER = 256
OWN = 2048
QT = 512
NQT = 4
KCS = (8, 16, 24, 32)
NFC = FF // 128
SCALE = 1.0 / math.sqrt(192.0)
EBIAS = -80.0 * SCALE
PLO = math.exp(2.0 * EBIAS)
THETA = 10000.0
EPS = 1e-5

POS_EVEN = (0, 3, 4, 7)
POS_ODD = (1, 2, 5, 6)

FS_W1 = 0
FS_W3 = 2048
FS_B1 = 4096
FS_B3 = 4096 + 256
FS_A2 = 4096 + 512
FS_COLS = 4096 + 768


def _own_rows(parity):
    pos = POS_EVEN if parity == 0 else POS_ODD
    r = []
    for p in pos:
        r.extend(range(p * QT, (p + 1) * QT))
    return np.array(r, dtype=np.int64)


def _lhsT4(wt, mj=128):
    din, dout = wt.shape
    ni, nj = din // 128, dout // mj
    return np.ascontiguousarray(wt.reshape(ni, 128, nj, mj).transpose(1, 0, 2, 3))


def _rope_perm():
    return np.concatenate([np.arange(0, 64, 2), np.arange(1, 64, 2)])


def _rope_tables(positions):
    inv = THETA ** (-np.arange(0, 64, 2, dtype=np.float64) / 64.0)
    ang = positions.astype(np.float64)[None, :] * inv[:, None]
    cos = np.cos(ang)
    sin = np.sin(ang)
    cos2 = np.concatenate([cos, cos], 0).astype(np.float32)
    sin2 = np.concatenate([-sin, sin], 0).astype(np.float32)
    return np.ascontiguousarray(cos2), np.ascontiguousarray(sin2)


def _host_inputs(inp):
    x = np.asarray(inp["L0_attn_x"], dtype=np.float32)
    perm = _rope_perm()
    sh = {}
    sh["wdq"] = _lhsT4(np.ascontiguousarray(np.asarray(inp["L0_attn_W_DQ"]).T))
    sh["wdkv"] = _lhsT4(np.ascontiguousarray(np.asarray(inp["L0_attn_W_DKV"]).T))
    sh["wkr"] = _lhsT4(np.ascontiguousarray(np.asarray(inp["L0_attn_W_KR"]).T[:, perm]), mj=64)
    sh["wuq"] = _lhsT4(np.ascontiguousarray(np.asarray(inp["L0_attn_W_UQ"]).T))
    wqrt = np.asarray(inp["L0_attn_W_QR"]).T.reshape(DL, NH, 64)[:, :, perm]
    sh["wqr"] = _lhsT4(np.ascontiguousarray(wqrt.reshape(DL, NH * 64)), mj=64)
    sh["wuk"] = _lhsT4(np.ascontiguousarray(np.asarray(inp["L0_attn_W_UK"]).T))
    sh["wuv"] = _lhsT4(np.ascontiguousarray(np.asarray(inp["L0_attn_W_UV"]).T))
    wot = np.asarray(inp["L0_attn_W_O"]).T
    sh["wo"] = np.ascontiguousarray(wot.reshape(NH, 128, 4, 512).transpose(1, 2, 0, 3))
    sh["wg"] = np.ascontiguousarray(
        np.asarray(inp["L0_ffn_Wg"]).T.reshape(16, 128, 8).transpose(1, 0, 2))

    w1t = np.asarray(inp["L0_ffn_W1"]).T
    w3t = np.asarray(inp["L0_ffn_W3"]).T
    b1st = np.asarray(inp["L0_ffn_B1"]).transpose(0, 2, 1).reshape(ER, FF)
    b3st = np.asarray(inp["L0_ffn_B3"]).transpose(0, 2, 1).reshape(ER, FF)
    a2st = np.ascontiguousarray(np.asarray(inp["L0_ffn_A2"]).reshape(ER, FF).T)
    fs = np.empty((NFC, 128, FS_COLS), dtype=np.float32)
    fs[:, :, FS_W1:FS_W1 + 2048] = w1t.reshape(16, 128, NFC, 128).transpose(2, 1, 0, 3).reshape(NFC, 128, 2048)
    fs[:, :, FS_W3:FS_W3 + 2048] = w3t.reshape(16, 128, NFC, 128).transpose(2, 1, 0, 3).reshape(NFC, 128, 2048)
    fs[:, :, FS_B1:FS_B1 + 256] = b1st.reshape(2, 128, NFC, 128).transpose(2, 1, 0, 3).reshape(NFC, 128, 256)
    fs[:, :, FS_B3:FS_B3 + 256] = b3st.reshape(2, 128, NFC, 128).transpose(2, 1, 0, 3).reshape(NFC, 128, 256)
    fs[:, :, FS_A2:FS_A2 + 256] = a2st.reshape(NFC, 128, 256)
    sh["fstream"] = fs
    sh["a1s"] = _lhsT4(np.ascontiguousarray(np.asarray(inp["L0_ffn_A1"]).reshape(ER, D).T))
    sh["a3s"] = _lhsT4(np.ascontiguousarray(np.asarray(inp["L0_ffn_A3"]).reshape(ER, D).T))
    b2st = np.asarray(inp["L0_ffn_B2"]).transpose(0, 2, 1).reshape(ER, D)
    sh["b2s"] = np.ascontiguousarray(b2st.reshape(2, 128, D).transpose(1, 0, 2))
    w2t = np.asarray(inp["L0_ffn_W2"]).T
    sh["w2"] = np.ascontiguousarray(w2t.reshape(NFC, 128, 4, 512).transpose(1, 2, 0, 3))

    sh["ident"] = np.eye(128, dtype=ml_dtypes.bfloat16)
    sh["ident32"] = np.eye(128, dtype=np.float32)
    sh["ones_col"] = np.ones((128, 1), dtype=ml_dtypes.bfloat16)
    sh["ones_row"] = np.ones((1, 128), dtype=ml_dtypes.bfloat16)
    sh["ones_row32"] = np.ones((1, 128), dtype=np.float32)
    sh["ebias"] = np.full((128, 1), EBIAS, dtype=np.float32)
    psm = np.zeros((64, 64), dtype=ml_dtypes.bfloat16)
    for i in range(32):
        psm[i, 32 + i] = 1
        psm[32 + i, i] = 1
    sh["ps64"] = psm
    e8 = np.zeros((8, 256), dtype=ml_dtypes.bfloat16)
    for e in range(8):
        e8[e, e * 32:(e + 1) * 32] = 1
    sh["e8"] = e8
    sh["kpos"] = np.ascontiguousarray(
        (np.arange(128)[:, None] + 128 * np.arange(32)[None, :]).astype(np.float32))
    kc2, ks2 = _rope_tables(np.arange(SEQ))
    sh["kcos"] = kc2
    sh["ksin"] = ks2

    maps = []
    for c in range(8):
        b, par = c // 2, c % 2
        rows = _own_rows(par)
        m = dict(sh)
        xb = x[b]
        xt = np.ascontiguousarray(xb.T)
        m["xT"] = np.ascontiguousarray(xt.reshape(16, 128, SEQ).transpose(1, 0, 2))
        xop = np.ascontiguousarray(xb[rows])
        m["xo"] = xop
        m["xTo"] = np.ascontiguousarray(xop.T.reshape(16, 128, OWN).transpose(1, 0, 2))
        qc2, qs2 = _rope_tables(rows)
        m["qcos"] = qc2
        m["qsin"] = qs2
        m["qposb"] = np.ascontiguousarray(
            np.broadcast_to(rows.astype(np.float32)[None, :], (128, OWN)).copy())
        maps.append(m)
    return maps


class Res:
    __slots__ = ("ready", "users")

    def __init__(self):
        self.ready = None
        self.users = []


class Sch:
    ENGS = ("T", "V", "S", "P", "Y")

    def __init__(self, nc):
        self.nc = nc
        self.q = {e: [] for e in self.ENGS}
        self.sem = {}
        self.cnt = {}
        self.hiw = {e: {} for e in self.ENGS}
        self._ctx = []
        for name in ("sT", "sV", "sS", "dY", "dS", "dP"):
            cm = nc.semaphore(name)
            self.sem[name] = cm.__enter__()
            self._ctx.append(cm)
            self.cnt[name] = 0
        self.esem = {"T": "sT", "V": "sV", "S": "sS"}
        self.qsem = {"P": "dP", "Y": "dY", "S": "dS"}
        self.trace = []

    def _wait(self, eng, toks):
        best = {}
        own = self.esem.get(eng)
        for t in toks:
            if t is None:
                continue
            s, v = t
            if s == own:
                continue
            if best.get(s, -1) < v:
                best[s] = v
        for s, v in best.items():
            if self.hiw[eng].get(s, 0) >= v:
                continue
            self.hiw[eng][s] = v
            h = self.sem[s]
            self.trace.append((eng, "wait", s, v))
            self.q[eng].append(lambda e, h=h, v=v: e.wait_ge(h, v))

    def emit(self, eng, fn, reads=(), writes=(), dma=False, ndma=1, haz=False, tag=""):
        toks = []
        for r in reads:
            if r is not None and r.ready is not None:
                toks.append(r.ready)
        for w in writes:
            toks.extend(w.users)
            if w.ready is not None:
                toks.append(w.ready)
        self._wait(eng, toks)
        sname = self.qsem[eng] if dma else self.esem[eng]
        amt = 16 * ndma if dma else 1
        self.cnt[sname] += amt
        tok = (sname, self.cnt[sname])
        h = self.sem[sname]
        inc = 16 if dma else 1
        if haz:
            vna = self.vnop_ap
            def fn2(e, fn=fn, vna=vna):
                fn(e)
                return e.tensor_scalar(vna, vna, 1.0, None, ALU.mult)
            fn = fn2
        self.trace.append((eng, "op:" + tag, sname, tok[1]))
        self.q[eng].append(lambda e, fn=fn, h=h, a=inc: fn(e).then_inc(h, a))
        for r in reads:
            if r is not None:
                r.users.append(tok)
        for w in writes:
            w.ready = tok
            w.users = []
        return tok

    def gate(self):
        toks = [(s, self.cnt[s]) for s in ("sT", "sV", "sS", "dS")]
        for e in self.ENGS:
            self._wait(e, toks)

    def tail(self):
        toks = [(s, self.cnt[s]) for s in self.cnt]
        self._wait("Y", toks)

    def finalize(self, block):
        for attr, eng in (("tensor", "T"), ("vector", "V"), ("scalar", "S"),
                          ("gpsimd", "P"), ("sync", "Y")):
            ops = self.q[eng]

            def body(e, ops=ops):
                for op in ops:
                    op(e)
            getattr(block, attr)(body)

    def close(self):
        for cm in reversed(self._ctx):
            cm.__exit__(None, None, None)


def build_nc(dump=()):
    nc = bass.Bass("TRN2", target_bir_lowering=False)

    def din(name, shape, dtype=F32):
        return nc.dram_tensor(name, list(shape), dtype, kind="ExternalInput")

    xT = din("xT", [128, 16, SEQ])
    xTo = din("xTo", [128, 16, OWN])
    xo = din("xo", [OWN, D])
    wdq = din("wdq", [128, 16, 4, 128])
    wdkv = din("wdkv", [128, 16, 4, 128])
    wkr = din("wkr", [128, 16, 1, 64])
    wuq = din("wuq", [128, 4, 16, 128])
    wqr = din("wqr", [128, 4, 16, 64])
    wuk = din("wuk", [128, 4, 16, 128])
    wuv = din("wuv", [128, 4, 16, 128])
    wo = din("wo", [128, 4, NH, 512])
    wg = din("wg", [128, 16, 8])
    fstream = din("fstream", [NFC, 128, FS_COLS])
    a1s = din("a1s", [128, 16, 2, 128])
    a3s = din("a3s", [128, 16, 2, 128])
    b2s = din("b2s", [128, 2, D])
    w2 = din("w2", [128, 4, NFC, 512])
    ident = din("ident", [128, 128], BF16)
    ident32 = din("ident32", [128, 128], F32)
    ones_col = din("ones_col", [128, 1], BF16)
    ones_row = din("ones_row", [1, 128], BF16)
    ones_row32 = din("ones_row32", [1, 128], F32)
    ebias = din("ebias", [128, 1], F32)
    ps64 = din("ps64", [64, 64], BF16)
    e8 = din("e8", [8, 256], BF16)
    kpos = din("kpos", [128, 32])
    kcos = din("kcos", [64, SEQ])
    ksin = din("ksin", [64, SEQ])
    qcos = din("qcos", [64, OWN])
    qsin = din("qsin", [64, OWN])
    qposb = din("qposb", [128, OWN])

    out = nc.dram_tensor("out", [OWN, D], F32, kind="ExternalOutput")
    x1d = nc.dram_tensor("x1d", [OWN, D], F32,
                         kind="ExternalOutput" if "x1" in dump else "Internal")
    attnd = nc.dram_tensor("attnd", [128, NH * OWN], BF16,
                           kind="ExternalOutput" if "attn" in dump else "Internal")
    s_wo = nc.dram_tensor("s_wo", [128, 4, NH, 512], BF16)
    s_fs = nc.dram_tensor("s_fs", [NFC, 128, FS_COLS], BF16)
    s_w2 = nc.dram_tensor("s_w2", [128, 4, NFC, 512], BF16)
    s_a1s = nc.dram_tensor("s_a1s", [128, 16, 2, 128], BF16)
    s_a3s = nc.dram_tensor("s_a3s", [128, 16, 2, 128], BF16)

    dmp = {}
    if "ffn" in dump:
        dmp["xnf"] = nc.dram_tensor("d_xnf", [128, 16 * 512], BF16, kind="ExternalOutput")
        dmp["maskf"] = nc.dram_tensor("d_maskf", [8, 1024], BF16, kind="ExternalOutput")
        dmp["rwb"] = nc.dram_tensor("d_rwb", [128, 1024], F32, kind="ExternalOutput")
        dmp["ash"] = nc.dram_tensor("d_ash", [128, 8 * 512], BF16, kind="ExternalOutput")
        dmp["y"] = nc.dram_tensor("d_y", [128, NFC * 512], BF16, kind="ExternalOutput")
        dmp["a2e"] = nc.dram_tensor("d_a2e", [128, 4 * 512], BF16, kind="ExternalOutput")
        dmp["fout"] = nc.dram_tensor("d_fout", [128, 4 * 2048], BF16, kind="ExternalOutput")
        dmp["lgE"] = nc.dram_tensor("d_lgE", [128, 32], F32, kind="ExternalOutput")
        dmp["g01"] = nc.dram_tensor("d_g01", [128, 16], BF16, kind="ExternalOutput")
        dmp["rwp"] = nc.dram_tensor("d_rwp", [128, 2], F32, kind="ExternalOutput")
        dmp["rws"] = nc.dram_tensor("d_rws", [1, 1024], F32, kind="ExternalOutput")
    if "attno" in dump:
        dmp["attno"] = nc.dram_tensor("d_attno", [128, 4 * D], F32, kind="ExternalOutput")
        dmp["strip"] = nc.dram_tensor("d_strip", [128, 8], F32, kind="ExternalOutput")
        dmp["sums"] = nc.dram_tensor("d_sums", [128, 16], F32, kind="ExternalOutput")
        dmp["sqc"] = nc.dram_tensor("d_sqc", [128, 4], F32, kind="ExternalOutput")
    if "ckv" in dump:
        dmp["ckv"] = nc.dram_tensor("d_ckv", [128, 4 * SEQ], BF16, kind="ExternalOutput")
        dmp["cq"] = nc.dram_tensor("d_cq", [128, 4 * OWN], BF16, kind="ExternalOutput")
        dmp["kr"] = nc.dram_tensor("d_kr", [64, SEQ], BF16, kind="ExternalOutput")

    sch = Sch(nc)
    E = sch.emit
    stack = []

    def sb(name, shape, dtype):
        cm = nc.sbuf_tensor(name, list(shape), dtype)
        t = cm.__enter__()
        stack.append((name, cm))
        return t

    nbank_res = [Res() for _ in range(8)]

    def pp(name, shape, dtype=F32):
        b0 = nc.psum_base
        cm = nc.psum_tensor(name, list(shape), dtype)
        t = cm.__enter__()
        b1 = nc.psum_base
        stack.append((name, cm))
        return t, nbank_res[b0:b1]

    def free(upto):
        while stack:
            name, cm = stack.pop()
            cm.__exit__(None, None, None)
            if name == upto:
                return
        raise RuntimeError(f"free: {upto} not found")

    R = Res

    vnop_sb = sb("vnop_sb", [128, 2], F32)
    sch.vnop_ap = vnop_sb[:, 0:2]
    ident_sb = sb("ident_sb", [128, 128], BF16)
    id32_sb = sb("id32_sb", [128, 128], F32)
    ones_c = sb("ones_c", [128, 1], BF16)
    ones_r = sb("ones_r", [1, 128], BF16)
    ones_r32 = sb("ones_r32", [1, 128], F32)
    ebias_sb = sb("ebias_sb", [128, 1], F32)
    ps64_sb = sb("ps64_sb", [64, 64], BF16)
    e8_sb = sb("e8_sb", [8, 256], BF16)
    kpos_sb = sb("kpos_sb", [128, 32], F32)
    wg_sb = sb("wg_sb", [128, 16 * 8], BF16)
    b2s_sb = sb("b2s_sb", [128, 2 * D], BF16)
    rconst = R()
    for dst, src in ((ident_sb, ident), (id32_sb, ident32), (ones_c, ones_col),
                     (ones_r, ones_row), (ones_r32, ones_row32), (ebias_sb, ebias),
                     (ps64_sb, ps64), (e8_sb, e8), (kpos_sb, kpos)):
        E("Y", lambda e, d=dst, s=src: e.dma_start(d[:], s[:]), writes=[rconst], dma=True)
    E("P", lambda e: e.dma_start(wg_sb[:].rearrange("p (a b) -> p a b", a=16), wg[:]),
      writes=[rconst], dma=True)
    E("P", lambda e: e.dma_start(b2s_sb[:].rearrange("p (a b) -> p a b", a=2), b2s[:]),
      writes=[rconst], dma=True)

    r_swo, r_sfs, r_sw2, r_sa13 = R(), R(), R(), R()
    cast_jobs = [(s_wo, wo, r_swo), (s_a1s, a1s, r_sa13), (s_a3s, a3s, r_sa13)]
    for i in range(0, NFC, 7):
        cast_jobs.append((s_fs[i:i + 7], fstream[i:i + 7], r_sfs))
    for i in range(4):
        cast_jobs.append((s_w2[:, i], w2[:, i], r_sw2))

    def emit_casts(k=1):
        for _ in range(k):
            if not cast_jobs:
                return
            dst, src, res = cast_jobs.pop(0)
            E("P", lambda e, d=dst, s=src: e.dma_start(d[:], s[:]), writes=[res], dma=True)

    ckv_sb = sb("ckv_sb", [128, 4 * SEQ], BF16)
    cq_sb = sb("cq_sb", [128, 4 * OWN], BF16)
    kr_sb = sb("kr_sb", [128, SEQ], BF16)
    r_ckv, r_cq, r_kr = R(), R(), R()
    wdq_sb = sb("wdq_sb", [128, 16 * 4 * 128], BF16)
    wdkv_sb = sb("wdkv_sb", [128, 16 * 4 * 128], BF16)
    wkr_sb = sb("wkr_sb", [128, 16 * 64], BF16)
    r_wdq, r_wdkv, r_wkr = R(), R(), R()
    E("P", lambda e: e.dma_start(
        wdq_sb[:].rearrange("p (a b c) -> p a b c", a=16, b=4), wdq[:]),
      writes=[r_wdq], dma=True)
    E("P", lambda e: e.dma_start(
        wdkv_sb[:].rearrange("p (a b c) -> p a b c", a=16, b=4), wdkv[:]),
      writes=[r_wdkv], dma=True)
    E("P", lambda e: e.dma_start(
        wkr_sb[:].rearrange("p (a c) -> p a c", a=16),
        wkr[:].rearrange("p a b c -> p a (b c)")),
      writes=[r_wkr], dma=True)
    kcos_sb = sb("kcos_sb", [64, SEQ], F32)
    ksin_sb = sb("ksin_sb", [64, SEQ], F32)
    r_ktab = R()
    E("Y", lambda e: e.dma_start(kcos_sb[:], kcos[:]), writes=[r_ktab], dma=True)
    E("Y", lambda e: e.dma_start(ksin_sb[:], ksin[:]), writes=[r_ktab], dma=True)

    xb_sb = sb("xb_sb", [128, 3 * 16 * 512], BF16)
    r_xb = [R(), R(), R()]
    xsq_sb = sb("xsq_sb", [128, 2048], BF16)
    r_xsq = R()
    stripA = sb("stripA", [1, 4 * 512], F32)
    r_stripA = R()
    mb_sb = sb("mb_sb", [128, 1024], F32)
    r_mb = R()
    kraw_sb = sb("kraw_sb", [64, 512], BF16)
    r_kraw = R()
    kt_sb = sb("kt_sb", [64, 1024], F32)
    r_kt = R()

    E("V", lambda e: e.memset(kr_sb[64:128, :], 0.0), writes=[r_kr])
    pA_sum, bA_sum = pp("pA_sum", [1, 512])
    pA_sq, bA_sq = pp("pA_sq", [1, 512])
    pA_bc, bA_bc = pp("pA_bc", [128, 1024])
    pA_p1, bA_p1 = pp("pA_p1", [128, 1024])
    pA_p2, bA_p2 = pp("pA_p2", [128, 1024])
    bA_sum, bA_sq = bA_sum[0], bA_sq[0]

    def ln_fm_stats(xs, rx):
        def f_sum(e, x=xs):
            for i in range(16):
                ins = e.matmul(pA_sum[:], ones_c[:], x[:, i * 512:(i + 1) * 512],
                               start=(i == 0), stop=(i == 15))
            return ins
        E("T", f_sum, reads=[rx, rconst], writes=[bA_sum])
        for g in range(4):
            E("V", lambda e, x=xs, g=g: e.tensor_tensor(
                xsq_sb[:], x[:, g * 2048:(g + 1) * 2048],
                x[:, g * 2048:(g + 1) * 2048], ALU.mult),
              reads=[rx], writes=[r_xsq])

            def f_sq(e, g=g):
                for i in range(4):
                    ins = e.matmul(pA_sq[:], ones_c[:], xsq_sb[:, i * 512:(i + 1) * 512],
                                   start=(g == 0 and i == 0), stop=(g == 3 and i == 3))
                return ins
            E("T", f_sq, reads=[r_xsq, rconst], writes=[bA_sq] if g == 3 else [])
        mS = stripA[:, 0:512]
        vS = stripA[:, 512:1024]
        tS = stripA[:, 1024:1536]
        E("V", lambda e: e.tensor_scalar(mS, pA_sum[:], 1.0 / D, None, ALU.mult),
          reads=[bA_sum], writes=[r_stripA])
        E("V", lambda e: e.tensor_scalar(tS, pA_sq[:], 1.0 / D, EPS, ALU.mult, ALU.add),
          reads=[bA_sq], writes=[r_stripA])
        E("V", lambda e: e.tensor_tensor(vS, mS, mS, ALU.mult), writes=[r_stripA])
        E("V", lambda e: e.tensor_tensor(vS, tS, vS, ALU.subtract), writes=[r_stripA])
        E("S", lambda e: e.activation(vS, vS, AF.Ln), reads=[r_stripA], writes=[r_stripA])
        E("S", lambda e: e.activation(vS, vS, AF.Exp, scale=-0.5),
          reads=[r_stripA], writes=[r_stripA])
        E("T", lambda e: e.matmul(pA_bc[:, 0:512], ones_r32[:], mS, start=True, stop=True),
          reads=[r_stripA, rconst], writes=[bA_bc[0]])
        E("T", lambda e: e.matmul(pA_bc[:, 512:1024], ones_r32[:], vS, start=True, stop=True),
          reads=[r_stripA], writes=[bA_bc[1]])
        E("V", lambda e: e.tensor_copy(mb_sb[:], pA_bc[:]),
          reads=bA_bc, writes=[r_mb])
        for i in range(16):
            E("V", lambda e, x=xs, i=i: e.tensor_tensor(
                x[:, i * 512:(i + 1) * 512], x[:, i * 512:(i + 1) * 512],
                mb_sb[:, 0:512], ALU.subtract), reads=[r_mb], writes=[rx])
        for i in range(16):
            E("V", lambda e, x=xs, i=i: e.tensor_tensor(
                x[:, i * 512:(i + 1) * 512], x[:, i * 512:(i + 1) * 512],
                mb_sb[:, 512:1024], ALU.mult), reads=[r_mb], writes=[rx])

    prjL = [(pA_p1[:, 0:512], bA_p1[0]), (pA_p1[:, 512:1024], bA_p1[1]),
            (pA_p2[:, 0:512], bA_p2[0]), (pA_p2[:, 512:1024], bA_p2[1])]

    def phaseA_load(idx, src):
        slot = idx % 3
        xs = xb_sb[:, slot * 8192:(slot + 1) * 8192]
        rx = r_xb[slot]
        E("P", lambda e, s=src, x=xs: e.dma_start(
            x.rearrange("p (i t) -> p i t", i=16), s), writes=[rx], dma=True)

    def phaseA_block(idx, tb, own):
        slot = idx % 3
        xs = xb_sb[:, slot * 8192:(slot + 1) * 8192]
        rx = r_xb[slot]
        ln_fm_stats(xs, rx)
        if own:
            wsb, rw, dst, rdst, blk = wdq_sb, r_wdq, cq_sb, r_cq, OWN
        else:
            wsb, rw, dst, rdst, blk = wdkv_sb, r_wdkv, ckv_sb, r_ckv, SEQ
        for j in range(4):
            pv, br = prjL[j]

            def f_prj(e, x=xs, w=wsb, j=j, pv=pv):
                for i in range(16):
                    ins = e.matmul(pv, w[:, (i * 4 + j) * 128:(i * 4 + j) * 128 + 128],
                                   x[:, i * 512:(i + 1) * 512],
                                   start=(i == 0), stop=(i == 15))
                return ins
            E("T", f_prj, reads=[rx, rw], writes=[br])
            E("V", lambda e, pv=pv, d=dst, j=j, tb=tb, blk=blk: e.tensor_copy(
                d[:, j * blk + tb * 512: j * blk + tb * 512 + 512], pv),
              reads=[br], writes=[rdst])
        if not own:
            pv, br = prjL[0]

            def f_kr(e, x=xs, pv=pv):
                pv64 = pv[0:64, :]
                for i in range(16):
                    ins = e.matmul(pv64, wkr_sb[:, i * 64:(i + 1) * 64],
                                   x[:, i * 512:(i + 1) * 512],
                                   start=(i == 0), stop=(i == 15))
                return ins
            E("T", f_kr, reads=[rx, r_wkr], writes=[br])
            E("V", lambda e, pv=pv: e.tensor_copy(kraw_sb[:], pv[0:64, :]),
              reads=[br], writes=[r_kraw])
            pv2, br2 = prjL[1]
            E("T", lambda e, pv2=pv2: e.matmul(pv2[0:64, :], ps64_sb[:], kraw_sb[:],
                                               start=True, stop=True),
              reads=[r_kraw, rconst], writes=[br2])
            t1 = kt_sb[:, 0:512]
            t2 = kt_sb[:, 512:1024]
            E("V", lambda e, tb=tb, t1=t1: e.tensor_tensor(
                t1, kraw_sb[:], kcos_sb[:, tb * 512:(tb + 1) * 512], ALU.mult),
              reads=[r_kraw, r_ktab], writes=[r_kt])
            E("V", lambda e, pv2=pv2, tb=tb, t2=t2: e.tensor_tensor(
                t2, pv2[0:64, :], ksin_sb[:, tb * 512:(tb + 1) * 512], ALU.mult),
              reads=[br2, r_ktab], writes=[r_kt])
            E("V", lambda e, tb=tb, t1=t1, t2=t2: e.tensor_tensor(
                kr_sb[0:64, tb * 512:(tb + 1) * 512], t1, t2, ALU.add),
              reads=[r_kt], writes=[r_kr])

    A_jobs = [(tb, xT[:, :, tb * 512:(tb + 1) * 512], False) for tb in range(8)]
    A_jobs += [(tb, xTo[:, :, tb * 512:(tb + 1) * 512], True) for tb in range(4)]
    phaseA_load(0, A_jobs[0][1])
    phaseA_load(1, A_jobs[1][1])
    for idx, (tb, src, own) in enumerate(A_jobs):
        if idx + 2 < len(A_jobs):
            phaseA_load(idx + 2, A_jobs[idx + 2][1])
        phaseA_block(idx, tb, own)

    if "ckv" in dump:
        rd = R()
        E("S", lambda e: e.dma_start(dmp["ckv"][:], ckv_sb[:]), reads=[r_ckv],
          writes=[rd], dma=True)
        E("S", lambda e: e.dma_start(dmp["cq"][:], cq_sb[:]), reads=[r_cq],
          writes=[rd], dma=True)
        E("S", lambda e: e.dma_start(dmp["kr"][:], kr_sb[:]), reads=[r_kr],
          writes=[rd], dma=True)

    free("wdq_sb")
    sch.gate()

    qcos_sb = sb("qcos_sb", [64, OWN], F32)
    qsin_sb = sb("qsin_sb", [64, OWN], F32)
    qpos_sb = sb("qpos_sb", [128, OWN], F32)
    r_qtab = R()
    E("Y", lambda e: e.dma_start(qcos_sb[:], qcos[:]), writes=[r_qtab], dma=True)
    E("Y", lambda e: e.dma_start(qsin_sb[:], qsin[:]), writes=[r_qtab], dma=True)
    E("Y", lambda e: e.dma_start(qpos_sb[:], qposb[:]), writes=[r_qtab], dma=True)

    WSL = 4 * 128 * 3 + 4 * 64
    wslf = sb("wslf", [128, WSL], F32)
    wslb = sb("wslb", [128, 2 * WSL], BF16)
    r_wslf = R()
    r_wslb = [R(), R()]
    khi_sb = sb("khi_sb", [128, SEQ], BF16)
    v_sb = sb("v_sb", [128, SEQ], BF16)
    qhi_sb = sb("qhi_sb", [128, OWN], BF16)
    qlo_sb = sb("qlo_sb", [128, OWN], BF16)
    qraw_sb = sb("qraw_sb", [64, OWN], BF16)
    qt_sb = sb("qt_sb", [64, 1024], F32)
    r_khi, r_v, r_qhi, r_qlo, r_qraw, r_qt = R(), R(), R(), R(), R(), R()
    p_sb = sb("p_sb", [128, 3 * 1024], BF16)
    r_p = [[R(), R()], [R(), R()], [R(), R()]]
    rl_sb = sb("rl_sb", [1, 512], F32)
    r_rl = R()
    atmp_sb = sb("atmp_sb", [128, 2 * 512], BF16)
    r_atmp = [R(), R()]

    E("V", lambda e: e.memset(qlo_sb[64:128, :], 0.0), writes=[r_qlo])
    pB_sc, bB_sc6 = pp("pB_sc", [128, 3072])
    pB_acc, bB_acc = pp("pB_acc", [128, 512])
    pB_l, bB_l = pp("pB_l", [1, 512])
    bB_sc = [bB_sc6[0:2], bB_sc6[2:4], bB_sc6[4:6]]
    pB_pr = pB_sc[:, 2048:3072]
    bB_pr = bB_sc6[4:6]
    bB_acc, bB_l = bB_acc[0], bB_l[0]

    def load_wslice(h):
        tok = None
        first = True
        for off, src, w in ((0, wuq, 512), (512, wuk, 512), (1024, wuv, 512),
                            (1536, wqr, 256)):
            tok = E("Y", lambda e, o=off, s=src, w=w, h=h: e.dma_start(
                wslf[:, o:o + w].rearrange("p (i c) -> p i c", i=4), s[:, :, h, :]),
                writes=[r_wslf] if first else [], dma=True)
            first = False
        r_wslf.ready = tok
        r_wslf.users = []
        slot = h % 2
        E("V", lambda e, slot=slot: e.tensor_copy(
            wslb[:, slot * WSL:(slot + 1) * WSL], wslf[:]),
          reads=[r_wslf], writes=[r_wslb[slot]])

    load_wslice(0)
    pending_evac = [None]
    for h in range(NH):
        emit_casts(3)
        slot = h % 2
        base = slot * WSL
        rb = r_wslb[slot]
        if h + 1 < NH:
            load_wslice(h + 1)
        for kb in range(8):
            half = kb % 2
            pv = pB_pr[:, half * 512:(half + 1) * 512]
            br = bB_pr[half]

            def f_khi(e, kb=kb, base=base, pv=pv):
                for i in range(4):
                    ins = e.matmul(pv, wslb[:, base + 512 + i * 128:base + 512 + i * 128 + 128],
                                   ckv_sb[:, i * SEQ + kb * 512:i * SEQ + kb * 512 + 512],
                                   start=(i == 0), stop=(i == 3))
                return ins
            E("T", f_khi, reads=[rb, r_ckv], writes=[br])
            E("V", lambda e, pv=pv, kb=kb: e.tensor_copy(
                khi_sb[:, kb * 512:(kb + 1) * 512], pv), reads=[br], writes=[r_khi])
        for g in range(8):
            half = g % 2
            pv = pB_pr[:, half * 512:(half + 1) * 512]
            br = bB_pr[half]

            def f_v(e, g=g, base=base, pv=pv):
                for c in range(4):
                    kc = g * 4 + c
                    for i in range(4):
                        ins = e.matmul(
                            pv[:, c * 128:(c + 1) * 128],
                            ckv_sb[:, i * SEQ + kc * 128:i * SEQ + kc * 128 + 128],
                            wslb[:, base + 1024 + i * 128:base + 1024 + i * 128 + 128],
                            start=(i == 0), stop=(i == 3))
                return ins
            E("T", f_v, reads=[rb, r_ckv], writes=[br])
            E("V", lambda e, pv=pv, g=g: e.tensor_copy(
                v_sb[:, g * 512:(g + 1) * 512], pv), reads=[br], writes=[r_v])
        for qb in range(4):
            half = qb % 2
            pv = pB_pr[:, half * 512:(half + 1) * 512]
            br = bB_pr[half]

            def f_qhi(e, qb=qb, base=base, pv=pv):
                for i in range(4):
                    ins = e.matmul(pv, wslb[:, base + i * 128:base + i * 128 + 128],
                                   cq_sb[:, i * OWN + qb * 512:i * OWN + qb * 512 + 512],
                                   start=(i == 0), stop=(i == 3))
                return ins
            E("T", f_qhi, reads=[rb, r_cq], writes=[br])
            E("V", lambda e, pv=pv, qb=qb: e.tensor_copy(
                qhi_sb[:, qb * 512:(qb + 1) * 512], pv), reads=[br], writes=[r_qhi])
        for qb in range(4):
            half = qb % 2
            pv = pB_pr[:, half * 512:(half + 1) * 512]
            br = bB_pr[half]

            def f_qr(e, qb=qb, base=base, pv=pv):
                pv64 = pv[0:64, :]
                for i in range(4):
                    ins = e.matmul(pv64, wslb[:, base + 1536 + i * 64:base + 1536 + i * 64 + 64],
                                   cq_sb[:, i * OWN + qb * 512:i * OWN + qb * 512 + 512],
                                   start=(i == 0), stop=(i == 3))
                return ins
            E("T", f_qr, reads=[rb, r_cq], writes=[br])
            E("V", lambda e, pv=pv, qb=qb: e.tensor_copy(
                qraw_sb[:, qb * 512:(qb + 1) * 512], pv[0:64, :]),
              reads=[br], writes=[r_qraw])
            pv2 = pB_pr[:, (1 - half) * 512:(2 - half) * 512]
            br2 = bB_pr[1 - half]
            E("T", lambda e, pv2=pv2, qb=qb: e.matmul(
                pv2[0:64, :], ps64_sb[:], qraw_sb[:, qb * 512:(qb + 1) * 512],
                start=True, stop=True),
              reads=[r_qraw, rconst], writes=[br2])
            t1 = qt_sb[:, 0:512]
            t2 = qt_sb[:, 512:1024]
            E("V", lambda e, qb=qb, t1=t1: e.tensor_tensor(
                t1, qraw_sb[:, qb * 512:(qb + 1) * 512],
                qcos_sb[:, qb * 512:(qb + 1) * 512], ALU.mult),
              reads=[r_qraw, r_qtab], writes=[r_qt])
            E("V", lambda e, pv2=pv2, qb=qb, t2=t2: e.tensor_tensor(
                t2, pv2[0:64, :], qsin_sb[:, qb * 512:(qb + 1) * 512], ALU.mult),
              reads=[br2, r_qtab], writes=[r_qt])
            E("V", lambda e, qb=qb, t1=t1, t2=t2: e.tensor_tensor(
                qlo_sb[0:64, qb * 512:(qb + 1) * 512], t1, t2, ALU.add),
              reads=[r_qt], writes=[r_qlo])
        for qt in range(NQT):
            np2 = KCS[qt] // 2

            def emit_scores(j2, qt=qt, np2=np2):
                sslot = j2 % 3
                sc = pB_sc[:, sslot * 1024:(sslot + 1) * 1024]
                bsc = bB_sc[sslot]
                kc0 = 2 * j2
                for half in range(2):
                    kc = kc0 + half
                    schalf = sc[:, half * 512:(half + 1) * 512]

                    def f_sc(e, kc=kc, qt=qt, schalf=schalf):
                        e.matmul(schalf,
                                 khi_sb[:, kc * 128:(kc + 1) * 128],
                                 qhi_sb[:, qt * 512:(qt + 1) * 512],
                                 start=True, stop=False)
                        return e.matmul(schalf,
                                        kr_sb[:, kc * 128:(kc + 1) * 128],
                                        qlo_sb[:, qt * 512:(qt + 1) * 512],
                                        start=False, stop=True)
                    E("T", f_sc, reads=[r_khi, r_qhi, r_qlo, r_kr],
                      writes=[bsc[half]])
                    ph = p_sb[:, sslot * 1024 + half * 512:sslot * 1024 + half * 512 + 512]
                    rp = r_p[sslot][half]
                    E("S", lambda e, sc=schalf, pq=ph: e.activation(
                        pq, sc, AF.Exp, bias=ebias_sb[:, 0:1], scale=SCALE),
                      reads=[bsc[half]], writes=[rp])
                    E("V", lambda e, pq=ph: e.tensor_scalar(
                        pq, pq, 1.0, PLO, ALU.min, ALU.max), writes=[rp])
                    if j2 >= np2 - 4:
                        E("V", lambda e, pq=ph, qt=qt, kc=kc:
                          e.scalar_tensor_tensor(
                              pq, qpos_sb[:, qt * 512:(qt + 1) * 512],
                              kpos_sb[:, kc:kc + 1], pq,
                              ALU.is_ge, ALU.mult),
                          reads=[rconst, r_qtab], writes=[rp], haz=True)

            def emit_pv(j2, qt=qt):
                sslot = j2 % 3
                kc0 = 2 * j2
                ph0 = p_sb[:, sslot * 1024:sslot * 1024 + 512]
                ph1 = p_sb[:, sslot * 1024 + 512:sslot * 1024 + 1024]
                st = (j2 == 0)

                def f_pv(e, kc0=kc0, st=st, p0=ph0, p1=ph1):
                    e.matmul(pB_l[:], ones_c[:], p0, start=st, stop=False)
                    e.matmul(pB_l[:], ones_c[:], p1, start=False, stop=False)
                    e.matmul(pB_acc[:], v_sb[:, kc0 * 128:kc0 * 128 + 128],
                             p0, start=st, stop=False)
                    return e.matmul(pB_acc[:], v_sb[:, kc0 * 128 + 128:kc0 * 128 + 256],
                                    p1, start=False, stop=False)
                E("T", f_pv, reads=[r_p[sslot][0], r_p[sslot][1], r_v, rconst],
                  writes=[bB_acc, bB_l])

            emit_scores(0)
            if np2 > 1:
                emit_scores(1)
            if pending_evac[0] is not None:
                pending_evac[0]()
                pending_evac[0] = None
            for j2 in range(np2):
                if j2 + 2 < np2:
                    emit_scores(j2 + 2)
                emit_pv(j2)

            def evac(h=h, qt=qt):
                aslot = qt % 2
                atile = atmp_sb[:, aslot * 512:(aslot + 1) * 512]
                ratile = r_atmp[aslot]
                E("V", lambda e, a=atile: e.tensor_copy(a, pB_acc[:]),
                  reads=[bB_acc], writes=[ratile])
                E("V", lambda e: e.tensor_copy(rl_sb[:], pB_l[:]), reads=[bB_l],
                  writes=[r_rl])
                E("S", lambda e: e.activation(rl_sb[:], rl_sb[:], AF.Ln),
                  reads=[r_rl], writes=[r_rl])
                E("S", lambda e: e.activation(rl_sb[:], rl_sb[:], AF.Exp, scale=-1.0),
                  reads=[r_rl], writes=[r_rl])
                E("T", lambda e: e.matmul(pB_acc[:], ones_r32[:], rl_sb[:],
                                          start=True, stop=True),
                  reads=[r_rl, rconst], writes=[bB_acc])
                E("V", lambda e, a=atile: e.tensor_tensor(a, a, pB_acc[:], ALU.mult),
                  reads=[bB_acc], writes=[ratile])
                E("S", lambda e, a=atile, h=h, qt=qt: e.dma_start(
                    attnd[:, h * OWN + qt * 512: h * OWN + qt * 512 + 512], a),
                  reads=[ratile], dma=True)
            pending_evac[0] = evac

    if pending_evac[0] is not None:
        pending_evac[0]()
        pending_evac[0] = None
    emit_casts(100)
    r_attnd = R()
    r_attnd.ready = ("dS", sch.cnt["dS"])
    free("qcos_sb")
    free("ckv_sb")
    sch.gate()

    wo_sb = sb("wo_sb", [128, 2 * NH * 512], BF16)
    r_wo = [R(), R()]
    attc_sb = sb("attc_sb", [128, 2 * NH * 512], BF16)
    r_attc = [R(), R()]
    attno_sb = sb("attno_sb", [128, 4 * D], F32)
    r_attno = [R() for _ in range(4)]
    sums_sb = sb("sums_sb", [128, 16], F32)
    r_sums = [R() for _ in range(4)]
    sqc_sb = sb("sqc_sb", [128, 4], F32)
    r_sqc = [R() for _ in range(4)]
    scrC = sb("scrC", [128, 2048], F32)
    r_scrC = R()
    stripC = sb("stripC", [128, 8], F32)
    r_stripC = R()
    xtm_sb = sb("xtm_sb", [128, 2 * D], F32)
    r_xtm = [R(), R()]
    x1b_sb = sb("x1b_sb", [128, 2 * D], F32)
    r_x1b = [R(), R()]
    pC, bC = pp("pC", [128, 2048])
    r_x1d = R()

    def ln_tm_strips(sumcol, sqcol, icol, ncol, rs, extra_reads):
        E("V", lambda e: e.tensor_scalar(ncol, sumcol, 1.0 / D, None, ALU.mult),
          reads=extra_reads, writes=[rs])
        E("V", lambda e: e.tensor_scalar(icol, sqcol, 1.0 / D, EPS, ALU.mult, ALU.add),
          reads=extra_reads, writes=[rs])
        E("V", lambda e: e.scalar_tensor_tensor(
            icol, ncol, ncol, icol, ALU.mult, ALU.subtract),
          writes=[rs], haz=True)
        E("V", lambda e: e.tensor_scalar(icol, icol, -1.0, None, ALU.mult),
          writes=[rs])
        E("S", lambda e: e.activation(icol, icol, AF.Ln), reads=[rs], writes=[rs])
        E("S", lambda e: e.activation(icol, icol, AF.Exp, scale=-0.5),
          reads=[rs], writes=[rs])
        E("V", lambda e: e.scalar_tensor_tensor(
            ncol, ncol, -1.0, icol, ALU.mult, ALU.mult),
          writes=[rs], haz=True)

    for qs in range(4):
        aslot = qs % 2
        attc = attc_sb[:, aslot * NH * 512:(aslot + 1) * NH * 512]
        rattc = r_attc[aslot]
        E("Y", lambda e, qs=qs, a=attc: e.dma_start(
            a.rearrange("p (h c) -> p h c", h=NH),
            attnd[:].rearrange("p (h t) -> p h t", h=NH)[:, :, qs * 512:(qs + 1) * 512]),
          reads=[r_attnd], writes=[rattc], dma=True)
        for ob in range(4):
            wslot = ob % 2
            wdst = wo_sb[:, wslot * NH * 512:(wslot + 1) * NH * 512]
            rwo = r_wo[wslot]
            E("Y", lambda e, ob=ob, w=wdst: e.dma_start(
                w.rearrange("p (h c) -> p h c", h=NH), s_wo[:, ob]),
              reads=[r_swo], writes=[rwo], dma=True)
            for qc4 in range(4):
                pv = pC[:, qc4 * 512:(qc4 + 1) * 512]
                bb = bC[qc4]

                def f_op(e, a=attc, qc4=qc4, w=wdst, pv=pv):
                    for hh in range(NH):
                        ins = e.matmul(pv, a[:, hh * 512 + qc4 * 128:hh * 512 + qc4 * 128 + 128],
                                       w[:, hh * 512:(hh + 1) * 512],
                                       start=(hh == 0), stop=(hh == NH - 1))
                    return ins
                E("T", f_op, reads=[rwo, rattc], writes=[bb])
                E("S", lambda e, pv=pv, qc4=qc4, ob=ob: e.activation(
                    attno_sb[:, qc4 * D + ob * 512:qc4 * D + ob * 512 + 512], pv,
                    AF.Copy),
                  reads=[bb], writes=[r_attno[qc4]])
        for qc4 in range(4):
            qc = qs * 4 + qc4
            asl = attno_sb[:, qc4 * D:(qc4 + 1) * D]
            xslot = qc4 % 2
            xtile = xtm_sb[:, xslot * D:(xslot + 1) * D]
            rx = r_xtm[xslot]
            E("Y", lambda e, qc=qc, xx=xtile: e.dma_start(xx, xo[qc * 128:(qc + 1) * 128, :]),
              writes=[rx], dma=True)
            E("V", lambda e, a=asl, qc4=qc4: e.scalar_tensor_tensor(
                scrC[:], a, 1.0, a, ALU.mult, ALU.mult,
                accum_out=sqc_sb[:, qc4:qc4 + 1]),
              reads=[r_attno[qc4]], writes=[r_scrC, r_sqc[qc4]], haz=True)
            scol = stripC[:, 0:1]
            icol = stripC[:, 1:2]
            ncol = stripC[:, 2:3]
            E("V", lambda e, a=asl, scol=scol: e.tensor_reduce(scol, a, AX.X, ALU.add),
              reads=[r_attno[qc4]], writes=[r_stripC], haz=True)
            ln_tm_strips(scol, sqc_sb[:, qc4:qc4 + 1], icol, ncol, r_stripC,
                         [r_sqc[qc4]])
            x1t = x1b_sb[:, xslot * D:(xslot + 1) * D]
            rx1 = r_x1b[xslot]
            E("V", lambda e, a=asl, x1t=x1t, icol=icol, ncol=ncol: e.tensor_scalar(
                x1t, a, icol, ncol, ALU.mult, ALU.add),
              reads=[r_attno[qc4], r_stripC], writes=[rx1])
            E("V", lambda e, x1t=x1t, xx=xtile: e.tensor_tensor(x1t, x1t, xx, ALU.add),
              reads=[rx], writes=[rx1])
            E("S", lambda e, x1t=x1t, qc=qc: e.dma_start(
                x1d[qc * 128:(qc + 1) * 128, :], x1t), reads=[rx1], writes=[r_x1d],
              dma=True)

    if "attno" in dump:
        rd = R()
        E("S", lambda e: e.dma_start(dmp["attno"][:], attno_sb[:]),
          reads=r_attno, writes=[rd], dma=True)
        E("S", lambda e: e.dma_start(dmp["strip"][:], stripC[:]),
          reads=[r_stripC], writes=[rd], dma=True)
        E("S", lambda e: e.dma_start(dmp["sums"][:], sums_sb[:]),
          reads=r_sums, writes=[rd], dma=True)
        E("S", lambda e: e.dma_start(dmp["sqc"][:], sqc_sb[:]),
          reads=r_sqc, writes=[rd], dma=True)
    free("wo_sb")
    sch.gate()

    y_sb = sb("y_sb", [128, NFC * 512], BF16)
    r_y = R()
    xnf_sb = sb("xnf_sb", [128, 16 * 512], BF16)
    r_xnf = R()
    xnt_sb = sb("xnt_sb", [128, 2048], BF16)
    r_xnt = R()
    x1t_sb = sb("x1t_sb", [128, 2 * 2048], F32)
    r_x1t = [R(), R()]
    fsr_sb = sb("fsr_sb", [128, 2 * FS_COLS], BF16)
    r_fsr = [R(), R()]
    a2r_sb = sb("a2r_sb", [128, 3 * 256], BF16)
    r_a2r = [R(), R(), R()]
    a13_sb = sb("a13_sb", [128, 16 * 128], BF16)
    r_a13 = R()
    ash_sb = sb("ash_sb", [128, 8 * 512], BF16)
    r_ash = R()
    m256_sb = sb("m256_sb", [128, 4 * 512], BF16)
    r_m256 = R()
    rwb_sb = sb("rwb_sb", [128, 2 * 512], F32)
    r_rwb = R()
    lgE_sb = sb("lgE_sb", [128, 32], F32)
    r_lgE = R()
    g01_sb = sb("g01_sb", [128, 16], BF16)
    r_g01 = R()
    rwp_sb = sb("rwp_sb", [128, 2], F32)
    r_rwp = R()
    maskf_sb = sb("maskf_sb", [8, 1024], BF16)
    r_maskf = R()
    rws_sb = sb("rws_sb", [1, 1024], F32)
    r_rws = R()
    stripD = sb("stripD", [128, 16], F32)
    r_stripD = R()
    scrD = sb("scrD", [128, 2048], F32)
    r_scrD = R()
    z_sb = scrD[:, 0:1024]
    r_z = [R(), R()]
    x2_sb = scrD[:, 1024:1536]
    r_x2 = R()
    y01_sb = sb("y01_sb", [128, 4 * 512], BF16)
    r_y01 = [R(), R(), R(), R()]
    a2e_sb = sb("a2e_sb", [128, 4 * 512], BF16)
    r_a2e = R()
    w2r_sb = sb("w2r_sb", [128, 2 * 1024], BF16)
    r_w2r = [R(), R()]
    fout_sb = sb("fout_sb", [128, 4 * 2048], BF16)
    r_fout = [R() for _ in range(4)]
    sumd_sb = sb("sumd_sb", [128, 16], F32)
    r_sumd = [R() for _ in range(4)]
    sqd_sb = sb("sqd_sb", [128, 4], F32)
    r_sqd = [R() for _ in range(4)]
    ob_sb = sb("ob_sb", [128, 2 * 2048], F32)
    r_ob = [R(), R()]

    pD_rot, bD_rot = pp("pD_rot", [128, 1536])
    pD_tr = None
    rotv = [(pD_rot[:, i * 512:(i + 1) * 512], bD_rot[i]) for i in range(3)]
    trp_t, trp_b = pp("pD_tr", [128, 1024], BF16)
    trp_b = trp_b[0]


    rotc = [0]

    def rot_next():
        i = rotc[0] % 3
        rotc[0] += 1
        return rotv[i]

    def emit_prologue(tbk):
        for tc in range(4):
            qc = tbk * 4 + tc
            xslot = tc % 2
            x1t = x1t_sb[:, xslot * 2048:(xslot + 1) * 2048]
            rx1 = r_x1t[xslot]
            E("Y", lambda e, qc=qc, xx=x1t: e.dma_start(
                xx, x1d[qc * 128:(qc + 1) * 128, :]), reads=[r_x1d], writes=[rx1],
              dma=True)
            scol = stripD[:, 0:1]
            sqcol = stripD[:, 1:2]
            icol = stripD[:, 2:3]
            ncol = stripD[:, 3:4]
            E("V", lambda e, xx=x1t, scol=scol: e.tensor_reduce(scol, xx, AX.X, ALU.add),
              reads=[rx1], writes=[r_stripD], haz=True)
            E("V", lambda e, xx=x1t, sqcol=sqcol: e.scalar_tensor_tensor(
                scrD[:], xx, 1.0, xx, ALU.mult, ALU.mult, accum_out=sqcol),
              reads=[rx1], writes=[r_scrD, r_stripD], haz=True)
            ln_tm_strips(scol, sqcol, icol, ncol, r_stripD, [])
            E("V", lambda e, xx=x1t, icol=icol, ncol=ncol: e.tensor_scalar(
                xnt_sb[:], xx, icol, ncol, ALU.mult, ALU.add),
              reads=[rx1, r_stripD], writes=[r_xnt])
            for b8 in range(2):
                def f_tr(e, b8=b8):
                    for i in range(8):
                        ins = e.transpose(trp_t[:, i * 128:(i + 1) * 128],
                                          xnt_sb[:, (b8 * 8 + i) * 128:(b8 * 8 + i) * 128 + 128],
                                          ident_sb[:])
                    return ins
                E("T", f_tr, reads=[r_xnt, rconst], writes=[trp_b])
                E("V", lambda e, b8=b8, tc=tc: e.tensor_copy(
                    xnf_sb[:].rearrange("p (i t) -> p i t", i=16)
                    [:, b8 * 8:(b8 + 1) * 8, tc * 128:(tc + 1) * 128],
                    trp_t[:].rearrange("p (i t) -> p i t", i=8)),
                  reads=[trp_b], writes=[r_xnf])
            pvl, bvl = rotv[0]
            lgp = pvl[:, 0:8]

            def f_lg(e, tc=tc, lgp=lgp):
                for i in range(16):
                    ins = e.matmul(lgp, xnf_sb[:, i * 512 + tc * 128:i * 512 + tc * 128 + 128],
                                   wg_sb[:, i * 8:(i + 1) * 8],
                                   start=(i == 0), stop=(i == 15))
                return ins
            E("T", f_lg, reads=[r_xnf, rconst], writes=[bvl])
            lg = lgE_sb[:, 0:8]
            Ee = lgE_sb[:, 8:16]
            E2 = lgE_sb[:, 16:24]
            c0 = lgE_sb[:, 24:25]
            c1 = lgE_sb[:, 25:26]
            E("V", lambda e, lgp=lgp: e.tensor_copy(lg, lgp), reads=[bvl], writes=[r_lgE], haz=True, tag=f"rt{tc}:cplg")
            E("V", lambda e: e.tensor_reduce(c0, lg, AX.X, ALU.max), writes=[r_lgE], haz=True, tag=f"rt{tc}:max")
            E("V", lambda e: e.tensor_scalar(c1, c0, -1.0, None, ALU.mult), writes=[r_lgE], haz=True, tag=f"rt{tc}:negc")
            E("S", lambda e: e.activation(Ee, lg, AF.Exp, bias=c1), reads=[r_lgE],
              writes=[r_lgE], tag=f"rt{tc}:exp")
            g0 = g01_sb[:, 0:8]
            g1 = g01_sb[:, 8:16]
            E("V", lambda e: e.tensor_scalar(g0, lg, c0, None, ALU.is_equal),
              reads=[r_lgE], writes=[r_g01], haz=True)
            E("V", lambda e: e.tensor_scalar(E2, g0, -1.0, 1.0, ALU.mult, ALU.add),
              reads=[r_g01], writes=[r_lgE], haz=True)
            E("V", lambda e: e.tensor_tensor(E2, E2, Ee, ALU.mult), writes=[r_lgE], haz=True)
            E("V", lambda e: e.tensor_reduce(c0, E2, AX.X, ALU.max), writes=[r_lgE], haz=True)
            E("V", lambda e: e.tensor_scalar(g1, E2, c0, None, ALU.is_equal),
              reads=[r_lgE], writes=[r_g01], haz=True)
            rw0 = rwp_sb[:, 0:1]
            rw1 = rwp_sb[:, 1:2]
            E("V", lambda e: e.tensor_scalar(c1, c0, 1.0, None, ALU.add), writes=[r_lgE], haz=True)
            E("V", lambda e: e.reciprocal(rw0, c1), reads=[r_lgE], writes=[r_rwp], haz=True, tag=f"rt{tc}:recip")
            E("V", lambda e: e.tensor_tensor(rw1, c0, rw0, ALU.mult),
              reads=[r_lgE], writes=[r_rwp], haz=True)
            def f_trm(e, tc=tc):
                e.transpose(trp_t[0:8, 0 * 512 + tc * 128:0 * 512 + tc * 128 + 128],
                            g01_sb[:, 0:8], ident_sb[:])
                return e.transpose(trp_t[0:8, 1 * 512 + tc * 128:1 * 512 + tc * 128 + 128],
                                   g01_sb[:, 8:16], ident_sb[:])
            E("T", f_trm, reads=[r_g01, rconst], writes=[trp_b])
            E("V", lambda e, tc=tc: e.tensor_copy(
                maskf_sb[:, 0 * 512 + tc * 128:0 * 512 + tc * 128 + 128],
                trp_t[0:8, 0 * 512 + tc * 128:0 * 512 + tc * 128 + 128]),
              reads=[trp_b], writes=[r_maskf])
            E("V", lambda e, tc=tc: e.tensor_copy(
                maskf_sb[:, 1 * 512 + tc * 128:1 * 512 + tc * 128 + 128],
                trp_t[0:8, 1 * 512 + tc * 128:1 * 512 + tc * 128 + 128]),
              reads=[trp_b], writes=[r_maskf])
            pvr, bvr = rotv[1]

            def f_trw(e, pvr=pvr):
                e.transpose(pvr[0:1, 0:128], rwp_sb[:, 0:1], id32_sb[:])
                return e.transpose(pvr[0:1, 128:256], rwp_sb[:, 1:2], id32_sb[:])
            E("T", f_trw, reads=[r_rwp, rconst], writes=[bvr])
            E("V", lambda e, tc=tc, pvr=pvr: e.tensor_copy(
                rws_sb[0:1, tc * 128:(tc + 1) * 128], pvr[0:1, 0:128]),
              reads=[bvr], writes=[r_rws])
            E("V", lambda e, tc=tc, pvr=pvr: e.tensor_copy(
                rws_sb[0:1, 512 + tc * 128:512 + (tc + 1) * 128], pvr[0:1, 128:256]),
              reads=[bvr], writes=[r_rws])
        for k in range(2):
            for j in range(2):
                pv, bv = rot_next()

                def f_ex(e, k=k, j=j, pv=pv):
                    return e.matmul(pv, e8_sb[:, j * 128:(j + 1) * 128],
                                    maskf_sb[0:8, k * 512:(k + 1) * 512],
                                    start=True, stop=True)
                E("T", f_ex, reads=[r_maskf, rconst], writes=[bv])
                E("V", lambda e, k=k, j=j, pv=pv: e.tensor_copy(
                    m256_sb[:, (k * 2 + j) * 512:(k * 2 + j) * 512 + 512], pv),
                  reads=[bv], writes=[r_m256])
            pv, bv = rot_next()
            E("T", lambda e, k=k, pv=pv: e.matmul(
                pv, ones_r32[:], rws_sb[0:1, k * 512:(k + 1) * 512], start=True, stop=True),
              reads=[r_rws, rconst], writes=[bv])
            E("V", lambda e, k=k, pv=pv: e.tensor_copy(
                rwb_sb[:, k * 512:(k + 1) * 512], pv), reads=[bv], writes=[r_rwb])
        for (m, asrc) in ((0, s_a1s), (1, s_a3s)):
            for j in range(2):
                E("Y", lambda e, asrc=asrc, j=j: e.dma_start(
                    a13_sb[:].rearrange("p (a c) -> p a c", a=16), asrc[:, :, j, :]),
                  reads=[r_sa13], writes=[r_a13], dma=True)
                pv, bv = rot_next()

                def f_a(e, j=j, pv=pv):
                    for i in range(16):
                        ins = e.matmul(pv, a13_sb[:, i * 128:(i + 1) * 128],
                                       xnf_sb[:, i * 512:(i + 1) * 512],
                                       start=(i == 0), stop=(i == 15))
                    return ins
                E("T", f_a, reads=[r_a13, r_xnf], writes=[bv])
                for k in range(2):
                    E("V", lambda e, m=m, k=k, j=j, pv=pv: e.tensor_tensor(
                        ash_sb[:, ((m * 2 + k) * 2 + j) * 512:((m * 2 + k) * 2 + j) * 512 + 512],
                        pv, m256_sb[:, (k * 2 + j) * 512:(k * 2 + j) * 512 + 512],
                        ALU.mult),
                      reads=[bv, r_m256], writes=[r_ash])
                E("V", lambda e, m=m, j=j: e.tensor_tensor(
                    ash_sb[:, ((m * 2 + 1) * 2 + j) * 512:((m * 2 + 1) * 2 + j) * 512 + 512],
                    ash_sb[:, ((m * 2 + 1) * 2 + j) * 512:((m * 2 + 1) * 2 + j) * 512 + 512],
                    ash_sb[:, ((m * 2 + 0) * 2 + j) * 512:((m * 2 + 0) * 2 + j) * 512 + 512],
                    ALU.subtract), writes=[r_ash])

    emit_prologue(0)
    for tbk in range(4):
        pD_a2a, bD_a2a = pp(f"pD_a2a{tbk}", [128, 1024])
        pD_a2b, bD_a2b = pp(f"pD_a2b{tbk}", [128, 1024])
        E("Y", lambda e: e.dma_start(
            fsr_sb[:, 0:FS_COLS], s_fs[0]), reads=[r_sfs], writes=[r_fsr[0]], dma=True)
        E("Y", lambda e: e.dma_start(
            a2r_sb[:, 0:256], s_fs[0][:, FS_A2:FS_COLS]),
          reads=[r_sfs], writes=[r_a2r[0]], dma=True)
        for fc in range(NFC):
            fslot = fc % 2
            fsl = fsr_sb[:, fslot * FS_COLS:(fslot + 1) * FS_COLS]
            rfs = r_fsr[fslot]
            if fc + 1 < NFC:
                nslot = (fc + 1) % 2
                E("Y", lambda e, fc=fc, ns=nslot: e.dma_start(
                    fsr_sb[:, ns * FS_COLS:(ns + 1) * FS_COLS], s_fs[fc + 1]),
                  reads=[r_sfs], writes=[r_fsr[nslot]], dma=True)
                E("Y", lambda e, fc=fc, ns=(fc + 1) % 3: e.dma_start(
                    a2r_sb[:, ns * 256:(ns + 1) * 256], s_fs[fc + 1][:, FS_A2:FS_COLS]),
                  reads=[r_sfs], writes=[r_a2r[(fc + 1) % 3]], dma=True)
            pv1, bv1 = rot_next()

            def f_x1(e, fsl=fsl, pv1=pv1):
                for i in range(16):
                    e.matmul(pv1, fsl[:, FS_W1 + i * 128:FS_W1 + i * 128 + 128],
                             xnf_sb[:, i * 512:(i + 1) * 512],
                             start=(i == 0), stop=False)
                for j in range(2):
                    ins = e.matmul(pv1, fsl[:, FS_B1 + j * 128:FS_B1 + j * 128 + 128],
                                   ash_sb[:, (0 * 2 + j) * 512:(0 * 2 + j) * 512 + 512],
                                   start=False, stop=False)
                return ins
            E("T", f_x1, reads=[rfs, r_xnf, r_ash], writes=[bv1])
            pv3, bv3 = rot_next()

            def f_x3(e, fsl=fsl, pv3=pv3):
                for i in range(16):
                    e.matmul(pv3, fsl[:, FS_W3 + i * 128:FS_W3 + i * 128 + 128],
                             xnf_sb[:, i * 512:(i + 1) * 512],
                             start=(i == 0), stop=False)
                for j in range(2):
                    ins = e.matmul(pv3, fsl[:, FS_B3 + j * 128:FS_B3 + j * 128 + 128],
                                   ash_sb[:, ((2 + 0) * 2 + j) * 512:((2 + 0) * 2 + j) * 512 + 512],
                                   start=False, stop=False)
                return ins
            E("T", f_x3, reads=[rfs, r_xnf, r_ash], writes=[bv3])
            for k in range(2):
                if k == 1:
                    def f_x1d(e, fsl=fsl, pv1=pv1):
                        for j in range(2):
                            ins = e.matmul(pv1, fsl[:, FS_B1 + j * 128:FS_B1 + j * 128 + 128],
                                           ash_sb[:, (1 * 2 + j) * 512:(1 * 2 + j) * 512 + 512],
                                           start=False, stop=(j == 1))
                        return ins
                    E("T", f_x1d, reads=[rfs, r_ash], writes=[bv1])

                    def f_x3d(e, fsl=fsl, pv3=pv3):
                        for j in range(2):
                            ins = e.matmul(pv3, fsl[:, FS_B3 + j * 128:FS_B3 + j * 128 + 128],
                                           ash_sb[:, ((2 + 1) * 2 + j) * 512:((2 + 1) * 2 + j) * 512 + 512],
                                           start=False, stop=(j == 1))
                        return ins
                    E("T", f_x3d, reads=[rfs, r_ash], writes=[bv3])
                fpar = fc % 2
                zsl = z_sb[:, k * 512:(k + 1) * 512]
                rz = r_z[k]
                E("S", lambda e, pv1=pv1, z=zsl: e.activation(z, pv1, AF.Silu),
                  reads=[bv1], writes=[rz])
                E("V", lambda e, z=zsl, pv3=pv3: e.tensor_tensor(
                    x2_sb[:], z, pv3, ALU.mult), reads=[rz, bv3], writes=[r_x2])
                ysl = y01_sb[:, (fpar * 2 + k) * 512:(fpar * 2 + k) * 512 + 512]
                ry = r_y01[fpar * 2 + k]
                E("V", lambda e, k=k, y=ysl: e.tensor_tensor(
                    y, x2_sb[:], rwb_sb[:, k * 512:(k + 1) * 512], ALU.mult),
                  reads=[r_x2, r_rwb], writes=[ry])
                if fc > 0:
                    ppar = (fc - 1) % 2
                    ypsl = y01_sb[:, (ppar * 2 + k) * 512:(ppar * 2 + k) * 512 + 512]
                    rpy = r_y01[ppar * 2 + k]
                    a2b = (bD_a2a if k == 0 else bD_a2b)
                    a2t = (pD_a2a if k == 0 else pD_a2b)
                    pslot = (fc - 1) % 3
                    pa2 = a2r_sb[:, pslot * 256:(pslot + 1) * 256]
                    prfs = r_a2r[pslot]

                    def f_a2(e, pa2=pa2, y=ypsl, a2t=a2t, fc=fc):
                        for j in range(2):
                            ins = e.matmul(a2t[:, j * 512:(j + 1) * 512],
                                           pa2[:, j * 128:(j + 1) * 128],
                                           y, start=(fc - 1 == 0), stop=False)
                        return ins
                    E("T", f_a2, reads=[prfs, rpy], writes=list(a2b))
            E("V", lambda e, fc=fc, fpar=fpar: e.tensor_tensor(
                y_sb[:, fc * 512:(fc + 1) * 512],
                y01_sb[:, fpar * 1024:fpar * 1024 + 512],
                y01_sb[:, fpar * 1024 + 512:fpar * 1024 + 1024], ALU.add),
              reads=[r_y01[fpar * 2], r_y01[fpar * 2 + 1]], writes=[r_y])

        for k in range(2):
            lpar = (NFC - 1) % 2
            lslot = (NFC - 1) % 3
            ypsl = y01_sb[:, (lpar * 2 + k) * 512:(lpar * 2 + k) * 512 + 512]
            rpy = r_y01[lpar * 2 + k]
            a2b = (bD_a2a if k == 0 else bD_a2b)
            a2t = (pD_a2a if k == 0 else pD_a2b)
            la2 = a2r_sb[:, lslot * 256:(lslot + 1) * 256]

            def f_a2f(e, la2=la2, y=ypsl, a2t=a2t):
                for j in range(2):
                    ins = e.matmul(a2t[:, j * 512:(j + 1) * 512],
                                   la2[:, j * 128:(j + 1) * 128],
                                   y, start=False, stop=True)
                return ins
            E("T", f_a2f, reads=[r_a2r[lslot], rpy], writes=list(a2b))
        for k in range(2):
            a2t = (pD_a2a if k == 0 else pD_a2b)
            a2b = (bD_a2a if k == 0 else bD_a2b)
            for j in range(2):
                E("V", lambda e, k=k, j=j, a2t=a2t: e.tensor_tensor(
                    a2e_sb[:, (k * 2 + j) * 512:(k * 2 + j) * 512 + 512],
                    a2t[:, j * 512:(j + 1) * 512],
                    m256_sb[:, (k * 2 + j) * 512:(k * 2 + j) * 512 + 512],
                    ALU.mult),
                  reads=[a2b[j], r_m256], writes=[r_a2e])
        if tbk == 0 and "ffn" in dump:
            rd0 = R()
            E("S", lambda e: e.dma_start(dmp["lgE"][:], lgE_sb[:]), reads=[r_lgE],
              writes=[rd0], dma=True)
            E("S", lambda e: e.dma_start(dmp["g01"][:], g01_sb[:]), reads=[r_g01],
              writes=[rd0], dma=True)
            E("S", lambda e: e.dma_start(dmp["rwp"][:], rwp_sb[:]), reads=[r_rwp],
              writes=[rd0], dma=True)
            E("S", lambda e: e.dma_start(dmp["rws"][:], rws_sb[:]), reads=[r_rws],
              writes=[rd0], dma=True)
            E("S", lambda e: e.dma_start(dmp["xnf"][:], xnf_sb[:]), reads=[r_xnf],
              writes=[rd0], dma=True)
            E("S", lambda e: e.dma_start(dmp["maskf"][:], maskf_sb[:]), reads=[r_maskf],
              writes=[rd0], dma=True)
            E("S", lambda e: e.dma_start(dmp["rwb"][:], rwb_sb[:]), reads=[r_rwb],
              writes=[rd0], dma=True)
            E("S", lambda e: e.dma_start(dmp["ash"][:], ash_sb[:]), reads=[r_ash],
              writes=[rd0], dma=True)
            E("S", lambda e: e.dma_start(dmp["y"][:], y_sb[:]), reads=[r_y],
              writes=[rd0], dma=True)
            E("S", lambda e: e.dma_start(dmp["a2e"][:], a2e_sb[:]), reads=[r_a2e],
              writes=[rd0], dma=True)
        free(f"pD_a2a{tbk}")
        if tbk + 1 < 4:
            emit_prologue(tbk + 1)
        pD_dn, bD_dn = pp(f"pD_dn{tbk}", [128, 2048])
        for ob in range(4):
            for bi in range(NFC // 2):
                wslot = bi % 2
                w2t = w2r_sb[:, wslot * 1024:(wslot + 1) * 1024]
                rw2 = r_w2r[wslot]
                E("Y", lambda e, ob=ob, bi=bi, w=w2t: e.dma_start(
                    w.rearrange("p (a c) -> p a c", a=2),
                    s_w2[:, ob, bi * 2:(bi + 1) * 2, :]),
                  reads=[r_sw2], writes=[rw2], dma=True)

                def f_dn(e, bi=bi, w=w2t, pd=pD_dn):
                    for f2 in range(2):
                        fc = bi * 2 + f2
                        for tc in range(4):
                            ins = e.matmul(
                                pd[:, tc * 512:(tc + 1) * 512],
                                y_sb[:, fc * 512 + tc * 128:fc * 512 + tc * 128 + 128],
                                w[:, f2 * 512:(f2 + 1) * 512],
                                start=(bi == 0 and f2 == 0), stop=False)
                    return ins
                E("T", f_dn, reads=[rw2, r_y], writes=list(bD_dn))

            def f_l2(e, ob=ob, pd=pD_dn):
                for tc in range(4):
                    for k in range(2):
                        for j in range(2):
                            ins = e.matmul(
                                pd[:, tc * 512:(tc + 1) * 512],
                                a2e_sb[:, (k * 2 + j) * 512 + tc * 128:(k * 2 + j) * 512 + tc * 128 + 128],
                                b2s_sb[:, j * D + ob * 512:j * D + ob * 512 + 512],
                                start=False, stop=(k == 1 and j == 1))
                return ins
            E("T", f_l2, reads=[r_a2e, rconst], writes=list(bD_dn))
            for tc in range(4):
                E("S", lambda e, tc=tc, ob=ob, pd=pD_dn: e.activation(
                    fout_sb[:, tc * 2048 + ob * 512:tc * 2048 + ob * 512 + 512],
                    pd[:, tc * 512:(tc + 1) * 512], AF.Copy),
                  reads=[bD_dn[tc]], writes=[r_fout[tc]])
        for tc in range(4):
            qc = tbk * 4 + tc
            fsl = fout_sb[:, tc * 2048:(tc + 1) * 2048]
            E("V", lambda e, f=fsl, tc=tc: e.scalar_tensor_tensor(
                scrD[:], f, 1.0, f, ALU.mult, ALU.mult,
                accum_out=sqd_sb[:, tc:tc + 1]),
              reads=[r_fout[tc]], writes=[r_scrD, r_sqd[tc]], haz=True)
            scol = stripD[:, 4:5]
            icol = stripD[:, 5:6]
            ncol = stripD[:, 6:7]
            E("V", lambda e, f=fsl, scol=scol: e.tensor_reduce(scol, f, AX.X, ALU.add),
              reads=[r_fout[tc]], writes=[r_stripD], haz=True)
            ln_tm_strips(scol, sqd_sb[:, tc:tc + 1], icol, ncol, r_stripD,
                         [r_sqd[tc]])
            oslot = tc % 2
            obuf = ob_sb[:, oslot * 2048:(oslot + 1) * 2048]
            ro = r_ob[oslot]
            xslot = tc % 2
            x1t = x1t_sb[:, xslot * 2048:(xslot + 1) * 2048]
            rx1 = r_x1t[xslot]
            E("Y", lambda e, qc=qc, xx=x1t: e.dma_start(
                xx, x1d[qc * 128:(qc + 1) * 128, :]), reads=[r_x1d], writes=[rx1],
              dma=True)
            E("V", lambda e, f=fsl, o=obuf, icol=icol, ncol=ncol: e.tensor_scalar(
                o, f, icol, ncol, ALU.mult, ALU.add),
              reads=[r_fout[tc], r_stripD], writes=[ro])
            E("V", lambda e, o=obuf, xx=x1t: e.tensor_tensor(o, o, xx, ALU.add),
              reads=[rx1], writes=[ro])
            E("S", lambda e, o=obuf, qc=qc: e.dma_start(
                out[qc * 128:(qc + 1) * 128, :], o), reads=[ro], dma=True)
        if tbk == 0 and "ffn" in dump:
            rd1 = R()
            E("S", lambda e: e.dma_start(dmp["fout"][:], fout_sb[:]),
              reads=r_fout, writes=[rd1], dma=True)
        free(f"pD_dn{tbk}")

    sch.tail()
    with nc.Block() as block:
        sch.finalize(block)
    free_all = list(stack)
    while stack:
        stack.pop()[1].__exit__(None, None, None)
    sch.close()
    return nc


def kernel(**inputs):
    maps = _host_inputs(inputs)
    nc = build_nc()
    res = run_bass_kernel_spmd(nc, maps, core_ids=list(range(8)))
    x = np.asarray(inputs["L0_attn_x"])
    outp = np.empty((4, SEQ, D), dtype=np.float32)
    for c in range(8):
        rows = _own_rows(c % 2)
        outp[c // 2][rows] = res.results[c]["out"]
    return outp
